# version 12
# baseline (speedup 1.0000x reference)
import numpy as np
import ml_dtypes
from contextlib import ExitStack
import concourse.bass as bass
import concourse.mybir as mybir
from concourse.bass_utils import run_bass_kernel_spmd
from concourse.alu_op_type import AluOpType as OP

F32 = mybir.dt.float32
BF16 = mybir.dt.bfloat16
AF = mybir.ActivationFunctionType
AX = mybir.AxisListType.X

T = 2048
D = 1024
NEXP = 9
H = 16
NITER = 5
CAP = 384
ZROW = 16 * CAP
NEG = -1.0e30
SLOPES = [float(np.exp2(np.float32(-(h + 1) * 8.0 / 16.0))) for h in range(H)]


def qke_cum(j):
    if j < 0:
        return 0
    return 8 + 12 * j


def build_program(debug=False, debug2=False):
    nc = bass.Bass("TRN2", target_bir_lowering=False, debug=False,
                   detect_race_conditions=False)

    x_own = nc.dram_tensor("x_own", [T, D], F32, kind="ExternalInput").ap()
    x_halo = nc.dram_tensor("x_halo", [512, D], F32, kind="ExternalInput").ap()
    w_inT = nc.dram_tensor("w_inT", [D, 3 * D], F32, kind="ExternalInput").ap()
    w_outT = nc.dram_tensor("w_outT", [D, D], F32, kind="ExternalInput").ap()
    W1H = nc.dram_tensor("W1H", [NEXP, 8, 128, 4096], F32, kind="ExternalInput").ap()
    W2H = nc.dram_tensor("W2H", [NEXP, 8, 128, 4096], F32, kind="ExternalInput").ap()
    tokid_in = nc.dram_tensor("tokid", [128, 8], mybir.dt.uint32, kind="ExternalInput").ap()
    posbase_in = nc.dram_tensor("posbase", [128, 16], F32, kind="ExternalInput").ap()
    zci_in = nc.dram_tensor("zci", [16 * CAP, 1], mybir.dt.uint32, kind="ExternalInput").ap()
    z1k_in = nc.dram_tensor("z1k", [1, 1024], BF16, kind="ExternalInput").ap()
    bfcB = nc.dram_tensor("bfcB", [NEXP, 128, 32], F32, kind="ExternalInput").ap()
    bprojB = nc.dram_tensor("bprojB", [NEXP, 128, 8], F32, kind="ExternalInput").ap()
    WrH = nc.dram_tensor("WrH", [128, 64], F32, kind="ExternalInput").ap()
    kjb_in = nc.dram_tensor("kjb", [128, 1024], F32, kind="ExternalInput").ap()
    tprev_in = nc.dram_tensor("tprev", [128, 128], BF16, kind="ExternalInput").ap()
    tcur_in = nc.dram_tensor("tcur", [128, 128], BF16, kind="ExternalInput").ap()
    maskP0_in = nc.dram_tensor("maskP0", [128, 512], BF16, kind="ExternalInput").ap()
    idbf_in = nc.dram_tensor("idbf", [128, 128], BF16, kind="ExternalInput").ap()
    idf_in = nc.dram_tensor("idf32", [128, 128], F32, kind="ExternalInput").ap()
    consts_in = nc.dram_tensor("consts", [128, 4], F32, kind="ExternalInput").ap()

    y_out = nc.dram_tensor("y", [T, D], F32, kind="ExternalOutput").ap()
    skind = "ExternalOutput" if debug else "Internal"
    x1_hbm = nc.dram_tensor("x1_hbm", [T, D], F32, kind=skind).ap()
    xn2_hbm = nc.dram_tensor("xn2_hbm", [T, D], BF16, kind=skind).ap()
    g8_hbm = nc.dram_tensor("g8_hbm", [8, T], F32, kind=skind).ap()
    idxl_hbm = nc.dram_tensor("idxl_hbm", [16 * CAP, 1], mybir.dt.uint32, kind="Internal").ap()
    W1b = nc.dram_tensor("W1b", [NEXP, 8, 128, 4096], BF16, kind="Internal").ap()
    W2b = nc.dram_tensor("W2b", [NEXP, 8, 128, 4096], BF16, kind="Internal").ap()
    out_perm = nc.dram_tensor("out_perm", [16 * CAP + 1, 1024], BF16, kind="Internal").ap()

    SBC = nc.alloc_sbuf_tensor
    idbf = SBC("c_idbf", [128, 128], BF16)
    idf32 = SBC("c_idf32", [128, 128], F32)
    consts = SBC("c_consts", [128, 4], F32)
    wr_sb = SBC("c_wr", [128, 64], BF16)
    bfc_sb = SBC("c_bfc", [128, NEXP * 32], F32)
    bproj_sb = SBC("c_bproj", [128, NEXP * 8], F32)
    tokid_sb = SBC("c_tokid", [128, 8], mybir.dt.uint32)
    posbase_sb = SBC("c_posbase", [128, 16], F32)

    S = {}
    for nm in ["pl", "plg", "x", "sq1", "va1", "std1", "st1", "xn1", "tpp", "tpe",
               "qkm", "qke", "sm1", "sc", "t2", "p", "pn", "tp", "pte", "pv",
               "oe", "op", "x1", "sq2", "va2", "std2", "st2", "xn2",
               "stx1", "stxn2",
               "xn2t", "w1", "w2", "fcp", "gel", "prm", "pse",
               "rtm", "gv1", "gs", "gv", "gtt", "g8v", "g8st",
               "pscn", "ptv", "ptt", "ix", "sca", "ixl", "xga", "xgp", "xge", "pwc", "pww",
               "ott", "opv", "ops", "cg",
               "cx1", "ct", "cy", "y", "vv"]:
        S[nm] = nc.alloc_semaphore("s_" + nm)

    def wg(eng, nm, val):
        if val > 0:
            eng.wait_ge(S[nm], val)

    vvc = [0]

    def vmark(inst):
        inst.then_inc(S["vv"], 1)
        vvc[0] += 1

    def vwait(v):
        if vvc[0] > 0:
            v.wait_ge(S["vv"], vvc[0])

    stackA = ExitStack()

    def SBA(name, shape, dt):
        return stackA.enter_context(nc.sbuf_tensor(name, shape, dt))

    winT = SBA("a_winT", [128, 8 * 3072], BF16)
    woutT = SBA("a_woutT", [128, 8 * 1024], BF16)
    kjb = SBA("a_kjb", [128, 1024], F32)
    tprev = SBA("a_tprev", [128, 128], BF16)
    tcur = SBA("a_tcur", [128, 128], BF16)
    maskP0 = SBA("a_maskP0", [128, 512], BF16)
    x_sb = SBA("a_x", [128, 4096], F32)
    xn1 = SBA("a_xn1", [128, 4096], BF16)
    xn1T = SBA("a_xn1T", [128, 8 * 512], BF16)
    q_sb = SBA("a_q", [128, 8 * 512], BF16)
    k_ring = SBA("a_kring", [128, 8 * 1024], BF16)
    v_ring = SBA("a_vring", [128, 8 * 1024], BF16)
    t_sb = SBA("a_t", [128, 3 * 640], F32)
    p_sb = SBA("a_p", [128, 3 * 640], BF16)
    pT_sb = SBA("a_pT", [128, 2 * 2560], BF16)
    o_sb = SBA("a_o", [128, 8 * 512], BF16)
    x1t = SBA("a_x1t", [128, 2 * 1024], F32)
    xn2t = SBA("a_xn2t", [128, 2 * 1024], BF16)
    sq_scr = SBA("a_sqscr", [128, 1024], F32)
    pwb = [SBA("a_pwb%d" % i, [128, 4096], BF16) for i in range(2)]
    st = SBA("a_stats", [128, 48], F32)

    psA = ExitStack()
    ps_a = [psA.enter_context(nc.psum_tensor("ps_a%d" % i, [128, 1024], F32)) for i in range(3)]
    ps_pt = [psA.enter_context(nc.psum_tensor("ps_pt0", [128, 1024], BF16))] * 2
    ps_o = [psA.enter_context(nc.psum_tensor("ps_o0", [128, 512], F32))] * 2

    with nc.Block() as bp:
        @bp.sync
        def _(sy):
            for dst, src in [(idbf, idbf_in), (idf32, idf_in), (consts, consts_in),
                             (kjb, kjb_in), (tprev, tprev_in), (tcur, tcur_in),
                             (maskP0, maskP0_in)]:
                sy.dma_start(out=dst[:], in_=src).then_inc(S["pl"], 16)
            sy.dma_start(out=bfc_sb[:], in_=bfcB.rearrange("e p c -> p e c")).then_inc(S["pl"], 16)
            sy.dma_start(out=bproj_sb[:], in_=bprojB.rearrange("e p c -> p e c")).then_inc(S["pl"], 16)
            sy.dma_start(out=tokid_sb[:], in_=tokid_in).then_inc(S["pl"], 16)
            sy.dma_start(out=posbase_sb[:], in_=posbase_in).then_inc(S["pl"], 16)
            sy.wait_ge(S["pl"], 11 * 16)

        @bp.gpsimd
        def _(g):
            for ec in range(8):
                g.dma_start(out=winT[:, ec * 3072:(ec + 1) * 3072],
                            in_=w_inT[ec * 128:(ec + 1) * 128, :],
                            max_dma_last_dim=8192).then_inc(S["plg"], 16)
                g.dma_start(out=woutT[:, ec * 1024:(ec + 1) * 1024],
                            in_=w_outT[ec * 128:(ec + 1) * 128, :],
                            max_dma_last_dim=8192).then_inc(S["plg"], 16)
            g.dma_start(out=wr_sb[:], in_=WrH, max_dma_last_dim=8192).then_inc(S["plg"], 16)
            g.dma_start(out=idxl_hbm, in_=zci_in).then_inc(S["plg"], 16)
            g.dma_start(out=out_perm[ZROW:ZROW + 1, :], in_=z1k_in).then_inc(S["plg"], 16)
            g.wait_ge(S["plg"], 19 * 16)

    nc.const_aps.register(F32, 1e-5, consts[:, 0:1])
    PL = 11 * 16
    PLG = 19 * 16

    def ln_scalar_sq(sc, src, base, tc, sem_sq, cnt_sq, sem_va, cnt_va_war):
        wg(sc, sem_va, cnt_va_war)
        c = base + (tc % 2) * 8
        sc.activation(sq_scr[:], src, AF.Square,
                      accum_out=st[:, c + 1:c + 2]).then_inc(S[sem_sq], 1)

    def ln_vector_stats(v, src, base, tc, sem_sq, cnt_sq, sem_va, sem_std, cnt_std,
                        sem_st, sem_xn, cnt_xn_war):
        c = base + (tc % 2) * 8
        wg(v, sem_xn, cnt_xn_war)
        vmark(v.reduce_sum(st[:, c + 0:c + 1], src, axis=AX))
        wg(v, sem_sq, cnt_sq)
        vwait(v)
        vmark(v.tensor_scalar_mul(st[:, c + 2:c + 3], st[:, c + 0:c + 1], 1.0 / 1024.0))
        v.tensor_scalar_mul(st[:, c + 3:c + 4], st[:, c + 1:c + 2], 1.0 / 1024.0)
        vwait(v)
        vmark(v.tensor_scalar(st[:, c + 4:c + 5], st[:, c + 2:c + 3],
                              st[:, c + 2:c + 3], 1.0, op0=OP.mult, op1=OP.mult))
        vwait(v)
        v.tensor_tensor(st[:, c + 3:c + 4], st[:, c + 3:c + 4],
                        st[:, c + 4:c + 5], op=OP.subtract).then_inc(S[sem_va], 1)
        wg(v, sem_std, cnt_std)
        vmark(v.reciprocal(st[:, c + 5:c + 6], st[:, c + 4:c + 5]))
        vwait(v)
        v.tensor_scalar(st[:, c + 6:c + 7], st[:, c + 2:c + 3],
                        st[:, c + 5:c + 6], -1.0, op0=OP.mult, op1=OP.mult
                        ).then_inc(S[sem_st], 1)

    def ln_scalar_sqrt(sc, base, tc, sem_va, cnt_va, sem_std, sem_st, cnt_st_war):
        c = base + (tc % 2) * 8
        wg(sc, sem_va, cnt_va)
        wg(sc, sem_st, cnt_st_war)
        sc.activation(st[:, c + 4:c + 5], st[:, c + 3:c + 4], AF.Sqrt,
                      bias=1e-5).then_inc(S[sem_std], 1)

    def ln_scalar_apply(sc, dst, src, base, tc, sem_st, cnt_st, sem_xn):
        c = base + (tc % 2) * 8
        wg(sc, sem_st, cnt_st)
        sc.activation(dst, src, AF.Identity, bias=st[:, c + 6:c + 7],
                      scale=st[:, c + 5:c + 6]).then_inc(S[sem_xn], 1)

    with nc.Block() as ba:
        @ba.gpsimd
        def _(g):
            slices = [(s_t, d_t, e, hs) for s_t, d_t in ((W2H, W2b),)
                      for e in range(NEXP) for hs in range(8)]
            for i in range(len(slices) + 1):
                if i < len(slices):
                    s_t, d_t, e, hs = slices[i]
                    wg(g, "pww", 16 * max(i - 1, 0))
                    g.dma_start(out=pwb[i % 2][:], in_=s_t[e, hs],
                                max_dma_last_dim=8192).then_inc(S["pwc"], 16)
                if i >= 1:
                    s_t, d_t, e, hs = slices[i - 1]
                    g.wait_ge(S["pwc"], 16 * i)
                    g.dma_start(out=d_t[e, hs], in_=pwb[(i - 1) % 2][:]
                                ).then_inc(S["pww"], 16)
            g.wait_ge(S["pww"], 16 * 72)

        @ba.sync
        def _(sy):
            for j in range(NITER):
                wg(sy, "xn1", 4 * j)
                wg(sy, "st1", 4 * j)
                wg(sy, "sq1", 4 * j)
                wg(sy, "x1", 4 * (j - 1))
                src = x_halo if j == 0 else x_own[(j - 1) * 512:j * 512, :]
                sy.dma_start(out=x_sb[:],
                             in_=src.rearrange("(tc p) f -> p tc f", p=128)
                             ).then_inc(S["x"], 16)
                if j == 0:
                    continue
                b = j - 1
                for tc in range(4):
                    wg(sy, "x1", 4 * b + tc + 1)
                    r0 = b * 512 + tc * 128
                    sy.dma_start(out=x1_hbm[r0:r0 + 128, :],
                                 in_=x1t[:, (tc % 2) * 1024:(tc % 2) * 1024 + 1024]
                                 ).then_inc(S["stx1"], 16)
                    wg(sy, "xn2", 4 * b + tc + 1)
                    sy.dma_start(out=xn2_hbm[r0:r0 + 128, :],
                                 in_=xn2t[:, (tc % 2) * 1024:(tc % 2) * 1024 + 1024]
                                 ).then_inc(S["stxn2"], 16)
            sy.wait_ge(S["stx1"], 16 * 16)
            sy.wait_ge(S["stxn2"], 16 * 16)

        @ba.tensor
        def _(te):
            te.wait_ge(S["pl"], PL)
            te.wait_ge(S["plg"], PLG)
            for j in range(NITER):
                b = j - 1
                slot = j % 2
                wg(te, "xn1", 4 * (j + 1))
                for ec in range(8):
                    wg(te, "tpe", 8 * j + ec)
                    if ec <= 1:
                        wg(te, "pte", 64 * max(b, 0))
                    for tc in range(4):
                        mm = te.transpose(ps_pt[ec % 2][:, tc * 128:tc * 128 + 128],
                                          xn1[:, tc * 1024 + ec * 128:tc * 1024 + ec * 128 + 128],
                                          idbf[:])
                    mm.then_inc(S["tpp"], 1)
                wg(te, "tpe", 8 * (j + 1))
                wg(te, "x1", 4 * (j - 1))
                nt = 0
                if j >= 1:
                    for fcp in range(4):
                        wg(te, "qke", qke_cum(j - 1) + max(nt - 2, 0))
                        pa = ps_a[nt % 3]
                        for half in range(2):
                            fc = fcp * 2 + half
                            for ec in range(8):
                                mm = te.matmul(pa[:, half * 512:half * 512 + 512],
                                               winT[:, ec * 3072 + fc * 128:ec * 3072 + fc * 128 + 128],
                                               xn1T[:, ec * 512:ec * 512 + 512],
                                               start=(ec == 0), stop=(ec == 7))
                        mm.then_inc(S["qkm"], 1)
                        nt += 1
                for fcp in range(4):
                    wg(te, "qke", qke_cum(j - 1) + max(nt - 2, 0))
                    pa = ps_a[nt % 3]
                    for half in range(2):
                        fc = fcp * 2 + half
                        for ec in range(8):
                            mm = te.matmul(pa[:, half * 512:half * 512 + 512],
                                           winT[:, ec * 3072 + 1024 + fc * 128:ec * 3072 + 1024 + fc * 128 + 128],
                                           xn1T[:, ec * 512:ec * 512 + 512],
                                           start=(ec == 0), stop=(ec == 7))
                    mm.then_inc(S["qkm"], 1)
                    nt += 1
                for tc in range(4):
                    wg(te, "qke", qke_cum(j - 1) + max(nt - 2, 0))
                    pa = ps_a[nt % 3]
                    for n in range(2):
                        for ec in range(8):
                            mm = te.matmul(pa[:, n * 512:n * 512 + 512],
                                           xn1T[:, ec * 512 + tc * 128:ec * 512 + tc * 128 + 128],
                                           winT[:, ec * 3072 + 2048 + n * 512:ec * 3072 + 2048 + n * 512 + 512],
                                           start=(ec == 0), stop=(ec == 7))
                    mm.then_inc(S["qkm"], 1)
                    nt += 1

                if j == 0:
                    continue
                wg(te, "qke", qke_cum(j))
                prev, cur = (j - 1) % 2, j % 2

                def emit_scores(s):
                    h, qc = s // 4, s % 4
                    hp, hh = h // 2, (h % 2) * 64
                    n1 = 512 - qc * 128
                    wg(te, "sm1", 64 * b + max(s - 2, 0))
                    pa = ps_a[s % 3]
                    q_ap = q_sb[hh:hh + 64, hp * 512 + qc * 128:hp * 512 + qc * 128 + 128]
                    te.matmul(pa[:, 0:n1], q_ap,
                              k_ring[hh:hh + 64, hp * 1024 + prev * 512 + qc * 128:hp * 1024 + prev * 512 + 512],
                              start=True, stop=True)
                    if qc > 0:
                        te.matmul(pa[:, n1:512], q_ap,
                                  k_ring[hh:hh + 64, hp * 1024 + cur * 512:hp * 1024 + cur * 512 + qc * 128],
                                  start=True, stop=True)
                    te.matmul(pa[:, 512:640], q_ap,
                              k_ring[hh:hh + 64, hp * 1024 + cur * 512 + qc * 128:hp * 1024 + cur * 512 + qc * 128 + 128],
                              start=True, stop=True).then_inc(S["sc"], 1)

                def emit_transposes(sg):
                    wg(te, "pn", 64 * b + sg + 1)
                    if sg >= 1:
                        wg(te, "pte", 64 * b + sg)
                    else:
                        wg(te, "tpe", 8 * (j + 1))
                    for kc in range(5):
                        mm = te.transpose(ps_pt[sg % 2][:, kc * 128:kc * 128 + 128],
                                          p_sb[:, (sg % 3) * 640 + kc * 128:(sg % 3) * 640 + kc * 128 + 128],
                                          idbf[:])
                    mm.then_inc(S["tp"], 1)

                def emit_pv(h):
                    pair, hh = h // 2, (h % 2) * 64
                    wg(te, "pte", 64 * b + 4 * h + 4)
                    if h % 2 == 0 and pair >= 1:
                        wg(te, "oe", 8 * b + pair)
                    po = ps_o[pair % 2]
                    for qc in range(4):
                        for kc in range(5):
                            wc = qc + kc
                            sl, tc2 = (prev, wc) if wc < 4 else (cur, wc - 4)
                            mm = te.matmul(po[hh:hh + 64, qc * 128:qc * 128 + 128],
                                           v_ring[:, (sl * 4 + tc2) * 1024 + h * 64:(sl * 4 + tc2) * 1024 + h * 64 + 64],
                                           pT_sb[:, (h % 2) * 2560 + qc * 640 + kc * 128:(h % 2) * 2560 + qc * 640 + kc * 128 + 128],
                                           start=(kc == 0), stop=(kc == 4))
                    mm.then_inc(S["pv"], 1)

                for s in range(66):
                    if s < 64:
                        emit_scores(s)
                    if s >= 2:
                        emit_transposes(s - 2)
                        if (s - 2) % 4 == 3:
                            emit_pv((s - 2) // 4)
                wg(te, "oe", 8 * (b + 1))
                wg(te, "sm1", 64 * (b + 1))
                for tc in range(4):
                    wg(te, "x1", 4 * b + max(tc - 2, 0))
                    pa = ps_a[tc % 3]
                    for n in range(2):
                        for ec in range(8):
                            mm = te.matmul(pa[:, n * 512:n * 512 + 512],
                                           o_sb[:, ec * 512 + tc * 128:ec * 512 + tc * 128 + 128],
                                           woutT[:, ec * 1024 + n * 512:ec * 1024 + n * 512 + 512],
                                           start=(ec == 0), stop=(ec == 7))
                    mm.then_inc(S["op"], 1)

        @ba.scalar
        def _(sc):
            sc.wait_ge(S["pl"], PL)
            sc.wait_ge(S["plg"], PLG)
            for j in range(NITER):
                b = j - 1
                slot = j % 2
                wg(sc, "x", 16 * (j + 1))
                xoff = 0
                for tc in range(4):
                    src = x_sb[:, tc * 1024:tc * 1024 + 1024]
                    ln_scalar_sq(sc, src, 0, tc, "sq1", 0, "va1", 4 * j + tc - 1)
                    ln_scalar_sqrt(sc, 0, tc, "va1", 4 * j + tc + 1, "std1", "st1",
                                   4 * j + tc - 1)
                    ln_scalar_apply(sc, xn1[:, tc * 1024:tc * 1024 + 1024], src,
                                    0, tc, "st1", 4 * j + tc + 1, "xn1")
                for ec in range(8):
                    wg(sc, "tpp", 8 * j + ec + 1)
                    sc.copy(xn1T[:, ec * 512:ec * 512 + 512],
                            ps_pt[ec % 2][:, 0:512]).then_inc(S["tpe"], 1)
                wg(sc, "pv", 16 * (j - 1))
                nt = 0
                if j >= 1:
                    for fcp in range(4):
                        wg(sc, "qkm", qke_cum(j - 1) + nt + 1)
                        sc.activation(q_sb[:, fcp * 1024:fcp * 1024 + 1024],
                                      ps_a[nt % 3][:, 0:1024], AF.Identity,
                                      scale=0.125).then_inc(S["qke"], 1)
                        nt += 1
                for fcp in range(4):
                    wg(sc, "qkm", qke_cum(j - 1) + nt + 1)
                    for half in range(2):
                        fc = fcp * 2 + half
                        ins = sc.copy(k_ring[:, fc * 1024 + slot * 512:fc * 1024 + slot * 512 + 512],
                                      ps_a[nt % 3][:, half * 512:half * 512 + 512])
                    ins.then_inc(S["qke"], 1)
                    nt += 1
                for tc in range(4):
                    wg(sc, "qkm", qke_cum(j - 1) + nt + 1)
                    sc.copy(v_ring[:, (slot * 4 + tc) * 1024:(slot * 4 + tc) * 1024 + 1024],
                            ps_a[nt % 3][:, 0:1024]).then_inc(S["qke"], 1)
                    nt += 1

                if j == 0:
                    continue

                def emit_pte(sg):
                    h2 = sg // 4
                    wg(sc, "tp", 64 * b + sg + 1)
                    if sg % 4 == 0:
                        wg(sc, "pv", 16 * b + max(h2 - 1, 0))
                    sc.copy(pT_sb[:, (h2 % 2) * 2560 + (sg % 4) * 640:(h2 % 2) * 2560 + (sg % 4) * 640 + 640],
                            ps_pt[sg % 2][:, 0:640]).then_inc(S["pte"], 1)
                    if sg % 8 == 7:
                        pair2 = sg // 8
                        wg(sc, "pv", 16 * b + 2 * pair2 + 2)
                        sc.copy(o_sb[:, pair2 * 512:pair2 * 512 + 512],
                                ps_o[pair2 % 2][:, 0:512]).then_inc(S["oe"], 1)

                for s in range(66):
                    if s < 64:
                        wg(sc, "t2", 64 * b + s + 1)
                        wg(sc, "tp", 64 * b + s - 2)
                        wg(sc, "pn", 64 * b + s - 2)
                        nmc, smc = 16 + (s % 3), 19 + (s % 3)
                        sc.activation(p_sb[:, (s % 3) * 640:(s % 3) * 640 + 640],
                                      t_sb[:, (s % 3) * 640:(s % 3) * 640 + 640],
                                      AF.Exp, bias=st[:, nmc:nmc + 1],
                                      accum_out=st[:, smc:smc + 1]).then_inc(S["p"], 1)
                    if s >= 2:
                        emit_pte(s - 2)
                for tc in range(4):
                    wg(sc, "x1", 4 * b + tc + 1)
                    src = x1t[:, (tc % 2) * 1024:(tc % 2) * 1024 + 1024]
                    ln_scalar_sq(sc, src, 28, tc, "sq2", 0, "va2", 4 * b + tc - 1)
                    ln_scalar_sqrt(sc, 28, tc, "va2", 4 * b + tc + 1, "std2", "st2",
                                   4 * b + tc - 1)
                    wg(sc, "stxn2", 16 * (4 * b + tc - 1))
                    ln_scalar_apply(sc, xn2t[:, (tc % 2) * 1024:(tc % 2) * 1024 + 1024],
                                    src, 28, tc, "st2", 4 * b + tc + 1, "xn2")

        @ba.vector
        def _(v):
            v.wait_ge(S["pl"], PL)
            v.wait_ge(S["plg"], PLG)
            for j in range(NITER):
                b = j - 1
                wg(v, "x", 16 * (j + 1))
                for tc in range(4):
                    ln_vector_stats(v, x_sb[:, tc * 1024:tc * 1024 + 1024], 0, tc,
                                    "sq1", 4 * j + tc + 1, "va1", "std1",
                                    4 * j + tc + 1, "st1", "xn1", 4 * j + tc - 1)
                if j == 0:
                    continue

                def emit_pn(sp):
                    wg(v, "p", 64 * b + sp + 1)
                    smc, rc = 19 + (sp % 3), 22 + (sp % 3)
                    vmark(v.reciprocal(st[:, rc:rc + 1], st[:, smc:smc + 1]))
                    vwait(v)
                    pp = p_sb[:, (sp % 3) * 640:(sp % 3) * 640 + 640]
                    v.tensor_scalar(pp, pp, st[:, rc:rc + 1], 1.0,
                                    op0=OP.mult, op1=OP.mult).then_inc(S["pn"], 1)

                for s in range(64):
                    h, qc = s // 4, s % 4
                    n1 = 512 - qc * 128
                    wg(v, "sc", 64 * b + s + 1)
                    wg(v, "p", 64 * b + s - 2)
                    t0 = (s % 3) * 640
                    tt = t_sb[:, t0:t0 + 640]
                    v.scalar_tensor_tensor(tt, kjb[:, qc * 128:qc * 128 + 640],
                                           -SLOPES[h], ps_a[s % 3][:, 0:640],
                                           op0=OP.mult, op1=OP.add).then_inc(S["sm1"], 1)
                    v.tensor_tensor(t_sb[:, t0:t0 + 128], t_sb[:, t0:t0 + 128],
                                    tprev[:], op=OP.add)
                    v.tensor_tensor(t_sb[:, t0 + 512:t0 + 640], t_sb[:, t0 + 512:t0 + 640],
                                    tcur[:], op=OP.add)
                    if b == 0:
                        vmark(v.tensor_tensor(t_sb[:, t0:t0 + n1], t_sb[:, t0:t0 + n1],
                                              maskP0[:, qc * 128:512], op=OP.add))
                        vwait(v)
                    nmc = 16 + (s % 3)
                    v.reduce_max(st[:, nmc:nmc + 1], tt, axis=AX,
                                 negate=True).then_inc(S["t2"], 1)
                    if s >= 1:
                        emit_pn(s - 1)
                if True:
                    emit_pn(63)
                for tc in range(4):
                    wg(v, "op", 4 * b + tc + 1)
                    wg(v, "xn2", 4 * b + tc - 1)
                    wg(v, "stx1", 16 * (4 * b + tc - 1))
                    v.tensor_tensor(x1t[:, (tc % 2) * 1024:(tc % 2) * 1024 + 1024],
                                    x_sb[:, tc * 1024:tc * 1024 + 1024],
                                    ps_a[tc % 3][:, 0:1024], op=OP.add).then_inc(S["x1"], 1)
                    ln_vector_stats(v, x1t[:, (tc % 2) * 1024:(tc % 2) * 1024 + 1024],
                                    28, tc, "sq2", 4 * b + tc + 1, "va2", "std2",
                                    4 * b + tc + 1, "st2", "xn2", 4 * b + tc - 1)

    if debug2:
        dbg_names = [("d_x", x_sb, F32, 4096), ("d_xn1", xn1, BF16, 4096),
                     ("d_xn1T", xn1T, BF16, 4096), ("d_q", q_sb, BF16, 4096),
                     ("d_k", k_ring, BF16, 8192), ("d_v", v_ring, BF16, 8192),
                     ("d_t", t_sb, F32, 2048), ("d_p", p_sb, BF16, 2048),
                     ("d_pT", pT_sb, BF16, 4096), ("d_o", o_sb, BF16, 4096),
                     ("d_st", st, F32, 48)]
        dbg_outs = {nm: nc.dram_tensor(nm, [128, w], dt, kind="ExternalOutput").ap()
                    for nm, _, dt, w in dbg_names}
        sdbg = nc.alloc_semaphore("s_dbg")
        with nc.Block() as bdg:
            @bdg.sync
            def _(sy):
                for nm, tile, dt, w in dbg_names:
                    sy.dma_start(out=dbg_outs[nm], in_=tile[:]).then_inc(sdbg, 16)
                sy.wait_ge(sdbg, 16 * len(dbg_names))

    stackA.close()
    psA.close()

    stackB = ExitStack()

    def SBB(name, shape, dt):
        return stackB.enter_context(nc.sbuf_tensor(name, shape, dt))

    stackB2 = ExitStack()
    acc = stackB2.enter_context(nc.sbuf_tensor("b_acc", [128, 8 * 2048], BF16))
    pos1u = stackB2.enter_context(nc.sbuf_tensor("b_pos1u", [128, 16], mybir.dt.uint32))
    pos2u = stackB2.enter_context(nc.sbuf_tensor("b_pos2u", [128, 16], mybir.dt.uint32))
    gcol1 = stackB2.enter_context(nc.sbuf_tensor("b_gcol1", [128, 16], F32))
    gcol2 = stackB2.enter_context(nc.sbuf_tensor("b_gcol2", [128, 16], F32))

    xn2T = SBB("b_xn2T", [128, 8 * 1024], BF16)
    h_sb = SBB("b_h", [128, 32 * 1024], BF16)
    W1_sb = [SBB("b_w1_%d" % i, [128, 4096], BF16) for i in range(2)]
    W2_sb = [SBB("b_w2_%d" % i, [128, 4096], BF16) for i in range(2)]
    lg_sb = SBB("b_lg", [128, 8], F32)
    lgm_sb = SBB("b_lgm", [128, 8], F32)
    eq1_sb = SBB("b_eq1", [128, 64], F32)
    eq2_sb = SBB("b_eq2", [128, 64], F32)
    d12_sb = SBB("b_d12", [128, 8], F32)
    g1_sb = SBB("b_g1", [128, 8], F32)
    g2_sb = SBB("b_g2", [128, 8], F32)
    gt_sb = SBB("b_gt", [128, 8], F32)
    gatef_sb = SBB("b_gatef", [128, 64], F32)
    gst = SBB("b_gst", [128, 4], F32)
    g8_sb = SBB("b_g8", [8, 1024], F32)
    me8 = SBB("b_me8", [8, 1024], F32)
    zer8 = SBB("b_zer8", [8, 1024], F32)
    posi8 = SBB("b_posi8", [8, 1024], F32)
    posT_sb = SBB("b_posT", [128, 64], F32)
    pv1 = SBB("b_pv1", [128, 8], F32)
    pv2 = SBB("b_pv2", [128, 8], F32)
    pv3 = SBB("b_pv3", [128, 8], F32)
    pc1 = SBB("b_pc1", [128, 1], F32)
    idxu_sb = SBB("b_idxu", [128, 64], mybir.dt.uint32)
    idxg_sb = [SBB("b_idxg%d" % i, [128, 3], mybir.dt.uint32) for i in range(2)]
    xg_nat = [SBB("b_xg0", [128, 3 * 1024], BF16)]
    xgT = SBB("b_xgT", [128, 8 * CAP], BF16)
    op_fm = SBB("b_opfm", [128, 8 * CAP], BF16)
    oprow = [SBB("b_oprow%d" % i, [128, 1024], BF16) for i in range(2)]

    psB = ExitStack()
    ps_fc = [psB.enter_context(nc.psum_tensor("ps_fc%d" % i, [128, 512], F32)) for i in range(2)]
    ps_out = [psB.enter_context(nc.psum_tensor("ps_out%d" % i, [128, 1024], F32)) for i in range(2)]
    ps_tb = psB.enter_context(nc.psum_tensor("ps_tb", [128, 1024], BF16))
    ps_tb2 = psB.enter_context(nc.psum_tensor("ps_tb2", [128, 1024], BF16))

    def fcp_slice_end(i1):
        if i1 < 0:
            return 0
        tth, r = divmod(i1, 72)
        e, hs = divmod(r, 8)
        base = 320 * tth
        if e == 0:
            return base + 8 * (hs + 1)
        return base + 64 + 4 * ((e - 1) * 8 + hs + 1)

    def fcp_e_end(tth, e):
        return 320 * tth + (64 if e >= 0 else 0) + 4 * 8 * max(e, 0)

    def gel_e_end(tth, e):
        return fcp_e_end(tth, e)

    with nc.Block() as bb:
        @bb.sync
        def _(sy):
            for tth in range(2):
                wg(sy, "fcp", 320 * tth)
                for ec in range(8):
                    sy.dma_start(out=xn2T[:, ec * 1024:ec * 1024 + 1024],
                                 in_=xn2_hbm[tth * 1024:tth * 1024 + 1024, ec * 128:ec * 128 + 128],
                                 transpose=True).then_inc(S["xn2t"], 16)
                wg(sy, "g8v", tth + 1)
                sy.dma_start(out=g8_hbm[:, tth * 1024:tth * 1024 + 1024],
                             in_=g8_sb[0:8, :]).then_inc(S["g8st"], 16)
                for e in range(1, NEXP):
                    slot = tth * 8 + (e - 1)
                    for gi in range(3):
                        og = slot * 3 + gi
                        wg(sy, "opv", og + 1)
                        sy.dma_start(out=out_perm[slot * CAP + gi * 128:slot * CAP + gi * 128 + 128, :],
                                     in_=oprow[og % 2][:]).then_inc(S["ops"], 16)

        @bb.gpsimd
        def _(g):
            r_cap = g.alloc_register()
            g.reg_mov(r_cap, CAP - 1)
            r_tok = g.alloc_register()
            g.reg_mov(r_tok, 1023)
            for tth in range(2):
                for e in range(NEXP):
                    if e >= 1:
                        ei = e - 1
                        slot = tth * 8 + ei
                        for tc in range(8):
                            wg(g, "ix", 8 * tth + tc + 1)
                            g.indirect_dma_start(
                                out=idxl_hbm,
                                out_offset=bass.IndirectOffsetOnAxis(
                                    idxu_sb[:, tc * 8 + ei:tc * 8 + ei + 1], axis=0),
                                in_=tokid_sb[:, tc:tc + 1], in_offset=None,
                                element_offset=slot * CAP,
                                bounds_check=r_cap, oob_is_err=False
                                ).then_inc(S["sca"], 16)
                        wg(g, "sca", 16 * (64 * tth + ei * 8 + 8))
                        wg(g, "xga", 16 * max(24 * tth + 3 * (e - 2), 0))
                        g.dma_start(out=idxg_sb[ei % 2][:],
                                    in_=idxl_hbm[slot * CAP:slot * CAP + CAP, :]
                                    .rearrange("(gi p) w -> p (gi w)", p=128),
                                    allow_slow_non_contiguous=True
                                    ).then_inc(S["ixl"], 16)
                        wg(g, "ixl", 16 * (8 * tth + e))
                        wg(g, "xgp", 8 * (8 * tth + ei))
                        for gi in range(3):
                            g.indirect_dma_start(
                                out=xg_nat[0][:, gi * 1024:gi * 1024 + 1024],
                                out_offset=None,
                                in_=xn2_hbm,
                                in_offset=bass.IndirectOffsetOnAxis(
                                    idxg_sb[ei % 2][:, gi:gi + 1], axis=0),
                                element_offset=tth * 1024 * 1024,
                                bounds_check=r_tok, oob_is_err=False
                                ).then_inc(S["xga"], 16)
                    for hs in range(8):
                        i1 = (tth * 9 + e) * 8 + hs
                        wg(g, "fcp", fcp_slice_end(i1 - 2))
                        g.dma_start(out=W1_sb[i1 % 2][:], in_=W1H[e, hs],
                                    max_dma_last_dim=8192).then_inc(S["w1"], 16)
                    for m in range(8):
                        i2 = (tth * 9 + e) * 8 + m
                        wg(g, "pse", i2 - 1)
                        g.dma_start(out=W2_sb[i2 % 2][:], in_=W2b[e, m]
                                    ).then_inc(S["w2"], 16)

        @bb.tensor
        def _(te):
            gfcp = [0]

            def fc_tile(i1pct, pslice, lhs_ap, rhs_ap):
                it = gfcp[0]
                wg(te, "gel", max(it - 1, 0))
                for ec in range(8):
                    mm = te.matmul(pslice, lhs_ap(ec), rhs_ap(ec),
                                   start=(ec == 0), stop=(ec == 7))
                mm.then_inc(S["fcp"], 1)
                gfcp[0] += 1
                return it

            for tth in range(2):
                wg(te, "xn2t", 16 * 8 * (tth + 1))
                wg(te, "gel", 320 * tth)
                wg(te, "pse", 72 * tth)
                for tc in range(8):
                    for ec in range(8):
                        mm = te.matmul(ps_fc[1][:, tc * 8:tc * 8 + 8],
                                       xn2T[:, ec * 1024 + tc * 128:ec * 1024 + tc * 128 + 128],
                                       wr_sb[:, ec * 8:ec * 8 + 8],
                                       start=(ec == 0), stop=(ec == 7))
                    mm.then_inc(S["rtm"], 1)
                for tc in range(8):
                    wg(te, "gv", 8 * tth + tc + 1)
                    te.transpose(ps_out[0][0:8, tc * 128:tc * 128 + 128],
                                 gatef_sb[:, tc * 8:tc * 8 + 8],
                                 idf32[:]).then_inc(S["gtt"], 1)
                wg(te, "g8v", tth + 1)
                wg(te, "pscn", tth + 1)
                for tc in range(8):
                    mm = te.transpose(ps_out[0][:, tc * 8:tc * 8 + 8],
                                      posi8[0:8, tc * 128:tc * 128 + 128],
                                      idf32[0:8, 0:8])
                mm.then_inc(S["ptt"], 1)
                wg(te, "gv1", 8 * (tth + 1))
                for e in range(NEXP):
                    ei = e - 1
                    if e >= 1:
                        wg(te, "xga", 16 * (24 * tth + 3 * e))
                        for ec in range(8):
                            xgpc = 64 * tth + 8 * ei + ec
                            wg(te, "xge", xgpc)
                            half = (ec % 2) * 512
                            for gi in range(3):
                                mm = te.transpose(ps_tb[:, half + gi * 128:half + gi * 128 + 128],
                                                  xg_nat[0][:, gi * 1024 + ec * 128:gi * 1024 + ec * 128 + 128],
                                                  idbf[:])
                            mm.then_inc(S["xgp"], 1)
                    for hs in range(8):
                        i1 = (tth * 9 + e) * 8 + hs
                        wg(te, "w1", 16 * (i1 + 1))
                        if e == 0 and hs == 0:
                            wg(te, "ptv", tth + 1)
                        if e >= 1 and hs == 0:
                            wg(te, "xge", 64 * tth + 8 * ei + 8)
                        for mf in range(4):
                            if e == 0:
                                for half in range(2):
                                    it = gfcp[0]
                                    pf = ps_fc[it % 2]
                                    wg(te, "gel", max(it - 1, 0))
                                    for ec in range(8):
                                        mm = te.matmul(pf[:, 0:512],
                                                       W1_sb[i1 % 2][:, ec * 512 + mf * 128:ec * 512 + mf * 128 + 128],
                                                       xn2T[:, ec * 1024 + half * 512:ec * 1024 + half * 512 + 512],
                                                       start=(ec == 0), stop=(ec == 7))
                                    mm.then_inc(S["fcp"], 1)
                                    gfcp[0] += 1
                            else:
                                it = gfcp[0]
                                pf = ps_fc[it % 2]
                                wg(te, "gel", max(it - 1, 0))
                                for ec in range(8):
                                    mm = te.matmul(pf[:, 0:CAP],
                                                   W1_sb[i1 % 2][:, ec * 512 + mf * 128:ec * 512 + mf * 128 + 128],
                                                   xgT[:, ec * CAP:ec * CAP + CAP],
                                                   start=(ec == 0), stop=(ec == 7))
                                mm.then_inc(S["fcp"], 1)
                                gfcp[0] += 1
                    wg(te, "gel", gel_e_end(tth, e))
                    for m in range(8):
                        i2 = (tth * 9 + e) * 8 + m
                        wg(te, "w2", 16 * (i2 + 1))
                        wg(te, "pse", max(i2 - 1, 0))
                        po = ps_out[i2 % 2]
                        if e == 0:
                            for hb in range(32):
                                for n in range(2):
                                    mm = te.matmul(po[:, n * 512:n * 512 + 512],
                                                   W2_sb[i2 % 2][:, hb * 128:hb * 128 + 128],
                                                   h_sb[:, hb * 1024 + n * 512:hb * 1024 + n * 512 + 512],
                                                   start=(hb == 0), stop=(hb == 31))
                        else:
                            for hb in range(32):
                                mm = te.matmul(po[:, 0:CAP],
                                               W2_sb[i2 % 2][:, hb * 128:hb * 128 + 128],
                                               h_sb[:, hb * 1024:hb * 1024 + CAP],
                                               start=(hb == 0), stop=(hb == 31))
                        mm.then_inc(S["prm"], 1)
                    if e >= 1:
                        for gi in range(3):
                            og = (tth * 8 + ei) * 3 + gi
                            wg(te, "pse", 72 * tth + 8 * (e + 1))
                            wg(te, "opv", og)
                            for m in range(8):
                                mm = te.transpose(ps_tb2[:, m * 128:m * 128 + 128],
                                                  op_fm[:, m * CAP + gi * 128:m * CAP + gi * 128 + 128],
                                                  idbf[:])
                            mm.then_inc(S["ott"], 1)

        @bb.scalar
        def _(sc):
            gfcp = [0]
            for tth in range(2):
                wg(sc, "gv1", 8 * (tth + 1))
                sc.activation(g1_sb[:], d12_sb[:], AF.Sigmoid)
                sc.activation(g2_sb[:], d12_sb[:], AF.Sigmoid,
                              scale=-1.0).then_inc(S["gs"], 1)
                for e in range(NEXP):
                    ei = e - 1
                    if e >= 1:
                        for ec in range(8):
                            xgpc = 64 * tth + 8 * ei + ec
                            wg(sc, "xgp", xgpc + 1)
                            if ec == 0:
                                wg(sc, "fcp", fcp_e_end(tth, e - 1))
                            half = (ec % 2) * 512
                            sc.copy(xgT[:, ec * CAP:ec * CAP + CAP],
                                    ps_tb[:, half:half + CAP]).then_inc(S["xge"], 1)
                    wg(sc, "prm", 8 * (tth * 9 + e))
                    for hs in range(8):
                        for mf in range(4):
                            hb = hs * 4 + mf
                            bcol = bfc_sb[:, e * 32 + hb:e * 32 + hb + 1]
                            if e == 0:
                                for half in range(2):
                                    it = gfcp[0]
                                    wg(sc, "fcp", it + 1)
                                    sc.activation(h_sb[:, hb * 1024 + half * 512:hb * 1024 + half * 512 + 512],
                                                  ps_fc[it % 2][:, 0:512], AF.Gelu_apprx_tanh,
                                                  bias=bcol).then_inc(S["gel"], 1)
                                    gfcp[0] += 1
                            else:
                                it = gfcp[0]
                                wg(sc, "fcp", it + 1)
                                sc.activation(h_sb[:, hb * 1024:hb * 1024 + CAP],
                                              ps_fc[it % 2][:, 0:CAP], AF.Gelu_apprx_tanh,
                                              bias=bcol).then_inc(S["gel"], 1)
                                gfcp[0] += 1
                    if e >= 1:
                        for m in range(8):
                            i2 = (tth * 9 + e) * 8 + m
                            wg(sc, "prm", i2 + 1)
                            if m == 0:
                                wg(sc, "ott", 3 * max(tth * 8 + ei, 0))
                            sc.activation(op_fm[:, m * CAP:m * CAP + CAP],
                                          ps_out[i2 % 2][:, 0:CAP], AF.Identity,
                                          bias=bproj_sb[:, e * 8 + m:e * 8 + m + 1]
                                          ).then_inc(S["pse"], 1)
                        for gi in range(3):
                            og = (tth * 8 + ei) * 3 + gi
                            wg(sc, "ott", og + 1)
                            wg(sc, "ops", 16 * max(og - 1, 0))
                            sc.copy(oprow[og % 2][:], ps_tb2[:]).then_inc(S["opv"], 1)

        @bb.vector
        def _(v):
            v.memset(zer8[:], 0.0)
            for tth in range(2):
                wg(v, "rtm", 8 * (tth + 1))
                for tc in range(8):
                    vmark(v.tensor_copy(lg_sb[:], ps_fc[1][:, tc * 8:tc * 8 + 8]))
                    vwait(v)
                    vmark(v.reduce_max(gst[:, 0:1], lg_sb[:], axis=AX))
                    vwait(v)
                    vmark(v.tensor_scalar(eq1_sb[:, tc * 8:tc * 8 + 8], lg_sb[:],
                                          gst[:, 0:1], 1.0, op0=OP.is_ge, op1=OP.mult))
                    vwait(v)
                    vmark(v.scalar_tensor_tensor(lgm_sb[:], eq1_sb[:, tc * 8:tc * 8 + 8],
                                                 -1.0e9, lg_sb[:], op0=OP.mult, op1=OP.add))
                    vwait(v)
                    vmark(v.reduce_max(gst[:, 1:2], lgm_sb[:], axis=AX))
                    vwait(v)
                    vmark(v.tensor_scalar(eq2_sb[:, tc * 8:tc * 8 + 8], lgm_sb[:],
                                          gst[:, 1:2], 1.0, op0=OP.is_ge, op1=OP.mult))
                    vwait(v)
                    v.tensor_tensor(d12_sb[:, tc:tc + 1], gst[:, 0:1], gst[:, 1:2],
                                    op=OP.subtract).then_inc(S["gv1"], 1)
                wg(v, "gs", tth + 1)
                wg(v, "gtt", 8 * tth)
                for tc in range(8):
                    vmark(v.tensor_scalar(gt_sb[:], eq1_sb[:, tc * 8:tc * 8 + 8],
                                          g1_sb[:, tc:tc + 1], 1.0, op0=OP.mult, op1=OP.mult))
                    vwait(v)
                    v.scalar_tensor_tensor(gatef_sb[:, tc * 8:tc * 8 + 8],
                                           eq2_sb[:, tc * 8:tc * 8 + 8],
                                           g2_sb[:, tc:tc + 1], gt_sb[:],
                                           op0=OP.mult, op1=OP.add).then_inc(S["gv"], 1)
                wg(v, "gtt", 8 * (tth + 1))
                wg(v, "g8st", 16 * tth)
                v.tensor_copy(g8_sb[0:8, :], ps_out[0][0:8, 0:1024]).then_inc(S["g8v"], 1)
                v.wait_ge(S["g8v"], tth + 1)
                vmark(v.tensor_scalar(me8[:], g8_sb[0:8, :], 0.0, 1.0,
                                      op0=OP.is_gt, op1=OP.mult))
                vwait(v)
                vmark(v.tensor_tensor_scan(posi8[:], me8[:], zer8[:], 0.0,
                                           op0=OP.add, op1=OP.add))
                vwait(v)
                v.tensor_tensor(posi8[:], posi8[:], me8[:],
                                op=OP.subtract).then_inc(S["pscn"], 1)
                wg(v, "ptt", tth + 1)
                wg(v, "sca", 16 * 64 * tth)
                v.tensor_copy(posT_sb[:], ps_out[0][:, 0:64]).then_inc(S["ptv"], 1)
                v.wait_ge(S["ptv"], tth + 1)
                for tc in range(8):
                    tg = tth * 8 + tc
                    rank = posT_sb[:, tc * 8:tc * 8 + 8]
                    vmark(v.tensor_scalar(pv3[:], gatef_sb[:, tc * 8:tc * 8 + 8],
                                          0.0, 1.0, op0=OP.is_gt, op1=OP.mult))
                    vwait(v)
                    vmark(v.tensor_scalar(pv2[:], pv3[:], 1.0, -1.0e9,
                                          op0=OP.subtract, op1=OP.mult))
                    vwait(v)
                    vmark(v.tensor_tensor(pv1[:], rank, pv2[:], op=OP.add))
                    vwait(v)
                    vmark(v.tensor_copy(idxu_sb[:, tc * 8:tc * 8 + 8], pv1[:]))
                    vwait(v)
                    vmark(v.tensor_tensor(pv1[:], rank, posbase_sb[:, tth * 8:tth * 8 + 8],
                                          op=OP.add))
                    vwait(v)
                    vmark(v.tensor_scalar(pv2[:], rank, CAP - 0.5, 1.0,
                                          op0=OP.is_gt, op1=OP.mult))
                    vwait(v)
                    vmark(v.tensor_scalar(pv3[:], pv2[:], -1.0, 1.0,
                                          op0=OP.mult, op1=OP.add))
                    vwait(v)
                    vmark(v.tensor_tensor(pv1[:], pv1[:], pv3[:], op=OP.mult))
                    vwait(v)
                    vmark(v.tensor_scalar(pv2[:], pv2[:], float(ZROW), 1.0,
                                          op0=OP.mult, op1=OP.mult))
                    vwait(v)
                    vmark(v.tensor_tensor(pv1[:], pv1[:], pv2[:], op=OP.add))
                    vwait(v)
                    vmark(v.tensor_tensor(pv2[:], eq1_sb[:, tc * 8:tc * 8 + 8], pv1[:],
                                          op=OP.mult))
                    vwait(v)
                    vmark(v.reduce_sum(pc1[:], pv2[:], axis=AX))
                    vwait(v)
                    vmark(v.tensor_scalar(pc1[:], pc1[:], float(ZROW), 1.0,
                                          op0=OP.min, op1=OP.mult))
                    vwait(v)
                    vmark(v.tensor_copy(pos1u[:, tg:tg + 1], pc1[:]))
                    vwait(v)
                    vmark(v.tensor_tensor(pv2[:], eq2_sb[:, tc * 8:tc * 8 + 8], pv1[:],
                                          op=OP.mult))
                    vwait(v)
                    vmark(v.reduce_sum(pc1[:], pv2[:], axis=AX))
                    vwait(v)
                    vmark(v.tensor_scalar(pc1[:], pc1[:], float(ZROW), 1.0,
                                          op0=OP.min, op1=OP.mult))
                    vwait(v)
                    vmark(v.tensor_copy(pos2u[:, tg:tg + 1], pc1[:]))
                    vwait(v)
                    vmark(v.tensor_copy(gcol1[:, tg:tg + 1], g1_sb[:, tc:tc + 1]))
                    vwait(v)
                    v.tensor_copy(gcol2[:, tg:tg + 1], g2_sb[:, tc:tc + 1]
                                  ).then_inc(S["ix"], 1)
                for m in range(8):
                    i2 = (tth * 9 + 0) * 8 + m
                    wg(v, "prm", i2 + 1)
                    aslice = acc[:, m * 2048 + tth * 1024:m * 2048 + tth * 1024 + 1024]
                    v.tensor_scalar(aslice, ps_out[i2 % 2][:, 0:1024],
                                    bproj_sb[:, 0 * 8 + m:0 * 8 + m + 1], 1.0,
                                    op0=OP.add, op1=OP.mult).then_inc(S["pse"], 1)

    stackB.close()
    psB.close()

    stackC = ExitStack()
    x1f = [stackC.enter_context(nc.sbuf_tensor("c_x1f%d" % i, [128, 1024], F32)) for i in range(2)]
    y_sb = [stackC.enter_context(nc.sbuf_tensor("c_y%d" % i, [128, 1024], F32)) for i in range(2)]
    c1_sb = [stackC.enter_context(nc.sbuf_tensor("c_c1_%d" % i, [128, 1024], BF16)) for i in range(2)]
    c2_sb = [stackC.enter_context(nc.sbuf_tensor("c_c2_%d" % i, [128, 1024], BF16)) for i in range(2)]
    tmpc = stackC.enter_context(nc.sbuf_tensor("c_tmpc", [128, 1024], F32))
    psC = ExitStack()
    ps_y = [psC.enter_context(nc.psum_tensor("ps_y%d" % i, [128, 1024], BF16)) for i in range(2)]

    with nc.Block() as bc:
        @bc.sync
        def _(sy):
            for tc in range(16):
                wg(sy, "cy", max(tc - 1, 0))
                sy.dma_start(out=x1f[tc % 2][:],
                             in_=x1_hbm[tc * 128:tc * 128 + 128, :]).then_inc(S["cx1"], 16)
                wg(sy, "cy", tc + 1)
                sy.dma_start(out=y_out[tc * 128:tc * 128 + 128, :],
                             in_=y_sb[tc % 2][:]).then_inc(S["y"], 16)
            sy.wait_ge(S["y"], 16 * 16)

        @bc.gpsimd
        def _(g):
            r_zr = g.alloc_register()
            g.reg_mov(r_zr, ZROW)
            g.wait_ge(S["ops"], 16 * 48)
            g.wait_ge(S["ix"], 16)
            for tc in range(16):
                wg(g, "cy", max(tc - 1, 0))
                for pu, dst in ((pos1u, c1_sb), (pos2u, c2_sb)):
                    g.indirect_dma_start(
                        out=dst[tc % 2][:], out_offset=None,
                        in_=out_perm,
                        in_offset=bass.IndirectOffsetOnAxis(pu[:, tc:tc + 1], axis=0),
                        bounds_check=r_zr, oob_is_err=False
                        ).then_inc(S["cg"], 16)

        @bc.tensor
        def _(te):
            wg(te, "pse", 144)
            for tc in range(16):
                wg(te, "cy", max(tc - 1, 0))
                for m in range(8):
                    mm = te.transpose(ps_y[tc % 2][:, m * 128:m * 128 + 128],
                                      acc[:, m * 2048 + tc * 128:m * 2048 + tc * 128 + 128],
                                      idbf[:])
                mm.then_inc(S["ct"], 1)

        @bc.vector
        def _(v):
            for tc in range(16):
                wg(v, "cx1", 16 * (tc + 1))
                wg(v, "ct", tc + 1)
                wg(v, "cg", 32 * (tc + 1))
                wg(v, "y", 16 * max(tc - 1, 0))
                ys = y_sb[tc % 2][:]
                v.tensor_tensor(ys, x1f[tc % 2][:], ps_y[tc % 2][:, 0:1024],
                                op=OP.add)
                v.tensor_scalar(tmpc[:], c1_sb[tc % 2][:], gcol1[:, tc:tc + 1], 1.0,
                                op0=OP.mult, op1=OP.mult)
                v.tensor_tensor(ys, ys, tmpc[:], op=OP.add)
                v.tensor_scalar(tmpc[:], c2_sb[tc % 2][:], gcol2[:, tc:tc + 1], 1.0,
                                op0=OP.mult, op1=OP.mult)
                v.tensor_tensor(ys, ys, tmpc[:], op=OP.add).then_inc(S["cy"], 1)

    stackC.close()
    psC.close()
    return nc


_CACHED = {}


def _get_nc():
    if "nc" not in _CACHED:
        _CACHED["nc"] = build_program(debug=False)
    return _CACHED["nc"]


def _host_inputs(L0_attn_x, L0_attn_w_in, L0_attn_w_out, L0_ffn_Wfc_s, L0_ffn_bfc_s,
                 L0_ffn_Wproj_s, L0_ffn_bproj_s, L0_ffn_Wr, L0_ffn_Wfc, L0_ffn_bfc,
                 L0_ffn_Wproj, L0_ffn_bproj):
    f32 = np.float32
    x = np.ascontiguousarray(np.asarray(L0_attn_x, f32).reshape(16384, 1024))
    w_inT = np.ascontiguousarray(np.asarray(L0_attn_w_in, f32).T)
    w_outT = np.ascontiguousarray(np.asarray(L0_attn_w_out, f32).T)
    Wfc_all = np.concatenate([np.asarray(L0_ffn_Wfc_s, f32)[None],
                              np.asarray(L0_ffn_Wfc, f32)], 0)
    Wproj_all = np.concatenate([np.asarray(L0_ffn_Wproj_s, f32)[None],
                                np.asarray(L0_ffn_Wproj, f32)], 0)
    bfc_all = np.concatenate([np.asarray(L0_ffn_bfc_s, f32)[None],
                              np.asarray(L0_ffn_bfc, f32)], 0)
    bproj_all = np.concatenate([np.asarray(L0_ffn_bproj_s, f32)[None],
                                np.asarray(L0_ffn_bproj, f32)], 0)
    W1H = np.ascontiguousarray(
        Wfc_all.reshape(NEXP, 8, 128, 8, 512).transpose(0, 3, 2, 1, 4).reshape(NEXP, 8, 128, 4096))
    W2H = np.ascontiguousarray(
        Wproj_all.reshape(NEXP, 32, 128, 8, 128).transpose(0, 3, 2, 1, 4).reshape(NEXP, 8, 128, 4096))
    bfcB = np.ascontiguousarray(bfc_all.reshape(NEXP, 32, 128).transpose(0, 2, 1))
    bprojB = np.ascontiguousarray(bproj_all.reshape(NEXP, 8, 128).transpose(0, 2, 1))
    WrH = np.ascontiguousarray(
        np.asarray(L0_ffn_Wr, f32).reshape(8, 128, 8).transpose(1, 0, 2).reshape(128, 64))
    kjb = np.ascontiguousarray(np.broadcast_to(np.arange(1024, dtype=f32), (128, 1024)))
    qi = np.arange(512, dtype=np.int64)[:, None]
    cc = np.arange(512, dtype=np.int64)[None, :]
    mprev = np.where(qi <= cc, 0.0, NEG).astype(f32)
    mcur = np.where(cc <= qi, 0.0, NEG).astype(f32)
    pp_ = np.arange(128)[:, None]
    cc_ = np.arange(128)[None, :]
    tprev = np.where(pp_ <= cc_, 0.0, NEG).astype(ml_dtypes.bfloat16)
    tcur = np.where(cc_ <= pp_, 0.0, NEG).astype(ml_dtypes.bfloat16)
    maskP0_z = np.zeros((128, 512), ml_dtypes.bfloat16)
    maskP0_n = np.full((128, 512), NEG, ml_dtypes.bfloat16)
    idbf = np.eye(128, dtype=ml_dtypes.bfloat16)
    idf32 = np.eye(128, dtype=f32)
    consts = np.zeros((128, 4), f32)
    consts[:, 0] = 1e-5

    tokid = np.ascontiguousarray(
        (np.arange(8)[None, :] * 128 + np.arange(128)[:, None]).astype(np.uint32))
    posbase = np.ascontiguousarray(np.broadcast_to(
        (np.arange(16, dtype=f32) * 384.0)[None, :], (128, 16)))
    zci = np.zeros((16 * 384, 1), np.uint32)
    z1k = np.zeros((1, 1024), ml_dtypes.bfloat16)
    shared = dict(w_inT=w_inT, w_outT=w_outT, W1H=W1H, W2H=W2H, bfcB=bfcB,
                  bprojB=bprojB, WrH=WrH, kjb=kjb, tprev=tprev, tcur=tcur,
                  idbf=idbf, idf32=idf32, consts=consts,
                  tokid=tokid, posbase=posbase, zci=zci, z1k=z1k)
    zeros_halo = np.zeros((512, 1024), f32)
    in_maps = []
    for c in range(8):
        m = dict(shared)
        m["x_own"] = np.ascontiguousarray(x[c * T:(c + 1) * T])
        m["x_halo"] = zeros_halo if c in (0, 4) else np.ascontiguousarray(x[c * T - 512:c * T])
        m["maskP0"] = maskP0_n if c in (0, 4) else maskP0_z
        in_maps.append(m)
    return in_maps


def kernel(**inputs):
    in_maps = _host_inputs(**inputs)
    nc = _get_nc()
    res = run_bass_kernel_spmd(nc, in_maps, core_ids=list(range(8)))
    y = np.concatenate([np.asarray(res.results[c]["y"]) for c in range(8)], axis=0)
    return y.reshape(2, 8192, 1024).astype(np.float32, copy=False)


# version 13
# speedup vs baseline: 1.0662x; 1.0662x over previous
import numpy as np
import ml_dtypes
from contextlib import ExitStack
import concourse.bass as bass
import concourse.mybir as mybir
from concourse.bass_utils import run_bass_kernel_spmd
from concourse.alu_op_type import AluOpType as OP

F32 = mybir.dt.float32
BF16 = mybir.dt.bfloat16
AF = mybir.ActivationFunctionType
AX = mybir.AxisListType.X

T = 2048
D = 1024
NEXP = 9
H = 16
NITER = 5
CAP = 384
ZROW = 16 * CAP
NEG = -1.0e30
SLOPES = [float(np.exp2(np.float32(-(h + 1) * 8.0 / 16.0))) for h in range(H)]


def qke_cum(j):
    if j < 0:
        return 0
    return 8 + 12 * j


def build_program(debug=False, debug2=False):
    nc = bass.Bass("TRN2", target_bir_lowering=False, debug=False,
                   detect_race_conditions=False)

    x_own = nc.dram_tensor("x_own", [T, D], F32, kind="ExternalInput").ap()
    x_halo = nc.dram_tensor("x_halo", [512, D], F32, kind="ExternalInput").ap()
    w_inT = nc.dram_tensor("w_inT", [D, 3 * D], F32, kind="ExternalInput").ap()
    w_outT = nc.dram_tensor("w_outT", [D, D], F32, kind="ExternalInput").ap()
    W1H = nc.dram_tensor("W1H", [NEXP, 8, 128, 4096], F32, kind="ExternalInput").ap()
    W2H = nc.dram_tensor("W2H", [NEXP, 8, 128, 4096], F32, kind="ExternalInput").ap()
    tokid_in = nc.dram_tensor("tokid", [128, 8], mybir.dt.uint32, kind="ExternalInput").ap()
    posbase_in = nc.dram_tensor("posbase", [128, 16], F32, kind="ExternalInput").ap()
    zci_in = nc.dram_tensor("zci", [16 * CAP, 1], mybir.dt.uint32, kind="ExternalInput").ap()
    z1k_in = nc.dram_tensor("z1k", [1, 1024], BF16, kind="ExternalInput").ap()
    bfcB = nc.dram_tensor("bfcB", [NEXP, 128, 32], F32, kind="ExternalInput").ap()
    bprojB = nc.dram_tensor("bprojB", [NEXP, 128, 8], F32, kind="ExternalInput").ap()
    WrH = nc.dram_tensor("WrH", [128, 64], F32, kind="ExternalInput").ap()
    kjb_in = nc.dram_tensor("kjb", [128, 1024], F32, kind="ExternalInput").ap()
    tprev_in = nc.dram_tensor("tprev", [128, 128], BF16, kind="ExternalInput").ap()
    tcur_in = nc.dram_tensor("tcur", [128, 128], BF16, kind="ExternalInput").ap()
    maskP0_in = nc.dram_tensor("maskP0", [128, 512], BF16, kind="ExternalInput").ap()
    idbf_in = nc.dram_tensor("idbf", [128, 128], BF16, kind="ExternalInput").ap()
    idf_in = nc.dram_tensor("idf32", [128, 128], F32, kind="ExternalInput").ap()
    consts_in = nc.dram_tensor("consts", [128, 4], F32, kind="ExternalInput").ap()

    y_out = nc.dram_tensor("y", [T, D], F32, kind="ExternalOutput").ap()
    skind = "ExternalOutput" if debug else "Internal"
    x1_hbm = nc.dram_tensor("x1_hbm", [T, D], F32, kind=skind).ap()
    xn2_hbm = nc.dram_tensor("xn2_hbm", [T, D], BF16, kind=skind).ap()
    g8_hbm = nc.dram_tensor("g8_hbm", [8, T], F32, kind=skind).ap()
    idxl_hbm = nc.dram_tensor("idxl_hbm", [16 * CAP, 1], mybir.dt.uint32, kind="Internal").ap()
    W1b = nc.dram_tensor("W1b", [NEXP, 8, 128, 4096], BF16, kind="Internal").ap()
    W2b = nc.dram_tensor("W2b", [NEXP, 8, 128, 4096], BF16, kind="Internal").ap()
    out_perm = nc.dram_tensor("out_perm", [16 * CAP + 1, 1024], BF16, kind="Internal").ap()

    SBC = nc.alloc_sbuf_tensor
    idbf = SBC("c_idbf", [128, 128], BF16)
    idf32 = SBC("c_idf32", [128, 128], F32)
    consts = SBC("c_consts", [128, 4], F32)
    wr_sb = SBC("c_wr", [128, 64], BF16)
    bfc_sb = SBC("c_bfc", [128, NEXP * 32], F32)
    bproj_sb = SBC("c_bproj", [128, NEXP * 8], F32)
    tokid_sb = SBC("c_tokid", [128, 8], mybir.dt.uint32)
    posbase_sb = SBC("c_posbase", [128, 16], F32)

    S = {}
    for nm in ["pl", "plg", "x", "sq1", "va1", "std1", "st1", "xn1", "tpp", "tpe",
               "qkm", "qke", "sm1", "sc", "t2", "p", "pn", "tp", "pte", "pv",
               "oe", "op", "x1", "sq2", "va2", "std2", "st2", "xn2",
               "stx1", "stxn2",
               "xn2t", "w1", "w2", "fcp", "gel", "prm", "pse",
               "rtm", "gv1", "gs", "gv", "gtt", "g8v", "g8st",
               "pscn", "ptv", "ptt", "ix", "sca", "ixl", "xga", "xgp", "xge", "pwc", "pww",
               "ott", "opv", "ops", "cg",
               "cx1", "ct", "cy", "y", "vv"]:
        S[nm] = nc.alloc_semaphore("s_" + nm)

    def wg(eng, nm, val):
        if val > 0:
            eng.wait_ge(S[nm], val)

    vvc = [0]

    def vmark(inst):
        inst.then_inc(S["vv"], 1)
        vvc[0] += 1

    def vwait(v):
        if vvc[0] > 0:
            v.wait_ge(S["vv"], vvc[0])

    stackA = ExitStack()

    def SBA(name, shape, dt):
        return stackA.enter_context(nc.sbuf_tensor(name, shape, dt))

    winT = SBA("a_winT", [128, 8 * 3072], BF16)
    woutT = SBA("a_woutT", [128, 8 * 1024], BF16)
    kjb = SBA("a_kjb", [128, 1024], F32)
    tprev = SBA("a_tprev", [128, 128], BF16)
    tcur = SBA("a_tcur", [128, 128], BF16)
    maskP0 = SBA("a_maskP0", [128, 512], BF16)
    x_sb = SBA("a_x", [128, 4096], F32)
    xn1 = SBA("a_xn1", [128, 4096], BF16)
    xn1T = SBA("a_xn1T", [128, 8 * 512], BF16)
    q_sb = SBA("a_q", [128, 8 * 512], BF16)
    k_ring = SBA("a_kring", [128, 8 * 1024], BF16)
    v_ring = SBA("a_vring", [128, 8 * 1024], BF16)
    t_sb = SBA("a_t", [128, 3 * 640], F32)
    p_sb = SBA("a_p", [128, 3 * 640], BF16)
    pT_sb = SBA("a_pT", [128, 2 * 2560], BF16)
    o_sb = SBA("a_o", [128, 8 * 512], BF16)
    x1t = SBA("a_x1t", [128, 2 * 1024], F32)
    xn2t = SBA("a_xn2t", [128, 2 * 1024], BF16)
    sq_scr = SBA("a_sqscr", [128, 1024], F32)
    pwb = [SBA("a_pwb%d" % i, [128, 4096], BF16) for i in range(2)]
    st = SBA("a_stats", [128, 48], F32)

    psA = ExitStack()
    ps_a = [psA.enter_context(nc.psum_tensor("ps_a%d" % i, [128, 1024], F32)) for i in range(2)]
    ps_pt = [psA.enter_context(nc.psum_tensor("ps_pt%d" % i, [128, 1024], BF16)) for i in range(2)]
    ps_o = [psA.enter_context(nc.psum_tensor("ps_o%d" % i, [128, 512], F32)) for i in range(2)]

    with nc.Block() as bp:
        @bp.sync
        def _(sy):
            for dst, src in [(idbf, idbf_in), (idf32, idf_in), (consts, consts_in),
                             (kjb, kjb_in), (tprev, tprev_in), (tcur, tcur_in),
                             (maskP0, maskP0_in)]:
                sy.dma_start(out=dst[:], in_=src).then_inc(S["pl"], 16)
            sy.dma_start(out=bfc_sb[:], in_=bfcB.rearrange("e p c -> p e c")).then_inc(S["pl"], 16)
            sy.dma_start(out=bproj_sb[:], in_=bprojB.rearrange("e p c -> p e c")).then_inc(S["pl"], 16)
            sy.dma_start(out=tokid_sb[:], in_=tokid_in).then_inc(S["pl"], 16)
            sy.dma_start(out=posbase_sb[:], in_=posbase_in).then_inc(S["pl"], 16)
            sy.wait_ge(S["pl"], 11 * 16)

        @bp.gpsimd
        def _(g):
            for ec in range(8):
                g.dma_start(out=winT[:, ec * 3072:(ec + 1) * 3072],
                            in_=w_inT[ec * 128:(ec + 1) * 128, :],
                            max_dma_last_dim=8192).then_inc(S["plg"], 16)
                g.dma_start(out=woutT[:, ec * 1024:(ec + 1) * 1024],
                            in_=w_outT[ec * 128:(ec + 1) * 128, :],
                            max_dma_last_dim=8192).then_inc(S["plg"], 16)
            g.dma_start(out=wr_sb[:], in_=WrH, max_dma_last_dim=8192).then_inc(S["plg"], 16)
            g.dma_start(out=idxl_hbm, in_=zci_in).then_inc(S["plg"], 16)
            g.dma_start(out=out_perm[ZROW:ZROW + 1, :], in_=z1k_in).then_inc(S["plg"], 16)
            g.wait_ge(S["plg"], 19 * 16)

    nc.const_aps.register(F32, 1e-5, consts[:, 0:1])
    PL = 11 * 16
    PLG = 19 * 16

    def ln_scalar_sq(sc, src, base, tc, sem_sq, cnt_sq, sem_va, cnt_va_war):
        wg(sc, sem_va, cnt_va_war)
        c = base + (tc % 2) * 8
        sc.activation(sq_scr[:], src, AF.Square,
                      accum_out=st[:, c + 1:c + 2]).then_inc(S[sem_sq], 1)

    def ln_vector_stats(v, src, base, tc, sem_sq, cnt_sq, sem_va, sem_std, cnt_std,
                        sem_st, sem_xn, cnt_xn_war):
        c = base + (tc % 2) * 8
        wg(v, sem_xn, cnt_xn_war)
        vmark(v.reduce_sum(st[:, c + 0:c + 1], src, axis=AX))
        wg(v, sem_sq, cnt_sq)
        vwait(v)
        vmark(v.tensor_scalar_mul(st[:, c + 2:c + 3], st[:, c + 0:c + 1], 1.0 / 1024.0))
        v.tensor_scalar_mul(st[:, c + 3:c + 4], st[:, c + 1:c + 2], 1.0 / 1024.0)
        vwait(v)
        vmark(v.tensor_scalar(st[:, c + 4:c + 5], st[:, c + 2:c + 3],
                              st[:, c + 2:c + 3], 1.0, op0=OP.mult, op1=OP.mult))
        vwait(v)
        v.tensor_tensor(st[:, c + 3:c + 4], st[:, c + 3:c + 4],
                        st[:, c + 4:c + 5], op=OP.subtract).then_inc(S[sem_va], 1)
        wg(v, sem_std, cnt_std)
        vmark(v.reciprocal(st[:, c + 5:c + 6], st[:, c + 4:c + 5]))
        vwait(v)
        v.tensor_scalar(st[:, c + 6:c + 7], st[:, c + 2:c + 3],
                        st[:, c + 5:c + 6], -1.0, op0=OP.mult, op1=OP.mult
                        ).then_inc(S[sem_st], 1)

    def ln_scalar_sqrt(sc, base, tc, sem_va, cnt_va, sem_std, sem_st, cnt_st_war):
        c = base + (tc % 2) * 8
        wg(sc, sem_va, cnt_va)
        wg(sc, sem_st, cnt_st_war)
        sc.activation(st[:, c + 4:c + 5], st[:, c + 3:c + 4], AF.Sqrt,
                      bias=1e-5).then_inc(S[sem_std], 1)

    def ln_scalar_apply(sc, dst, src, base, tc, sem_st, cnt_st, sem_xn):
        c = base + (tc % 2) * 8
        wg(sc, sem_st, cnt_st)
        sc.activation(dst, src, AF.Identity, bias=st[:, c + 6:c + 7],
                      scale=st[:, c + 5:c + 6]).then_inc(S[sem_xn], 1)

    with nc.Block() as ba:
        @ba.gpsimd
        def _(g):
            slices = [(s_t, d_t, e, hs) for s_t, d_t in ((W2H, W2b),)
                      for e in range(NEXP) for hs in range(8)]
            for i in range(len(slices) + 1):
                if i < len(slices):
                    s_t, d_t, e, hs = slices[i]
                    wg(g, "pww", 16 * max(i - 1, 0))
                    g.dma_start(out=pwb[i % 2][:], in_=s_t[e, hs],
                                max_dma_last_dim=8192).then_inc(S["pwc"], 16)
                if i >= 1:
                    s_t, d_t, e, hs = slices[i - 1]
                    g.wait_ge(S["pwc"], 16 * i)
                    g.dma_start(out=d_t[e, hs], in_=pwb[(i - 1) % 2][:]
                                ).then_inc(S["pww"], 16)
            g.wait_ge(S["pww"], 16 * 72)

        @ba.sync
        def _(sy):
            for j in range(NITER):
                wg(sy, "xn1", 4 * j)
                wg(sy, "st1", 4 * j)
                wg(sy, "sq1", 4 * j)
                wg(sy, "x1", 4 * (j - 1))
                src = x_halo if j == 0 else x_own[(j - 1) * 512:j * 512, :]
                sy.dma_start(out=x_sb[:],
                             in_=src.rearrange("(tc p) f -> p tc f", p=128)
                             ).then_inc(S["x"], 16)
                if j == 0:
                    continue
                b = j - 1
                for tc in range(4):
                    wg(sy, "x1", 4 * b + tc + 1)
                    r0 = b * 512 + tc * 128
                    sy.dma_start(out=x1_hbm[r0:r0 + 128, :],
                                 in_=x1t[:, (tc % 2) * 1024:(tc % 2) * 1024 + 1024]
                                 ).then_inc(S["stx1"], 16)
                    wg(sy, "xn2", 4 * b + tc + 1)
                    sy.dma_start(out=xn2_hbm[r0:r0 + 128, :],
                                 in_=xn2t[:, (tc % 2) * 1024:(tc % 2) * 1024 + 1024]
                                 ).then_inc(S["stxn2"], 16)
            sy.wait_ge(S["stx1"], 16 * 16)
            sy.wait_ge(S["stxn2"], 16 * 16)

        @ba.tensor
        def _(te):
            te.wait_ge(S["pl"], PL)
            te.wait_ge(S["plg"], PLG)
            for j in range(NITER):
                b = j - 1
                slot = j % 2
                wg(te, "xn1", 4 * (j + 1))
                for ec in range(8):
                    wg(te, "tpe", 8 * j + max(ec - 1, 0))
                    if ec <= 1:
                        wg(te, "pte", 64 * max(b, 0))
                    for tc in range(4):
                        mm = te.transpose(ps_pt[ec % 2][:, tc * 128:tc * 128 + 128],
                                          xn1[:, tc * 1024 + ec * 128:tc * 1024 + ec * 128 + 128],
                                          idbf[:])
                    mm.then_inc(S["tpp"], 1)
                wg(te, "tpe", 8 * (j + 1))
                wg(te, "x1", 4 * (j - 1))
                nt = 0
                if j >= 1:
                    for fcp in range(4):
                        wg(te, "qke", qke_cum(j - 1) + max(nt - 1, 0))
                        pa = ps_a[nt % 2]
                        for half in range(2):
                            fc = fcp * 2 + half
                            for ec in range(8):
                                mm = te.matmul(pa[:, half * 512:half * 512 + 512],
                                               winT[:, ec * 3072 + fc * 128:ec * 3072 + fc * 128 + 128],
                                               xn1T[:, ec * 512:ec * 512 + 512],
                                               start=(ec == 0), stop=(ec == 7))
                        mm.then_inc(S["qkm"], 1)
                        nt += 1
                for fcp in range(4):
                    wg(te, "qke", qke_cum(j - 1) + max(nt - 1, 0))
                    pa = ps_a[nt % 2]
                    for half in range(2):
                        fc = fcp * 2 + half
                        for ec in range(8):
                            mm = te.matmul(pa[:, half * 512:half * 512 + 512],
                                           winT[:, ec * 3072 + 1024 + fc * 128:ec * 3072 + 1024 + fc * 128 + 128],
                                           xn1T[:, ec * 512:ec * 512 + 512],
                                           start=(ec == 0), stop=(ec == 7))
                    mm.then_inc(S["qkm"], 1)
                    nt += 1
                for tc in range(4):
                    wg(te, "qke", qke_cum(j - 1) + max(nt - 1, 0))
                    pa = ps_a[nt % 2]
                    for n in range(2):
                        for ec in range(8):
                            mm = te.matmul(pa[:, n * 512:n * 512 + 512],
                                           xn1T[:, ec * 512 + tc * 128:ec * 512 + tc * 128 + 128],
                                           winT[:, ec * 3072 + 2048 + n * 512:ec * 3072 + 2048 + n * 512 + 512],
                                           start=(ec == 0), stop=(ec == 7))
                    mm.then_inc(S["qkm"], 1)
                    nt += 1

                if j == 0:
                    continue
                wg(te, "qke", qke_cum(j))
                prev, cur = (j - 1) % 2, j % 2

                def emit_scores(s):
                    h, qc = s // 4, s % 4
                    hp, hh = h // 2, (h % 2) * 64
                    n1 = 512 - qc * 128
                    wg(te, "sm1", 64 * b + max(s - 1, 0))
                    pa = ps_a[s % 2]
                    q_ap = q_sb[hh:hh + 64, hp * 512 + qc * 128:hp * 512 + qc * 128 + 128]
                    te.matmul(pa[:, 0:n1], q_ap,
                              k_ring[hh:hh + 64, hp * 1024 + prev * 512 + qc * 128:hp * 1024 + prev * 512 + 512],
                              start=True, stop=True)
                    if qc > 0:
                        te.matmul(pa[:, n1:512], q_ap,
                                  k_ring[hh:hh + 64, hp * 1024 + cur * 512:hp * 1024 + cur * 512 + qc * 128],
                                  start=True, stop=True)
                    te.matmul(pa[:, 512:640], q_ap,
                              k_ring[hh:hh + 64, hp * 1024 + cur * 512 + qc * 128:hp * 1024 + cur * 512 + qc * 128 + 128],
                              start=True, stop=True).then_inc(S["sc"], 1)

                def emit_transposes(sg):
                    wg(te, "pn", 64 * b + sg + 1)
                    if sg >= 2:
                        wg(te, "pte", 64 * b + sg - 1)
                    else:
                        wg(te, "tpe", 8 * (j + 1))
                    for kc in range(5):
                        mm = te.transpose(ps_pt[sg % 2][:, kc * 128:kc * 128 + 128],
                                          p_sb[:, (sg % 3) * 640 + kc * 128:(sg % 3) * 640 + kc * 128 + 128],
                                          idbf[:])
                    mm.then_inc(S["tp"], 1)

                def emit_pv(h):
                    pair, hh = h // 2, (h % 2) * 64
                    wg(te, "pte", 64 * b + 4 * h + 4)
                    if h % 2 == 0 and pair >= 2:
                        wg(te, "oe", 8 * b + pair - 1)
                    po = ps_o[pair % 2]
                    for qc in range(4):
                        for kc in range(5):
                            wc = qc + kc
                            sl, tc2 = (prev, wc) if wc < 4 else (cur, wc - 4)
                            mm = te.matmul(po[hh:hh + 64, qc * 128:qc * 128 + 128],
                                           v_ring[:, (sl * 4 + tc2) * 1024 + h * 64:(sl * 4 + tc2) * 1024 + h * 64 + 64],
                                           pT_sb[:, (h % 2) * 2560 + qc * 640 + kc * 128:(h % 2) * 2560 + qc * 640 + kc * 128 + 128],
                                           start=(kc == 0), stop=(kc == 4))
                    mm.then_inc(S["pv"], 1)

                for s in range(68):
                    if s < 64:
                        emit_scores(s)
                    if 2 <= s < 66:
                        emit_transposes(s - 2)
                    if s >= 4 and (s - 4) % 4 == 3:
                        emit_pv((s - 4) // 4)
                wg(te, "oe", 8 * (b + 1))
                wg(te, "sm1", 64 * (b + 1))
                for tc in range(4):
                    wg(te, "x1", 4 * b + max(tc - 1, 0))
                    pa = ps_a[tc % 2]
                    for n in range(2):
                        for ec in range(8):
                            mm = te.matmul(pa[:, n * 512:n * 512 + 512],
                                           o_sb[:, ec * 512 + tc * 128:ec * 512 + tc * 128 + 128],
                                           woutT[:, ec * 1024 + n * 512:ec * 1024 + n * 512 + 512],
                                           start=(ec == 0), stop=(ec == 7))
                    mm.then_inc(S["op"], 1)

        @ba.scalar
        def _(sc):
            sc.wait_ge(S["pl"], PL)
            sc.wait_ge(S["plg"], PLG)
            for j in range(NITER):
                b = j - 1
                slot = j % 2
                wg(sc, "x", 16 * (j + 1))
                xoff = 0
                for tc in range(4):
                    src = x_sb[:, tc * 1024:tc * 1024 + 1024]
                    ln_scalar_sq(sc, src, 0, tc, "sq1", 0, "va1", 4 * j + tc - 1)
                    ln_scalar_sqrt(sc, 0, tc, "va1", 4 * j + tc + 1, "std1", "st1",
                                   4 * j + tc - 1)
                    ln_scalar_apply(sc, xn1[:, tc * 1024:tc * 1024 + 1024], src,
                                    0, tc, "st1", 4 * j + tc + 1, "xn1")
                for ec in range(8):
                    wg(sc, "tpp", 8 * j + ec + 1)
                    sc.copy(xn1T[:, ec * 512:ec * 512 + 512],
                            ps_pt[ec % 2][:, 0:512]).then_inc(S["tpe"], 1)
                wg(sc, "pv", 16 * (j - 1))
                nt = 0
                if j >= 1:
                    for fcp in range(4):
                        wg(sc, "qkm", qke_cum(j - 1) + nt + 1)
                        sc.activation(q_sb[:, fcp * 1024:fcp * 1024 + 1024],
                                      ps_a[nt % 2][:, 0:1024], AF.Identity,
                                      scale=0.125).then_inc(S["qke"], 1)
                        nt += 1
                for fcp in range(4):
                    wg(sc, "qkm", qke_cum(j - 1) + nt + 1)
                    for half in range(2):
                        fc = fcp * 2 + half
                        ins = sc.copy(k_ring[:, fc * 1024 + slot * 512:fc * 1024 + slot * 512 + 512],
                                      ps_a[nt % 2][:, half * 512:half * 512 + 512])
                    ins.then_inc(S["qke"], 1)
                    nt += 1
                for tc in range(4):
                    wg(sc, "qkm", qke_cum(j - 1) + nt + 1)
                    sc.copy(v_ring[:, (slot * 4 + tc) * 1024:(slot * 4 + tc) * 1024 + 1024],
                            ps_a[nt % 2][:, 0:1024]).then_inc(S["qke"], 1)
                    nt += 1

                if j == 0:
                    continue

                def emit_pte(sg):
                    h2 = sg // 4
                    wg(sc, "tp", 64 * b + sg + 1)
                    if sg % 4 == 0:
                        wg(sc, "pv", 16 * b + max(h2 - 1, 0))
                    sc.copy(pT_sb[:, (h2 % 2) * 2560 + (sg % 4) * 640:(h2 % 2) * 2560 + (sg % 4) * 640 + 640],
                            ps_pt[sg % 2][:, 0:640]).then_inc(S["pte"], 1)
                    if sg % 8 == 7:
                        pair2 = sg // 8
                        wg(sc, "pv", 16 * b + 2 * pair2 + 2)
                        sc.copy(o_sb[:, pair2 * 512:pair2 * 512 + 512],
                                ps_o[pair2 % 2][:, 0:512]).then_inc(S["oe"], 1)

                for s in range(66):
                    if s < 64:
                        wg(sc, "t2", 64 * b + s + 1)
                        wg(sc, "tp", 64 * b + s - 2)
                        wg(sc, "pn", 64 * b + s - 2)
                        nmc, smc = 16 + (s % 3), 19 + (s % 3)
                        sc.activation(p_sb[:, (s % 3) * 640:(s % 3) * 640 + 640],
                                      t_sb[:, (s % 3) * 640:(s % 3) * 640 + 640],
                                      AF.Exp, bias=st[:, nmc:nmc + 1],
                                      accum_out=st[:, smc:smc + 1]).then_inc(S["p"], 1)
                    if s >= 2:
                        emit_pte(s - 2)
                for tc in range(4):
                    wg(sc, "x1", 4 * b + tc + 1)
                    src = x1t[:, (tc % 2) * 1024:(tc % 2) * 1024 + 1024]
                    ln_scalar_sq(sc, src, 28, tc, "sq2", 0, "va2", 4 * b + tc - 1)
                    ln_scalar_sqrt(sc, 28, tc, "va2", 4 * b + tc + 1, "std2", "st2",
                                   4 * b + tc - 1)
                    wg(sc, "stxn2", 16 * (4 * b + tc - 1))
                    ln_scalar_apply(sc, xn2t[:, (tc % 2) * 1024:(tc % 2) * 1024 + 1024],
                                    src, 28, tc, "st2", 4 * b + tc + 1, "xn2")

        @ba.vector
        def _(v):
            v.wait_ge(S["pl"], PL)
            v.wait_ge(S["plg"], PLG)
            for j in range(NITER):
                b = j - 1
                wg(v, "x", 16 * (j + 1))
                for tc in range(4):
                    ln_vector_stats(v, x_sb[:, tc * 1024:tc * 1024 + 1024], 0, tc,
                                    "sq1", 4 * j + tc + 1, "va1", "std1",
                                    4 * j + tc + 1, "st1", "xn1", 4 * j + tc - 1)
                if j == 0:
                    continue

                def emit_pn(sp):
                    wg(v, "p", 64 * b + sp + 1)
                    smc, rc = 19 + (sp % 3), 22 + (sp % 3)
                    vmark(v.reciprocal(st[:, rc:rc + 1], st[:, smc:smc + 1]))
                    vwait(v)
                    pp = p_sb[:, (sp % 3) * 640:(sp % 3) * 640 + 640]
                    v.tensor_scalar(pp, pp, st[:, rc:rc + 1], 1.0,
                                    op0=OP.mult, op1=OP.mult).then_inc(S["pn"], 1)

                for s in range(64):
                    h, qc = s // 4, s % 4
                    n1 = 512 - qc * 128
                    wg(v, "sc", 64 * b + s + 1)
                    wg(v, "p", 64 * b + s - 2)
                    t0 = (s % 3) * 640
                    tt = t_sb[:, t0:t0 + 640]
                    v.scalar_tensor_tensor(tt, kjb[:, qc * 128:qc * 128 + 640],
                                           -SLOPES[h], ps_a[s % 2][:, 0:640],
                                           op0=OP.mult, op1=OP.add).then_inc(S["sm1"], 1)
                    v.tensor_tensor(t_sb[:, t0:t0 + 128], t_sb[:, t0:t0 + 128],
                                    tprev[:], op=OP.add)
                    v.tensor_tensor(t_sb[:, t0 + 512:t0 + 640], t_sb[:, t0 + 512:t0 + 640],
                                    tcur[:], op=OP.add)
                    if b == 0:
                        vmark(v.tensor_tensor(t_sb[:, t0:t0 + n1], t_sb[:, t0:t0 + n1],
                                              maskP0[:, qc * 128:512], op=OP.add))
                        vwait(v)
                    nmc = 16 + (s % 3)
                    v.reduce_max(st[:, nmc:nmc + 1], tt, axis=AX,
                                 negate=True).then_inc(S["t2"], 1)
                    if s >= 1:
                        emit_pn(s - 1)
                if True:
                    emit_pn(63)
                for tc in range(4):
                    wg(v, "op", 4 * b + tc + 1)
                    wg(v, "xn2", 4 * b + tc - 1)
                    wg(v, "stx1", 16 * (4 * b + tc - 1))
                    v.tensor_tensor(x1t[:, (tc % 2) * 1024:(tc % 2) * 1024 + 1024],
                                    x_sb[:, tc * 1024:tc * 1024 + 1024],
                                    ps_a[tc % 2][:, 0:1024], op=OP.add).then_inc(S["x1"], 1)
                    ln_vector_stats(v, x1t[:, (tc % 2) * 1024:(tc % 2) * 1024 + 1024],
                                    28, tc, "sq2", 4 * b + tc + 1, "va2", "std2",
                                    4 * b + tc + 1, "st2", "xn2", 4 * b + tc - 1)

    if debug2:
        dbg_names = [("d_x", x_sb, F32, 4096), ("d_xn1", xn1, BF16, 4096),
                     ("d_xn1T", xn1T, BF16, 4096), ("d_q", q_sb, BF16, 4096),
                     ("d_k", k_ring, BF16, 8192), ("d_v", v_ring, BF16, 8192),
                     ("d_t", t_sb, F32, 2048), ("d_p", p_sb, BF16, 2048),
                     ("d_pT", pT_sb, BF16, 4096), ("d_o", o_sb, BF16, 4096),
                     ("d_st", st, F32, 48)]
        dbg_outs = {nm: nc.dram_tensor(nm, [128, w], dt, kind="ExternalOutput").ap()
                    for nm, _, dt, w in dbg_names}
        sdbg = nc.alloc_semaphore("s_dbg")
        with nc.Block() as bdg:
            @bdg.sync
            def _(sy):
                for nm, tile, dt, w in dbg_names:
                    sy.dma_start(out=dbg_outs[nm], in_=tile[:]).then_inc(sdbg, 16)
                sy.wait_ge(sdbg, 16 * len(dbg_names))

    stackA.close()
    psA.close()

    stackB = ExitStack()

    def SBB(name, shape, dt):
        return stackB.enter_context(nc.sbuf_tensor(name, shape, dt))

    stackB2 = ExitStack()
    acc = stackB2.enter_context(nc.sbuf_tensor("b_acc", [128, 8 * 2048], BF16))
    pos1u = stackB2.enter_context(nc.sbuf_tensor("b_pos1u", [128, 16], mybir.dt.uint32))
    pos2u = stackB2.enter_context(nc.sbuf_tensor("b_pos2u", [128, 16], mybir.dt.uint32))
    gcol1 = stackB2.enter_context(nc.sbuf_tensor("b_gcol1", [128, 16], F32))
    gcol2 = stackB2.enter_context(nc.sbuf_tensor("b_gcol2", [128, 16], F32))

    xn2T = SBB("b_xn2T", [128, 8 * 1024], BF16)
    h_sb = SBB("b_h", [128, 32 * 1024], BF16)
    W1_sb = [SBB("b_w1_%d" % i, [128, 4096], BF16) for i in range(2)]
    W2_sb = [SBB("b_w2_%d" % i, [128, 4096], BF16) for i in range(2)]
    lg_sb = SBB("b_lg", [128, 8], F32)
    lgm_sb = SBB("b_lgm", [128, 8], F32)
    eq1_sb = SBB("b_eq1", [128, 64], F32)
    eq2_sb = SBB("b_eq2", [128, 64], F32)
    d12_sb = SBB("b_d12", [128, 8], F32)
    g1_sb = SBB("b_g1", [128, 8], F32)
    g2_sb = SBB("b_g2", [128, 8], F32)
    gt_sb = SBB("b_gt", [128, 8], F32)
    gatef_sb = SBB("b_gatef", [128, 64], F32)
    gst = SBB("b_gst", [128, 4], F32)
    g8_sb = SBB("b_g8", [8, 1024], F32)
    me8 = SBB("b_me8", [8, 1024], F32)
    zer8 = SBB("b_zer8", [8, 1024], F32)
    posi8 = SBB("b_posi8", [8, 1024], F32)
    posT_sb = SBB("b_posT", [128, 64], F32)
    pv1 = SBB("b_pv1", [128, 8], F32)
    pv2 = SBB("b_pv2", [128, 8], F32)
    pv3 = SBB("b_pv3", [128, 8], F32)
    pc1 = SBB("b_pc1", [128, 1], F32)
    idxu_sb = SBB("b_idxu", [128, 64], mybir.dt.uint32)
    idxg_sb = [SBB("b_idxg%d" % i, [128, 3], mybir.dt.uint32) for i in range(2)]
    xg_nat = [SBB("b_xg0", [128, 3 * 1024], BF16)]
    xgT = SBB("b_xgT", [128, 8 * CAP], BF16)
    op_fm = SBB("b_opfm", [128, 8 * CAP], BF16)
    oprow = [SBB("b_oprow%d" % i, [128, 1024], BF16) for i in range(2)]

    psB = ExitStack()
    ps_fc = [psB.enter_context(nc.psum_tensor("ps_fc%d" % i, [128, 512], F32)) for i in range(2)]
    ps_out = [psB.enter_context(nc.psum_tensor("ps_out%d" % i, [128, 1024], F32)) for i in range(2)]
    ps_tb = psB.enter_context(nc.psum_tensor("ps_tb", [128, 1024], BF16))
    ps_tb2 = psB.enter_context(nc.psum_tensor("ps_tb2", [128, 1024], BF16))

    def fcp_slice_end(i1):
        if i1 < 0:
            return 0
        tth, r = divmod(i1, 72)
        e, hs = divmod(r, 8)
        base = 320 * tth
        if e == 0:
            return base + 8 * (hs + 1)
        return base + 64 + 4 * ((e - 1) * 8 + hs + 1)

    def fcp_e_end(tth, e):
        return 320 * tth + (64 if e >= 0 else 0) + 4 * 8 * max(e, 0)

    def gel_e_end(tth, e):
        return fcp_e_end(tth, e)

    with nc.Block() as bb:
        @bb.sync
        def _(sy):
            for tth in range(2):
                wg(sy, "fcp", 320 * tth)
                for ec in range(8):
                    sy.dma_start(out=xn2T[:, ec * 1024:ec * 1024 + 1024],
                                 in_=xn2_hbm[tth * 1024:tth * 1024 + 1024, ec * 128:ec * 128 + 128],
                                 transpose=True).then_inc(S["xn2t"], 16)
                wg(sy, "g8v", tth + 1)
                sy.dma_start(out=g8_hbm[:, tth * 1024:tth * 1024 + 1024],
                             in_=g8_sb[0:8, :]).then_inc(S["g8st"], 16)
                for e in range(1, NEXP):
                    slot = tth * 8 + (e - 1)
                    for gi in range(3):
                        og = slot * 3 + gi
                        wg(sy, "opv", og + 1)
                        sy.dma_start(out=out_perm[slot * CAP + gi * 128:slot * CAP + gi * 128 + 128, :],
                                     in_=oprow[og % 2][:]).then_inc(S["ops"], 16)

        @bb.gpsimd
        def _(g):
            r_cap = g.alloc_register()
            g.reg_mov(r_cap, CAP - 1)
            r_tok = g.alloc_register()
            g.reg_mov(r_tok, 1023)
            for tth in range(2):
                for e in range(NEXP):
                    if e >= 1:
                        ei = e - 1
                        slot = tth * 8 + ei
                        for tc in range(8):
                            wg(g, "ix", 8 * tth + tc + 1)
                            g.indirect_dma_start(
                                out=idxl_hbm,
                                out_offset=bass.IndirectOffsetOnAxis(
                                    idxu_sb[:, tc * 8 + ei:tc * 8 + ei + 1], axis=0),
                                in_=tokid_sb[:, tc:tc + 1], in_offset=None,
                                element_offset=slot * CAP,
                                bounds_check=r_cap, oob_is_err=False
                                ).then_inc(S["sca"], 16)
                        wg(g, "sca", 16 * (64 * tth + ei * 8 + 8))
                        wg(g, "xga", 16 * max(24 * tth + 3 * (e - 2), 0))
                        g.dma_start(out=idxg_sb[ei % 2][:],
                                    in_=idxl_hbm[slot * CAP:slot * CAP + CAP, :]
                                    .rearrange("(gi p) w -> p (gi w)", p=128),
                                    allow_slow_non_contiguous=True
                                    ).then_inc(S["ixl"], 16)
                        wg(g, "ixl", 16 * (8 * tth + e))
                        wg(g, "xgp", 8 * (8 * tth + ei))
                        for gi in range(3):
                            g.indirect_dma_start(
                                out=xg_nat[0][:, gi * 1024:gi * 1024 + 1024],
                                out_offset=None,
                                in_=xn2_hbm,
                                in_offset=bass.IndirectOffsetOnAxis(
                                    idxg_sb[ei % 2][:, gi:gi + 1], axis=0),
                                element_offset=tth * 1024 * 1024,
                                bounds_check=r_tok, oob_is_err=False
                                ).then_inc(S["xga"], 16)
                    for hs in range(8):
                        i1 = (tth * 9 + e) * 8 + hs
                        wg(g, "fcp", fcp_slice_end(i1 - 2))
                        g.dma_start(out=W1_sb[i1 % 2][:], in_=W1H[e, hs],
                                    max_dma_last_dim=8192).then_inc(S["w1"], 16)
                    for m in range(8):
                        i2 = (tth * 9 + e) * 8 + m
                        wg(g, "pse", i2 - 1)
                        g.dma_start(out=W2_sb[i2 % 2][:], in_=W2b[e, m]
                                    ).then_inc(S["w2"], 16)

        @bb.tensor
        def _(te):
            gfcp = [0]

            def fc_tile(i1pct, pslice, lhs_ap, rhs_ap):
                it = gfcp[0]
                wg(te, "gel", max(it - 1, 0))
                for ec in range(8):
                    mm = te.matmul(pslice, lhs_ap(ec), rhs_ap(ec),
                                   start=(ec == 0), stop=(ec == 7))
                mm.then_inc(S["fcp"], 1)
                gfcp[0] += 1
                return it

            for tth in range(2):
                wg(te, "xn2t", 16 * 8 * (tth + 1))
                wg(te, "gel", 320 * tth)
                wg(te, "pse", 72 * tth)
                for tc in range(8):
                    for ec in range(8):
                        mm = te.matmul(ps_fc[1][:, tc * 8:tc * 8 + 8],
                                       xn2T[:, ec * 1024 + tc * 128:ec * 1024 + tc * 128 + 128],
                                       wr_sb[:, ec * 8:ec * 8 + 8],
                                       start=(ec == 0), stop=(ec == 7))
                    mm.then_inc(S["rtm"], 1)
                for tc in range(8):
                    wg(te, "gv", 8 * tth + tc + 1)
                    te.transpose(ps_out[0][0:8, tc * 128:tc * 128 + 128],
                                 gatef_sb[:, tc * 8:tc * 8 + 8],
                                 idf32[:]).then_inc(S["gtt"], 1)
                wg(te, "g8v", tth + 1)
                wg(te, "pscn", tth + 1)
                for tc in range(8):
                    mm = te.transpose(ps_out[0][:, tc * 8:tc * 8 + 8],
                                      posi8[0:8, tc * 128:tc * 128 + 128],
                                      idf32[0:8, 0:8])
                mm.then_inc(S["ptt"], 1)
                wg(te, "gv1", 8 * (tth + 1))
                for e in range(NEXP):
                    ei = e - 1
                    if e >= 1:
                        wg(te, "xga", 16 * (24 * tth + 3 * e))
                        for ec in range(8):
                            xgpc = 64 * tth + 8 * ei + ec
                            wg(te, "xge", xgpc)
                            half = (ec % 2) * 512
                            for gi in range(3):
                                mm = te.transpose(ps_tb[:, half + gi * 128:half + gi * 128 + 128],
                                                  xg_nat[0][:, gi * 1024 + ec * 128:gi * 1024 + ec * 128 + 128],
                                                  idbf[:])
                            mm.then_inc(S["xgp"], 1)
                    for hs in range(8):
                        i1 = (tth * 9 + e) * 8 + hs
                        wg(te, "w1", 16 * (i1 + 1))
                        if e == 0 and hs == 0:
                            wg(te, "ptv", tth + 1)
                        if e >= 1 and hs == 0:
                            wg(te, "xge", 64 * tth + 8 * ei + 8)
                        for mf in range(4):
                            if e == 0:
                                for half in range(2):
                                    it = gfcp[0]
                                    pf = ps_fc[it % 2]
                                    wg(te, "gel", max(it - 1, 0))
                                    for ec in range(8):
                                        mm = te.matmul(pf[:, 0:512],
                                                       W1_sb[i1 % 2][:, ec * 512 + mf * 128:ec * 512 + mf * 128 + 128],
                                                       xn2T[:, ec * 1024 + half * 512:ec * 1024 + half * 512 + 512],
                                                       start=(ec == 0), stop=(ec == 7))
                                    mm.then_inc(S["fcp"], 1)
                                    gfcp[0] += 1
                            else:
                                it = gfcp[0]
                                pf = ps_fc[it % 2]
                                wg(te, "gel", max(it - 1, 0))
                                for ec in range(8):
                                    mm = te.matmul(pf[:, 0:CAP],
                                                   W1_sb[i1 % 2][:, ec * 512 + mf * 128:ec * 512 + mf * 128 + 128],
                                                   xgT[:, ec * CAP:ec * CAP + CAP],
                                                   start=(ec == 0), stop=(ec == 7))
                                mm.then_inc(S["fcp"], 1)
                                gfcp[0] += 1
                    wg(te, "gel", gel_e_end(tth, e))
                    for m in range(8):
                        i2 = (tth * 9 + e) * 8 + m
                        wg(te, "w2", 16 * (i2 + 1))
                        wg(te, "pse", max(i2 - 1, 0))
                        po = ps_out[i2 % 2]
                        if e == 0:
                            for hb in range(32):
                                for n in range(2):
                                    mm = te.matmul(po[:, n * 512:n * 512 + 512],
                                                   W2_sb[i2 % 2][:, hb * 128:hb * 128 + 128],
                                                   h_sb[:, hb * 1024 + n * 512:hb * 1024 + n * 512 + 512],
                                                   start=(hb == 0), stop=(hb == 31))
                        else:
                            for hb in range(32):
                                mm = te.matmul(po[:, 0:CAP],
                                               W2_sb[i2 % 2][:, hb * 128:hb * 128 + 128],
                                               h_sb[:, hb * 1024:hb * 1024 + CAP],
                                               start=(hb == 0), stop=(hb == 31))
                        mm.then_inc(S["prm"], 1)
                    if e >= 1:
                        for gi in range(3):
                            og = (tth * 8 + ei) * 3 + gi
                            wg(te, "pse", 72 * tth + 8 * (e + 1))
                            wg(te, "opv", og)
                            for m in range(8):
                                mm = te.transpose(ps_tb2[:, m * 128:m * 128 + 128],
                                                  op_fm[:, m * CAP + gi * 128:m * CAP + gi * 128 + 128],
                                                  idbf[:])
                            mm.then_inc(S["ott"], 1)

        @bb.scalar
        def _(sc):
            gfcp = [0]
            for tth in range(2):
                wg(sc, "gv1", 8 * (tth + 1))
                sc.activation(g1_sb[:], d12_sb[:], AF.Sigmoid)
                sc.activation(g2_sb[:], d12_sb[:], AF.Sigmoid,
                              scale=-1.0).then_inc(S["gs"], 1)
                for e in range(NEXP):
                    ei = e - 1
                    if e >= 1:
                        for ec in range(8):
                            xgpc = 64 * tth + 8 * ei + ec
                            wg(sc, "xgp", xgpc + 1)
                            if ec == 0:
                                wg(sc, "fcp", fcp_e_end(tth, e - 1))
                            half = (ec % 2) * 512
                            sc.copy(xgT[:, ec * CAP:ec * CAP + CAP],
                                    ps_tb[:, half:half + CAP]).then_inc(S["xge"], 1)
                    wg(sc, "prm", 8 * (tth * 9 + e))
                    for hs in range(8):
                        for mf in range(4):
                            hb = hs * 4 + mf
                            bcol = bfc_sb[:, e * 32 + hb:e * 32 + hb + 1]
                            if e == 0:
                                for half in range(2):
                                    it = gfcp[0]
                                    wg(sc, "fcp", it + 1)
                                    sc.activation(h_sb[:, hb * 1024 + half * 512:hb * 1024 + half * 512 + 512],
                                                  ps_fc[it % 2][:, 0:512], AF.Gelu_apprx_tanh,
                                                  bias=bcol).then_inc(S["gel"], 1)
                                    gfcp[0] += 1
                            else:
                                it = gfcp[0]
                                wg(sc, "fcp", it + 1)
                                sc.activation(h_sb[:, hb * 1024:hb * 1024 + CAP],
                                              ps_fc[it % 2][:, 0:CAP], AF.Gelu_apprx_tanh,
                                              bias=bcol).then_inc(S["gel"], 1)
                                gfcp[0] += 1
                    if e >= 1:
                        for m in range(8):
                            i2 = (tth * 9 + e) * 8 + m
                            wg(sc, "prm", i2 + 1)
                            if m == 0:
                                wg(sc, "ott", 3 * max(tth * 8 + ei, 0))
                            sc.activation(op_fm[:, m * CAP:m * CAP + CAP],
                                          ps_out[i2 % 2][:, 0:CAP], AF.Identity,
                                          bias=bproj_sb[:, e * 8 + m:e * 8 + m + 1]
                                          ).then_inc(S["pse"], 1)
                        for gi in range(3):
                            og = (tth * 8 + ei) * 3 + gi
                            wg(sc, "ott", og + 1)
                            wg(sc, "ops", 16 * max(og - 1, 0))
                            sc.copy(oprow[og % 2][:], ps_tb2[:]).then_inc(S["opv"], 1)

        @bb.vector
        def _(v):
            v.memset(zer8[:], 0.0)
            for tth in range(2):
                wg(v, "rtm", 8 * (tth + 1))
                for tc in range(8):
                    vmark(v.tensor_copy(lg_sb[:], ps_fc[1][:, tc * 8:tc * 8 + 8]))
                    vwait(v)
                    vmark(v.reduce_max(gst[:, 0:1], lg_sb[:], axis=AX))
                    vwait(v)
                    vmark(v.tensor_scalar(eq1_sb[:, tc * 8:tc * 8 + 8], lg_sb[:],
                                          gst[:, 0:1], 1.0, op0=OP.is_ge, op1=OP.mult))
                    vwait(v)
                    vmark(v.scalar_tensor_tensor(lgm_sb[:], eq1_sb[:, tc * 8:tc * 8 + 8],
                                                 -1.0e9, lg_sb[:], op0=OP.mult, op1=OP.add))
                    vwait(v)
                    vmark(v.reduce_max(gst[:, 1:2], lgm_sb[:], axis=AX))
                    vwait(v)
                    vmark(v.tensor_scalar(eq2_sb[:, tc * 8:tc * 8 + 8], lgm_sb[:],
                                          gst[:, 1:2], 1.0, op0=OP.is_ge, op1=OP.mult))
                    vwait(v)
                    v.tensor_tensor(d12_sb[:, tc:tc + 1], gst[:, 0:1], gst[:, 1:2],
                                    op=OP.subtract).then_inc(S["gv1"], 1)
                wg(v, "gs", tth + 1)
                wg(v, "gtt", 8 * tth)
                for tc in range(8):
                    vmark(v.tensor_scalar(gt_sb[:], eq1_sb[:, tc * 8:tc * 8 + 8],
                                          g1_sb[:, tc:tc + 1], 1.0, op0=OP.mult, op1=OP.mult))
                    vwait(v)
                    v.scalar_tensor_tensor(gatef_sb[:, tc * 8:tc * 8 + 8],
                                           eq2_sb[:, tc * 8:tc * 8 + 8],
                                           g2_sb[:, tc:tc + 1], gt_sb[:],
                                           op0=OP.mult, op1=OP.add).then_inc(S["gv"], 1)
                wg(v, "gtt", 8 * (tth + 1))
                wg(v, "g8st", 16 * tth)
                v.tensor_copy(g8_sb[0:8, :], ps_out[0][0:8, 0:1024]).then_inc(S["g8v"], 1)
                v.wait_ge(S["g8v"], tth + 1)
                vmark(v.tensor_scalar(me8[:], g8_sb[0:8, :], 0.0, 1.0,
                                      op0=OP.is_gt, op1=OP.mult))
                vwait(v)
                vmark(v.tensor_tensor_scan(posi8[:], me8[:], zer8[:], 0.0,
                                           op0=OP.add, op1=OP.add))
                vwait(v)
                v.tensor_tensor(posi8[:], posi8[:], me8[:],
                                op=OP.subtract).then_inc(S["pscn"], 1)
                wg(v, "ptt", tth + 1)
                wg(v, "sca", 16 * 64 * tth)
                v.tensor_copy(posT_sb[:], ps_out[0][:, 0:64]).then_inc(S["ptv"], 1)
                v.wait_ge(S["ptv"], tth + 1)
                for tc in range(8):
                    tg = tth * 8 + tc
                    rank = posT_sb[:, tc * 8:tc * 8 + 8]
                    vmark(v.tensor_scalar(pv3[:], gatef_sb[:, tc * 8:tc * 8 + 8],
                                          0.0, 1.0, op0=OP.is_gt, op1=OP.mult))
                    vwait(v)
                    vmark(v.tensor_scalar(pv2[:], pv3[:], 1.0, -1.0e9,
                                          op0=OP.subtract, op1=OP.mult))
                    vwait(v)
                    vmark(v.tensor_tensor(pv1[:], rank, pv2[:], op=OP.add))
                    vwait(v)
                    vmark(v.tensor_copy(idxu_sb[:, tc * 8:tc * 8 + 8], pv1[:]))
                    vwait(v)
                    vmark(v.tensor_tensor(pv1[:], rank, posbase_sb[:, tth * 8:tth * 8 + 8],
                                          op=OP.add))
                    vwait(v)
                    vmark(v.tensor_scalar(pv2[:], rank, CAP - 0.5, 1.0,
                                          op0=OP.is_gt, op1=OP.mult))
                    vwait(v)
                    vmark(v.tensor_scalar(pv3[:], pv2[:], -1.0, 1.0,
                                          op0=OP.mult, op1=OP.add))
                    vwait(v)
                    vmark(v.tensor_tensor(pv1[:], pv1[:], pv3[:], op=OP.mult))
                    vwait(v)
                    vmark(v.tensor_scalar(pv2[:], pv2[:], float(ZROW), 1.0,
                                          op0=OP.mult, op1=OP.mult))
                    vwait(v)
                    vmark(v.tensor_tensor(pv1[:], pv1[:], pv2[:], op=OP.add))
                    vwait(v)
                    vmark(v.tensor_tensor(pv2[:], eq1_sb[:, tc * 8:tc * 8 + 8], pv1[:],
                                          op=OP.mult))
                    vwait(v)
                    vmark(v.reduce_sum(pc1[:], pv2[:], axis=AX))
                    vwait(v)
                    vmark(v.tensor_scalar(pc1[:], pc1[:], float(ZROW), 1.0,
                                          op0=OP.min, op1=OP.mult))
                    vwait(v)
                    vmark(v.tensor_copy(pos1u[:, tg:tg + 1], pc1[:]))
                    vwait(v)
                    vmark(v.tensor_tensor(pv2[:], eq2_sb[:, tc * 8:tc * 8 + 8], pv1[:],
                                          op=OP.mult))
                    vwait(v)
                    vmark(v.reduce_sum(pc1[:], pv2[:], axis=AX))
                    vwait(v)
                    vmark(v.tensor_scalar(pc1[:], pc1[:], float(ZROW), 1.0,
                                          op0=OP.min, op1=OP.mult))
                    vwait(v)
                    vmark(v.tensor_copy(pos2u[:, tg:tg + 1], pc1[:]))
                    vwait(v)
                    vmark(v.tensor_copy(gcol1[:, tg:tg + 1], g1_sb[:, tc:tc + 1]))
                    vwait(v)
                    v.tensor_copy(gcol2[:, tg:tg + 1], g2_sb[:, tc:tc + 1]
                                  ).then_inc(S["ix"], 1)
                for m in range(8):
                    i2 = (tth * 9 + 0) * 8 + m
                    wg(v, "prm", i2 + 1)
                    aslice = acc[:, m * 2048 + tth * 1024:m * 2048 + tth * 1024 + 1024]
                    v.tensor_scalar(aslice, ps_out[i2 % 2][:, 0:1024],
                                    bproj_sb[:, 0 * 8 + m:0 * 8 + m + 1], 1.0,
                                    op0=OP.add, op1=OP.mult).then_inc(S["pse"], 1)

    stackB.close()
    psB.close()

    stackC = ExitStack()
    x1f = [stackC.enter_context(nc.sbuf_tensor("c_x1f%d" % i, [128, 1024], F32)) for i in range(2)]
    y_sb = [stackC.enter_context(nc.sbuf_tensor("c_y%d" % i, [128, 1024], F32)) for i in range(2)]
    c1_sb = [stackC.enter_context(nc.sbuf_tensor("c_c1_%d" % i, [128, 1024], BF16)) for i in range(2)]
    c2_sb = [stackC.enter_context(nc.sbuf_tensor("c_c2_%d" % i, [128, 1024], BF16)) for i in range(2)]
    tmpc = stackC.enter_context(nc.sbuf_tensor("c_tmpc", [128, 1024], F32))
    psC = ExitStack()
    ps_y = [psC.enter_context(nc.psum_tensor("ps_y%d" % i, [128, 1024], BF16)) for i in range(2)]

    with nc.Block() as bc:
        @bc.sync
        def _(sy):
            for tc in range(16):
                wg(sy, "cy", max(tc - 1, 0))
                sy.dma_start(out=x1f[tc % 2][:],
                             in_=x1_hbm[tc * 128:tc * 128 + 128, :]).then_inc(S["cx1"], 16)
                wg(sy, "cy", tc + 1)
                sy.dma_start(out=y_out[tc * 128:tc * 128 + 128, :],
                             in_=y_sb[tc % 2][:]).then_inc(S["y"], 16)
            sy.wait_ge(S["y"], 16 * 16)

        @bc.gpsimd
        def _(g):
            r_zr = g.alloc_register()
            g.reg_mov(r_zr, ZROW)
            g.wait_ge(S["ops"], 16 * 48)
            g.wait_ge(S["ix"], 16)
            for tc in range(16):
                wg(g, "cy", max(tc - 1, 0))
                for pu, dst in ((pos1u, c1_sb), (pos2u, c2_sb)):
                    g.indirect_dma_start(
                        out=dst[tc % 2][:], out_offset=None,
                        in_=out_perm,
                        in_offset=bass.IndirectOffsetOnAxis(pu[:, tc:tc + 1], axis=0),
                        bounds_check=r_zr, oob_is_err=False
                        ).then_inc(S["cg"], 16)

        @bc.tensor
        def _(te):
            wg(te, "pse", 144)
            for tc in range(16):
                wg(te, "cy", max(tc - 1, 0))
                for m in range(8):
                    mm = te.transpose(ps_y[tc % 2][:, m * 128:m * 128 + 128],
                                      acc[:, m * 2048 + tc * 128:m * 2048 + tc * 128 + 128],
                                      idbf[:])
                mm.then_inc(S["ct"], 1)

        @bc.vector
        def _(v):
            for tc in range(16):
                wg(v, "cx1", 16 * (tc + 1))
                wg(v, "ct", tc + 1)
                wg(v, "cg", 32 * (tc + 1))
                wg(v, "y", 16 * max(tc - 1, 0))
                ys = y_sb[tc % 2][:]
                v.tensor_tensor(ys, x1f[tc % 2][:], ps_y[tc % 2][:, 0:1024],
                                op=OP.add)
                v.tensor_scalar(tmpc[:], c1_sb[tc % 2][:], gcol1[:, tc:tc + 1], 1.0,
                                op0=OP.mult, op1=OP.mult)
                v.tensor_tensor(ys, ys, tmpc[:], op=OP.add)
                v.tensor_scalar(tmpc[:], c2_sb[tc % 2][:], gcol2[:, tc:tc + 1], 1.0,
                                op0=OP.mult, op1=OP.mult)
                v.tensor_tensor(ys, ys, tmpc[:], op=OP.add).then_inc(S["cy"], 1)

    stackC.close()
    psC.close()
    return nc


_CACHED = {}


def _get_nc():
    if "nc" not in _CACHED:
        _CACHED["nc"] = build_program(debug=False)
    return _CACHED["nc"]


def _host_inputs(L0_attn_x, L0_attn_w_in, L0_attn_w_out, L0_ffn_Wfc_s, L0_ffn_bfc_s,
                 L0_ffn_Wproj_s, L0_ffn_bproj_s, L0_ffn_Wr, L0_ffn_Wfc, L0_ffn_bfc,
                 L0_ffn_Wproj, L0_ffn_bproj):
    f32 = np.float32
    x = np.ascontiguousarray(np.asarray(L0_attn_x, f32).reshape(16384, 1024))
    w_inT = np.ascontiguousarray(np.asarray(L0_attn_w_in, f32).T)
    w_outT = np.ascontiguousarray(np.asarray(L0_attn_w_out, f32).T)
    Wfc_all = np.concatenate([np.asarray(L0_ffn_Wfc_s, f32)[None],
                              np.asarray(L0_ffn_Wfc, f32)], 0)
    Wproj_all = np.concatenate([np.asarray(L0_ffn_Wproj_s, f32)[None],
                                np.asarray(L0_ffn_Wproj, f32)], 0)
    bfc_all = np.concatenate([np.asarray(L0_ffn_bfc_s, f32)[None],
                              np.asarray(L0_ffn_bfc, f32)], 0)
    bproj_all = np.concatenate([np.asarray(L0_ffn_bproj_s, f32)[None],
                                np.asarray(L0_ffn_bproj, f32)], 0)
    W1H = np.ascontiguousarray(
        Wfc_all.reshape(NEXP, 8, 128, 8, 512).transpose(0, 3, 2, 1, 4).reshape(NEXP, 8, 128, 4096))
    W2H = np.ascontiguousarray(
        Wproj_all.reshape(NEXP, 32, 128, 8, 128).transpose(0, 3, 2, 1, 4).reshape(NEXP, 8, 128, 4096))
    bfcB = np.ascontiguousarray(bfc_all.reshape(NEXP, 32, 128).transpose(0, 2, 1))
    bprojB = np.ascontiguousarray(bproj_all.reshape(NEXP, 8, 128).transpose(0, 2, 1))
    WrH = np.ascontiguousarray(
        np.asarray(L0_ffn_Wr, f32).reshape(8, 128, 8).transpose(1, 0, 2).reshape(128, 64))
    kjb = np.ascontiguousarray(np.broadcast_to(np.arange(1024, dtype=f32), (128, 1024)))
    qi = np.arange(512, dtype=np.int64)[:, None]
    cc = np.arange(512, dtype=np.int64)[None, :]
    mprev = np.where(qi <= cc, 0.0, NEG).astype(f32)
    mcur = np.where(cc <= qi, 0.0, NEG).astype(f32)
    pp_ = np.arange(128)[:, None]
    cc_ = np.arange(128)[None, :]
    tprev = np.where(pp_ <= cc_, 0.0, NEG).astype(ml_dtypes.bfloat16)
    tcur = np.where(cc_ <= pp_, 0.0, NEG).astype(ml_dtypes.bfloat16)
    maskP0_z = np.zeros((128, 512), ml_dtypes.bfloat16)
    maskP0_n = np.full((128, 512), NEG, ml_dtypes.bfloat16)
    idbf = np.eye(128, dtype=ml_dtypes.bfloat16)
    idf32 = np.eye(128, dtype=f32)
    consts = np.zeros((128, 4), f32)
    consts[:, 0] = 1e-5

    tokid = np.ascontiguousarray(
        (np.arange(8)[None, :] * 128 + np.arange(128)[:, None]).astype(np.uint32))
    posbase = np.ascontiguousarray(np.broadcast_to(
        (np.arange(16, dtype=f32) * 384.0)[None, :], (128, 16)))
    zci = np.zeros((16 * 384, 1), np.uint32)
    z1k = np.zeros((1, 1024), ml_dtypes.bfloat16)
    shared = dict(w_inT=w_inT, w_outT=w_outT, W1H=W1H, W2H=W2H, bfcB=bfcB,
                  bprojB=bprojB, WrH=WrH, kjb=kjb, tprev=tprev, tcur=tcur,
                  idbf=idbf, idf32=idf32, consts=consts,
                  tokid=tokid, posbase=posbase, zci=zci, z1k=z1k)
    zeros_halo = np.zeros((512, 1024), f32)
    in_maps = []
    for c in range(8):
        m = dict(shared)
        m["x_own"] = np.ascontiguousarray(x[c * T:(c + 1) * T])
        m["x_halo"] = zeros_halo if c in (0, 4) else np.ascontiguousarray(x[c * T - 512:c * T])
        m["maskP0"] = maskP0_n if c in (0, 4) else maskP0_z
        in_maps.append(m)
    return in_maps


def kernel(**inputs):
    in_maps = _host_inputs(**inputs)
    nc = _get_nc()
    res = run_bass_kernel_spmd(nc, in_maps, core_ids=list(range(8)))
    y = np.concatenate([np.asarray(res.results[c]["y"]) for c in range(8)], axis=0)
    return y.reshape(2, 8192, 1024).astype(np.float32, copy=False)


# version 14
# speedup vs baseline: 1.0810x; 1.0138x over previous
import numpy as np
import ml_dtypes
from contextlib import ExitStack
import concourse.bass as bass
import concourse.mybir as mybir
from concourse.bass_utils import run_bass_kernel_spmd
from concourse.alu_op_type import AluOpType as OP

F32 = mybir.dt.float32
BF16 = mybir.dt.bfloat16
AF = mybir.ActivationFunctionType
AX = mybir.AxisListType.X

T = 2048
D = 1024
NEXP = 9
H = 16
NITER = 5
CAP = 384
ZROW = 16 * CAP
NEG = -1.0e30
SLOPES = [float(np.exp2(np.float32(-(h + 1) * 8.0 / 16.0))) for h in range(H)]


def qke_cum(j):
    if j < 0:
        return 0
    return 8 + 12 * j


def build_program(debug=False, debug2=False):
    nc = bass.Bass("TRN2", target_bir_lowering=False, debug=False,
                   detect_race_conditions=False)

    x_own = nc.dram_tensor("x_own", [T, D], F32, kind="ExternalInput").ap()
    x_halo = nc.dram_tensor("x_halo", [512, D], F32, kind="ExternalInput").ap()
    w_inT = nc.dram_tensor("w_inT", [D, 3 * D], F32, kind="ExternalInput").ap()
    w_outT = nc.dram_tensor("w_outT", [D, D], F32, kind="ExternalInput").ap()
    W1H = nc.dram_tensor("W1H", [NEXP, 8, 128, 4096], F32, kind="ExternalInput").ap()
    W2H = nc.dram_tensor("W2H", [NEXP, 8, 128, 4096], F32, kind="ExternalInput").ap()
    tokid_in = nc.dram_tensor("tokid", [128, 8], mybir.dt.uint32, kind="ExternalInput").ap()
    posbase_in = nc.dram_tensor("posbase", [128, 16], F32, kind="ExternalInput").ap()
    zci_in = nc.dram_tensor("zci", [16 * CAP, 1], mybir.dt.uint32, kind="ExternalInput").ap()
    z1k_in = nc.dram_tensor("z1k", [1, 1024], BF16, kind="ExternalInput").ap()
    bfcB = nc.dram_tensor("bfcB", [NEXP, 128, 32], F32, kind="ExternalInput").ap()
    bprojB = nc.dram_tensor("bprojB", [NEXP, 128, 8], F32, kind="ExternalInput").ap()
    WrH = nc.dram_tensor("WrH", [128, 64], F32, kind="ExternalInput").ap()
    kjb_in = nc.dram_tensor("kjb", [128, 1024], F32, kind="ExternalInput").ap()
    tprev_in = nc.dram_tensor("tprev", [128, 128], BF16, kind="ExternalInput").ap()
    tcur_in = nc.dram_tensor("tcur", [128, 128], BF16, kind="ExternalInput").ap()
    maskP0_in = nc.dram_tensor("maskP0", [128, 512], BF16, kind="ExternalInput").ap()
    idbf_in = nc.dram_tensor("idbf", [128, 128], BF16, kind="ExternalInput").ap()
    idf_in = nc.dram_tensor("idf32", [128, 128], F32, kind="ExternalInput").ap()
    consts_in = nc.dram_tensor("consts", [128, 4], F32, kind="ExternalInput").ap()

    y_out = nc.dram_tensor("y", [T, D], F32, kind="ExternalOutput").ap()
    skind = "ExternalOutput" if debug else "Internal"
    x1_hbm = nc.dram_tensor("x1_hbm", [T, D], F32, kind=skind).ap()
    xn2_hbm = nc.dram_tensor("xn2_hbm", [T, D], BF16, kind=skind).ap()
    g8_hbm = nc.dram_tensor("g8_hbm", [8, T], F32, kind=skind).ap()
    idxl_hbm = nc.dram_tensor("idxl_hbm", [16 * CAP, 1], mybir.dt.uint32, kind="Internal").ap()
    W1b = nc.dram_tensor("W1b", [NEXP, 8, 128, 4096], BF16, kind="Internal").ap()
    W2b = nc.dram_tensor("W2b", [NEXP, 8, 128, 4096], BF16, kind="Internal").ap()
    out_perm = nc.dram_tensor("out_perm", [16 * CAP + 1, 1024], BF16, kind="Internal").ap()

    SBC = nc.alloc_sbuf_tensor
    idbf = SBC("c_idbf", [128, 128], BF16)
    idf32 = SBC("c_idf32", [128, 128], F32)
    consts = SBC("c_consts", [128, 4], F32)
    wr_sb = SBC("c_wr", [128, 64], BF16)
    bfc_sb = SBC("c_bfc", [128, NEXP * 32], F32)
    bproj_sb = SBC("c_bproj", [128, NEXP * 8], F32)
    tokid_sb = SBC("c_tokid", [128, 8], mybir.dt.uint32)
    posbase_sb = SBC("c_posbase", [128, 16], F32)

    S = {}
    for nm in ["pl", "plg", "x", "sq1", "va1", "std1", "st1", "xn1", "tpp", "tpe",
               "qkm", "qke", "sm1", "sc", "t2", "p", "pn", "tp", "pte", "pv",
               "oe", "op", "x1", "sq2", "va2", "std2", "st2", "xn2",
               "stx1", "stxn2",
               "xn2t", "w1", "w2", "fcp", "gel", "prm", "pse",
               "rtm", "gv1", "gs", "gv", "gtt", "g8v", "g8st",
               "pscn", "ptv", "ptt", "ix", "sca", "ixl", "xga", "xgp", "xge", "pwc", "pww",
               "ott", "opv", "ops", "cg",
               "cx1", "ct", "cy", "y", "vv"]:
        S[nm] = nc.alloc_semaphore("s_" + nm)

    def wg(eng, nm, val):
        if val > 0:
            eng.wait_ge(S[nm], val)

    vvc = [0]

    def vmark(inst):
        inst.then_inc(S["vv"], 1)
        vvc[0] += 1

    def vwait(v):
        if vvc[0] > 0:
            v.wait_ge(S["vv"], vvc[0])

    stackA = ExitStack()

    def SBA(name, shape, dt):
        return stackA.enter_context(nc.sbuf_tensor(name, shape, dt))

    winT = SBA("a_winT", [128, 8 * 3072], BF16)
    woutT = SBA("a_woutT", [128, 8 * 1024], BF16)
    kjb = SBA("a_kjb", [128, 1024], F32)
    tprev = SBA("a_tprev", [128, 128], BF16)
    tcur = SBA("a_tcur", [128, 128], BF16)
    maskP0 = SBA("a_maskP0", [128, 512], BF16)
    x_sb = SBA("a_x", [128, 4096], F32)
    xn1 = SBA("a_xn1", [128, 4096], BF16)
    xn1T = SBA("a_xn1T", [128, 8 * 512], BF16)
    q_sb = SBA("a_q", [128, 8 * 512], BF16)
    k_ring = SBA("a_kring", [128, 8 * 1024], BF16)
    v_ring = SBA("a_vring", [128, 8 * 1024], BF16)
    t_sb = SBA("a_t", [128, 3 * 640], F32)
    p_sb = SBA("a_p", [128, 3 * 640], BF16)
    pT_sb = SBA("a_pT", [128, 2 * 2560], BF16)
    o_sb = SBA("a_o", [128, 8 * 512], BF16)
    x1t = SBA("a_x1t", [128, 2 * 1024], F32)
    xn2t = SBA("a_xn2t", [128, 2 * 1024], BF16)
    sq_scr = SBA("a_sqscr", [128, 1024], F32)
    pwb = [SBA("a_pwb%d" % i, [128, 4096], BF16) for i in range(2)]
    st = SBA("a_stats", [128, 48], F32)

    psA = ExitStack()
    ps_a = [psA.enter_context(nc.psum_tensor("ps_a%d" % i, [128, 1024], F32)) for i in range(2)]
    ps_pt = [psA.enter_context(nc.psum_tensor("ps_pt%d" % i, [128, 1024], BF16)) for i in range(2)]
    ps_o = [psA.enter_context(nc.psum_tensor("ps_o%d" % i, [128, 512], F32)) for i in range(2)]

    with nc.Block() as bp:
        @bp.sync
        def _(sy):
            for dst, src in [(idbf, idbf_in), (idf32, idf_in), (consts, consts_in),
                             (kjb, kjb_in), (tprev, tprev_in), (tcur, tcur_in),
                             (maskP0, maskP0_in)]:
                sy.dma_start(out=dst[:], in_=src).then_inc(S["pl"], 16)
            sy.dma_start(out=bfc_sb[:], in_=bfcB.rearrange("e p c -> p e c")).then_inc(S["pl"], 16)
            sy.dma_start(out=bproj_sb[:], in_=bprojB.rearrange("e p c -> p e c")).then_inc(S["pl"], 16)
            sy.dma_start(out=tokid_sb[:], in_=tokid_in).then_inc(S["pl"], 16)
            sy.dma_start(out=posbase_sb[:], in_=posbase_in).then_inc(S["pl"], 16)
            sy.wait_ge(S["pl"], 11 * 16)

        @bp.gpsimd
        def _(g):
            for ec in range(8):
                g.dma_start(out=winT[:, ec * 3072:(ec + 1) * 3072],
                            in_=w_inT[ec * 128:(ec + 1) * 128, :],
                            max_dma_last_dim=8192).then_inc(S["plg"], 16)
                g.dma_start(out=woutT[:, ec * 1024:(ec + 1) * 1024],
                            in_=w_outT[ec * 128:(ec + 1) * 128, :],
                            max_dma_last_dim=8192).then_inc(S["plg"], 16)
            g.dma_start(out=wr_sb[:], in_=WrH, max_dma_last_dim=8192).then_inc(S["plg"], 16)
            g.dma_start(out=idxl_hbm, in_=zci_in).then_inc(S["plg"], 16)
            g.dma_start(out=out_perm[ZROW:ZROW + 1, :], in_=z1k_in).then_inc(S["plg"], 16)
            g.wait_ge(S["plg"], 19 * 16)

    nc.const_aps.register(F32, 1e-5, consts[:, 0:1])
    PL = 11 * 16
    PLG = 19 * 16

    def ln_scalar_sq(sc, src, base, tc, sem_sq, cnt_sq, sem_va, cnt_va_war):
        wg(sc, sem_va, cnt_va_war)
        c = base + (tc % 2) * 8
        sc.activation(sq_scr[:], src, AF.Square,
                      accum_out=st[:, c + 1:c + 2]).then_inc(S[sem_sq], 1)

    def ln_vector_stats(v, src, base, tc, sem_sq, cnt_sq, sem_va, sem_std, cnt_std,
                        sem_st, sem_xn, cnt_xn_war):
        c = base + (tc % 2) * 8
        wg(v, sem_xn, cnt_xn_war)
        vmark(v.reduce_sum(st[:, c + 0:c + 1], src, axis=AX))
        wg(v, sem_sq, cnt_sq)
        vwait(v)
        vmark(v.tensor_scalar_mul(st[:, c + 2:c + 3], st[:, c + 0:c + 1], 1.0 / 1024.0))
        v.tensor_scalar_mul(st[:, c + 3:c + 4], st[:, c + 1:c + 2], 1.0 / 1024.0)
        vwait(v)
        vmark(v.tensor_scalar(st[:, c + 4:c + 5], st[:, c + 2:c + 3],
                              st[:, c + 2:c + 3], 1.0, op0=OP.mult, op1=OP.mult))
        vwait(v)
        v.tensor_tensor(st[:, c + 3:c + 4], st[:, c + 3:c + 4],
                        st[:, c + 4:c + 5], op=OP.subtract).then_inc(S[sem_va], 1)
        wg(v, sem_std, cnt_std)
        vmark(v.reciprocal(st[:, c + 5:c + 6], st[:, c + 4:c + 5]))
        vwait(v)
        v.tensor_scalar(st[:, c + 6:c + 7], st[:, c + 2:c + 3],
                        st[:, c + 5:c + 6], -1.0, op0=OP.mult, op1=OP.mult
                        ).then_inc(S[sem_st], 1)

    def ln_scalar_sqrt(sc, base, tc, sem_va, cnt_va, sem_std, sem_st, cnt_st_war):
        c = base + (tc % 2) * 8
        wg(sc, sem_va, cnt_va)
        wg(sc, sem_st, cnt_st_war)
        sc.activation(st[:, c + 4:c + 5], st[:, c + 3:c + 4], AF.Sqrt,
                      bias=1e-5).then_inc(S[sem_std], 1)

    def ln_scalar_apply(sc, dst, src, base, tc, sem_st, cnt_st, sem_xn):
        c = base + (tc % 2) * 8
        wg(sc, sem_st, cnt_st)
        sc.activation(dst, src, AF.Identity, bias=st[:, c + 6:c + 7],
                      scale=st[:, c + 5:c + 6]).then_inc(S[sem_xn], 1)

    with nc.Block() as ba:
        @ba.gpsimd
        def _(g):
            slices = [(s_t, d_t, e, hs) for s_t, d_t in ((W2H, W2b),)
                      for e in range(NEXP) for hs in range(8)]
            for i in range(len(slices) + 1):
                if i < len(slices):
                    s_t, d_t, e, hs = slices[i]
                    wg(g, "pww", 16 * max(i - 1, 0))
                    g.dma_start(out=pwb[i % 2][:], in_=s_t[e, hs],
                                max_dma_last_dim=8192).then_inc(S["pwc"], 16)
                if i >= 1:
                    s_t, d_t, e, hs = slices[i - 1]
                    g.wait_ge(S["pwc"], 16 * i)
                    g.dma_start(out=d_t[e, hs], in_=pwb[(i - 1) % 2][:]
                                ).then_inc(S["pww"], 16)
            g.wait_ge(S["pww"], 16 * 72)

        @ba.sync
        def _(sy):
            for j in range(NITER):
                wg(sy, "xn1", 4 * j)
                wg(sy, "st1", 4 * j)
                wg(sy, "sq1", 4 * j)
                wg(sy, "x1", 4 * (j - 1))
                src = x_halo if j == 0 else x_own[(j - 1) * 512:j * 512, :]
                sy.dma_start(out=x_sb[:],
                             in_=src.rearrange("(tc p) f -> p tc f", p=128)
                             ).then_inc(S["x"], 16)
                if j == 0:
                    continue
                b = j - 1
                for tc in range(4):
                    wg(sy, "x1", 4 * b + tc + 1)
                    r0 = b * 512 + tc * 128
                    sy.dma_start(out=x1_hbm[r0:r0 + 128, :],
                                 in_=x1t[:, (tc % 2) * 1024:(tc % 2) * 1024 + 1024]
                                 ).then_inc(S["stx1"], 16)
                    wg(sy, "xn2", 4 * b + tc + 1)
                    sy.dma_start(out=xn2_hbm[r0:r0 + 128, :],
                                 in_=xn2t[:, (tc % 2) * 1024:(tc % 2) * 1024 + 1024]
                                 ).then_inc(S["stxn2"], 16)
            sy.wait_ge(S["stx1"], 16 * 16)
            sy.wait_ge(S["stxn2"], 16 * 16)

        @ba.tensor
        def _(te):
            te.wait_ge(S["pl"], PL)
            te.wait_ge(S["plg"], PLG)
            for j in range(NITER):
                b = j - 1
                slot = j % 2
                wg(te, "xn1", 4 * (j + 1))
                for ec in range(8):
                    wg(te, "tpe", 8 * j + max(ec - 1, 0))
                    if ec <= 1:
                        wg(te, "pte", 64 * max(b, 0))
                    for tc in range(4):
                        mm = te.transpose(ps_pt[ec % 2][:, tc * 128:tc * 128 + 128],
                                          xn1[:, tc * 1024 + ec * 128:tc * 1024 + ec * 128 + 128],
                                          idbf[:])
                    mm.then_inc(S["tpp"], 1)
                wg(te, "tpe", 8 * (j + 1))
                wg(te, "x1", 4 * (j - 1))
                nt = 0
                if j >= 1:
                    for fcp in range(4):
                        wg(te, "qke", qke_cum(j - 1) + max(nt - 1, 0))
                        pa = ps_a[nt % 2]
                        for half in range(2):
                            fc = fcp * 2 + half
                            for ec in range(8):
                                mm = te.matmul(pa[:, half * 512:half * 512 + 512],
                                               winT[:, ec * 3072 + fc * 128:ec * 3072 + fc * 128 + 128],
                                               xn1T[:, ec * 512:ec * 512 + 512],
                                               start=(ec == 0), stop=(ec == 7))
                        mm.then_inc(S["qkm"], 1)
                        nt += 1
                for fcp in range(4):
                    wg(te, "qke", qke_cum(j - 1) + max(nt - 1, 0))
                    pa = ps_a[nt % 2]
                    for half in range(2):
                        fc = fcp * 2 + half
                        for ec in range(8):
                            mm = te.matmul(pa[:, half * 512:half * 512 + 512],
                                           winT[:, ec * 3072 + 1024 + fc * 128:ec * 3072 + 1024 + fc * 128 + 128],
                                           xn1T[:, ec * 512:ec * 512 + 512],
                                           start=(ec == 0), stop=(ec == 7))
                    mm.then_inc(S["qkm"], 1)
                    nt += 1
                for tc in range(4):
                    wg(te, "qke", qke_cum(j - 1) + max(nt - 1, 0))
                    pa = ps_a[nt % 2]
                    for n in range(2):
                        for ec in range(8):
                            mm = te.matmul(pa[:, n * 512:n * 512 + 512],
                                           xn1T[:, ec * 512 + tc * 128:ec * 512 + tc * 128 + 128],
                                           winT[:, ec * 3072 + 2048 + n * 512:ec * 3072 + 2048 + n * 512 + 512],
                                           start=(ec == 0), stop=(ec == 7))
                    mm.then_inc(S["qkm"], 1)
                    nt += 1

                if j == 0:
                    continue
                wg(te, "qke", qke_cum(j))
                prev, cur = (j - 1) % 2, j % 2

                def emit_scores(s):
                    h, qc = s // 4, s % 4
                    hp, hh = h // 2, (h % 2) * 64
                    n1 = 512 - qc * 128
                    wg(te, "sm1", 64 * b + max(s - 1, 0))
                    pa = ps_a[s % 2]
                    q_ap = q_sb[hh:hh + 64, hp * 512 + qc * 128:hp * 512 + qc * 128 + 128]
                    te.matmul(pa[:, 0:n1], q_ap,
                              k_ring[hh:hh + 64, hp * 1024 + prev * 512 + qc * 128:hp * 1024 + prev * 512 + 512],
                              start=True, stop=True)
                    if qc > 0:
                        te.matmul(pa[:, n1:512], q_ap,
                                  k_ring[hh:hh + 64, hp * 1024 + cur * 512:hp * 1024 + cur * 512 + qc * 128],
                                  start=True, stop=True)
                    te.matmul(pa[:, 512:640], q_ap,
                              k_ring[hh:hh + 64, hp * 1024 + cur * 512 + qc * 128:hp * 1024 + cur * 512 + qc * 128 + 128],
                              start=True, stop=True).then_inc(S["sc"], 1)

                def emit_transposes(sg):
                    wg(te, "pn", 64 * b + sg + 1)
                    if sg >= 2:
                        wg(te, "pte", 64 * b + sg - 1)
                    else:
                        wg(te, "tpe", 8 * (j + 1))
                    for kc in range(5):
                        mm = te.transpose(ps_pt[sg % 2][:, kc * 128:kc * 128 + 128],
                                          p_sb[:, (sg % 3) * 640 + kc * 128:(sg % 3) * 640 + kc * 128 + 128],
                                          idbf[:])
                    mm.then_inc(S["tp"], 1)

                def emit_pv(h):
                    pair, hh = h // 2, (h % 2) * 64
                    wg(te, "pte", 64 * b + 4 * h + 4)
                    if h % 2 == 0 and pair >= 2:
                        wg(te, "oe", 8 * b + pair - 1)
                    po = ps_o[pair % 2]
                    for qc in range(4):
                        for kc in range(5):
                            wc = qc + kc
                            sl, tc2 = (prev, wc) if wc < 4 else (cur, wc - 4)
                            mm = te.matmul(po[hh:hh + 64, qc * 128:qc * 128 + 128],
                                           v_ring[:, (sl * 4 + tc2) * 1024 + h * 64:(sl * 4 + tc2) * 1024 + h * 64 + 64],
                                           pT_sb[:, (h % 2) * 2560 + qc * 640 + kc * 128:(h % 2) * 2560 + qc * 640 + kc * 128 + 128],
                                           start=(kc == 0), stop=(kc == 4))
                    mm.then_inc(S["pv"], 1)

                for s in range(68):
                    if s < 64:
                        emit_scores(s)
                    if 2 <= s < 66:
                        emit_transposes(s - 2)
                    if s >= 4 and (s - 4) % 4 == 3:
                        emit_pv((s - 4) // 4)
                wg(te, "oe", 8 * (b + 1))
                wg(te, "sm1", 64 * (b + 1))
                for tc in range(4):
                    wg(te, "x1", 4 * b + max(tc - 1, 0))
                    pa = ps_a[tc % 2]
                    for n in range(2):
                        for ec in range(8):
                            mm = te.matmul(pa[:, n * 512:n * 512 + 512],
                                           o_sb[:, ec * 512 + tc * 128:ec * 512 + tc * 128 + 128],
                                           woutT[:, ec * 1024 + n * 512:ec * 1024 + n * 512 + 512],
                                           start=(ec == 0), stop=(ec == 7))
                    mm.then_inc(S["op"], 1)

        @ba.scalar
        def _(sc):
            sc.wait_ge(S["pl"], PL)
            sc.wait_ge(S["plg"], PLG)
            for j in range(NITER):
                b = j - 1
                slot = j % 2
                wg(sc, "x", 16 * (j + 1))
                xoff = 0
                for tc in range(4):
                    src = x_sb[:, tc * 1024:tc * 1024 + 1024]
                    ln_scalar_sq(sc, src, 0, tc, "sq1", 0, "va1", 4 * j + tc - 1)
                    ln_scalar_sqrt(sc, 0, tc, "va1", 4 * j + tc + 1, "std1", "st1",
                                   4 * j + tc - 1)
                    ln_scalar_apply(sc, xn1[:, tc * 1024:tc * 1024 + 1024], src,
                                    0, tc, "st1", 4 * j + tc + 1, "xn1")
                for ec in range(8):
                    wg(sc, "tpp", 8 * j + ec + 1)
                    sc.copy(xn1T[:, ec * 512:ec * 512 + 512],
                            ps_pt[ec % 2][:, 0:512]).then_inc(S["tpe"], 1)
                wg(sc, "pv", 16 * (j - 1))
                nt = 0
                if j >= 1:
                    for fcp in range(4):
                        wg(sc, "qkm", qke_cum(j - 1) + nt + 1)
                        sc.activation(q_sb[:, fcp * 1024:fcp * 1024 + 1024],
                                      ps_a[nt % 2][:, 0:1024], AF.Identity,
                                      scale=0.125).then_inc(S["qke"], 1)
                        nt += 1
                for fcp in range(4):
                    wg(sc, "qkm", qke_cum(j - 1) + nt + 1)
                    for half in range(2):
                        fc = fcp * 2 + half
                        ins = sc.copy(k_ring[:, fc * 1024 + slot * 512:fc * 1024 + slot * 512 + 512],
                                      ps_a[nt % 2][:, half * 512:half * 512 + 512])
                    ins.then_inc(S["qke"], 1)
                    nt += 1
                for tc in range(4):
                    wg(sc, "qkm", qke_cum(j - 1) + nt + 1)
                    sc.copy(v_ring[:, (slot * 4 + tc) * 1024:(slot * 4 + tc) * 1024 + 1024],
                            ps_a[nt % 2][:, 0:1024]).then_inc(S["qke"], 1)
                    nt += 1

                if j == 0:
                    continue

                def emit_pte(sg):
                    h2 = sg // 4
                    wg(sc, "tp", 64 * b + sg + 1)
                    if sg % 4 == 0:
                        wg(sc, "pv", 16 * b + max(h2 - 1, 0))
                    sc.copy(pT_sb[:, (h2 % 2) * 2560 + (sg % 4) * 640:(h2 % 2) * 2560 + (sg % 4) * 640 + 640],
                            ps_pt[sg % 2][:, 0:640]).then_inc(S["pte"], 1)
                    if sg % 8 == 7:
                        pair2 = sg // 8
                        wg(sc, "pv", 16 * b + 2 * pair2 + 2)
                        sc.copy(o_sb[:, pair2 * 512:pair2 * 512 + 512],
                                ps_o[pair2 % 2][:, 0:512]).then_inc(S["oe"], 1)

                for s in range(66):
                    if s < 64:
                        wg(sc, "t2", 64 * b + s + 1)
                        wg(sc, "tp", 64 * b + s - 2)
                        wg(sc, "pn", 64 * b + s - 2)
                        nmc, smc = 16 + (s % 3), 19 + (s % 3)
                        sc.activation(p_sb[:, (s % 3) * 640:(s % 3) * 640 + 640],
                                      t_sb[:, (s % 3) * 640:(s % 3) * 640 + 640],
                                      AF.Exp, bias=st[:, nmc:nmc + 1],
                                      accum_out=st[:, smc:smc + 1]).then_inc(S["p"], 1)
                    if s >= 2:
                        emit_pte(s - 2)
                for tc in range(4):
                    wg(sc, "x1", 4 * b + tc + 1)
                    src = x1t[:, (tc % 2) * 1024:(tc % 2) * 1024 + 1024]
                    ln_scalar_sq(sc, src, 28, tc, "sq2", 0, "va2", 4 * b + tc - 1)
                    ln_scalar_sqrt(sc, 28, tc, "va2", 4 * b + tc + 1, "std2", "st2",
                                   4 * b + tc - 1)
                    wg(sc, "stxn2", 16 * (4 * b + tc - 1))
                    ln_scalar_apply(sc, xn2t[:, (tc % 2) * 1024:(tc % 2) * 1024 + 1024],
                                    src, 28, tc, "st2", 4 * b + tc + 1, "xn2")

        @ba.vector
        def _(v):
            v.wait_ge(S["pl"], PL)
            v.wait_ge(S["plg"], PLG)
            for j in range(NITER):
                b = j - 1
                wg(v, "x", 16 * (j + 1))
                for tc in range(4):
                    ln_vector_stats(v, x_sb[:, tc * 1024:tc * 1024 + 1024], 0, tc,
                                    "sq1", 4 * j + tc + 1, "va1", "std1",
                                    4 * j + tc + 1, "st1", "xn1", 4 * j + tc - 1)
                if j == 0:
                    continue

                def emit_pn(sp):
                    wg(v, "p", 64 * b + sp + 1)
                    smc, rc = 19 + (sp % 3), 22 + (sp % 3)
                    vmark(v.reciprocal(st[:, rc:rc + 1], st[:, smc:smc + 1]))
                    vwait(v)
                    pp = p_sb[:, (sp % 3) * 640:(sp % 3) * 640 + 640]
                    v.tensor_scalar(pp, pp, st[:, rc:rc + 1], 1.0,
                                    op0=OP.mult, op1=OP.mult).then_inc(S["pn"], 1)

                for s in range(64):
                    h, qc = s // 4, s % 4
                    n1 = 512 - qc * 128
                    wg(v, "sc", 64 * b + s + 1)
                    wg(v, "p", 64 * b + s - 2)
                    t0 = (s % 3) * 640
                    tt = t_sb[:, t0:t0 + 640]
                    v.scalar_tensor_tensor(tt, kjb[:, qc * 128:qc * 128 + 640],
                                           -SLOPES[h], ps_a[s % 2][:, 0:640],
                                           op0=OP.mult, op1=OP.add).then_inc(S["sm1"], 1)
                    v.tensor_tensor(t_sb[:, t0:t0 + 128], t_sb[:, t0:t0 + 128],
                                    tprev[:], op=OP.add)
                    v.tensor_tensor(t_sb[:, t0 + 512:t0 + 640], t_sb[:, t0 + 512:t0 + 640],
                                    tcur[:], op=OP.add)
                    if b == 0:
                        vmark(v.tensor_tensor(t_sb[:, t0:t0 + n1], t_sb[:, t0:t0 + n1],
                                              maskP0[:, qc * 128:512], op=OP.add))
                        vwait(v)
                    nmc = 16 + (s % 3)
                    v.reduce_max(st[:, nmc:nmc + 1], tt, axis=AX,
                                 negate=True).then_inc(S["t2"], 1)
                    if s >= 1:
                        emit_pn(s - 1)
                if True:
                    emit_pn(63)
                for tc in range(4):
                    wg(v, "op", 4 * b + tc + 1)
                    wg(v, "xn2", 4 * b + tc - 1)
                    wg(v, "stx1", 16 * (4 * b + tc - 1))
                    v.tensor_tensor(x1t[:, (tc % 2) * 1024:(tc % 2) * 1024 + 1024],
                                    x_sb[:, tc * 1024:tc * 1024 + 1024],
                                    ps_a[tc % 2][:, 0:1024], op=OP.add).then_inc(S["x1"], 1)
                    ln_vector_stats(v, x1t[:, (tc % 2) * 1024:(tc % 2) * 1024 + 1024],
                                    28, tc, "sq2", 4 * b + tc + 1, "va2", "std2",
                                    4 * b + tc + 1, "st2", "xn2", 4 * b + tc - 1)

    if debug2:
        dbg_names = [("d_x", x_sb, F32, 4096), ("d_xn1", xn1, BF16, 4096),
                     ("d_xn1T", xn1T, BF16, 4096), ("d_q", q_sb, BF16, 4096),
                     ("d_k", k_ring, BF16, 8192), ("d_v", v_ring, BF16, 8192),
                     ("d_t", t_sb, F32, 2048), ("d_p", p_sb, BF16, 2048),
                     ("d_pT", pT_sb, BF16, 4096), ("d_o", o_sb, BF16, 4096),
                     ("d_st", st, F32, 48)]
        dbg_outs = {nm: nc.dram_tensor(nm, [128, w], dt, kind="ExternalOutput").ap()
                    for nm, _, dt, w in dbg_names}
        sdbg = nc.alloc_semaphore("s_dbg")
        with nc.Block() as bdg:
            @bdg.sync
            def _(sy):
                for nm, tile, dt, w in dbg_names:
                    sy.dma_start(out=dbg_outs[nm], in_=tile[:]).then_inc(sdbg, 16)
                sy.wait_ge(sdbg, 16 * len(dbg_names))

    stackA.close()
    psA.close()

    stackB = ExitStack()

    def SBB(name, shape, dt):
        return stackB.enter_context(nc.sbuf_tensor(name, shape, dt))

    stackB2 = ExitStack()
    acc = stackB2.enter_context(nc.sbuf_tensor("b_acc", [128, 8 * 2048], BF16))
    pos1u = stackB2.enter_context(nc.sbuf_tensor("b_pos1u", [128, 16], mybir.dt.uint32))
    pos2u = stackB2.enter_context(nc.sbuf_tensor("b_pos2u", [128, 16], mybir.dt.uint32))
    gcol1 = stackB2.enter_context(nc.sbuf_tensor("b_gcol1", [128, 16], F32))
    gcol2 = stackB2.enter_context(nc.sbuf_tensor("b_gcol2", [128, 16], F32))

    xn2T = SBB("b_xn2T", [128, 8 * 1024], BF16)
    h_sb = SBB("b_h", [128, 32 * 1024], BF16)
    W1_sb = [SBB("b_w1_%d" % i, [128, 4096], BF16) for i in range(2)]
    W2_sb = [SBB("b_w2_%d" % i, [128, 4096], BF16) for i in range(2)]
    lg_sb = SBB("b_lg", [128, 8], F32)
    lgm_sb = SBB("b_lgm", [128, 8], F32)
    eq1_sb = SBB("b_eq1", [128, 64], F32)
    eq2_sb = SBB("b_eq2", [128, 64], F32)
    d12_sb = SBB("b_d12", [128, 8], F32)
    g1_sb = SBB("b_g1", [128, 8], F32)
    g2_sb = SBB("b_g2", [128, 8], F32)
    gt_sb = SBB("b_gt", [128, 8], F32)
    gatef_sb = SBB("b_gatef", [128, 64], F32)
    gst = SBB("b_gst", [128, 4], F32)
    g8_sb = SBB("b_g8", [8, 1024], F32)
    me8 = SBB("b_me8", [8, 1024], F32)
    zer8 = SBB("b_zer8", [8, 1024], F32)
    posi8 = SBB("b_posi8", [8, 1024], F32)
    posT_sb = SBB("b_posT", [128, 64], F32)
    pv1 = SBB("b_pv1", [128, 8], F32)
    pv2 = SBB("b_pv2", [128, 8], F32)
    pv3 = SBB("b_pv3", [128, 8], F32)
    pc1 = SBB("b_pc1", [128, 1], F32)
    idxu_sb = SBB("b_idxu", [128, 64], mybir.dt.uint32)
    idxg_sb = [SBB("b_idxg%d" % i, [128, 3], mybir.dt.uint32) for i in range(2)]
    xg_nat = [SBB("b_xg%d" % i, [128, 3 * 1024], BF16) for i in range(2)]
    xgT = SBB("b_xgT", [128, 8 * CAP], BF16)
    op_fm = SBB("b_opfm", [128, 8 * CAP], BF16)
    oprow = [SBB("b_oprow%d" % i, [128, 1024], BF16) for i in range(2)]

    psB = ExitStack()
    ps_fc = [psB.enter_context(nc.psum_tensor("ps_fc%d" % i, [128, 512], F32)) for i in range(2)]
    ps_out = [psB.enter_context(nc.psum_tensor("ps_out%d" % i, [128, 1024], F32)) for i in range(2)]
    ps_tb = psB.enter_context(nc.psum_tensor("ps_tb", [128, 1024], BF16))
    ps_tb2 = psB.enter_context(nc.psum_tensor("ps_tb2", [128, 1024], BF16))

    def fcp_slice_end(i1):
        if i1 < 0:
            return 0
        tth, r = divmod(i1, 72)
        e, hs = divmod(r, 8)
        base = 320 * tth
        if e == 0:
            return base + 8 * (hs + 1)
        return base + 64 + 4 * ((e - 1) * 8 + hs + 1)

    def fcp_e_end(tth, e):
        return 320 * tth + (64 if e >= 0 else 0) + 4 * 8 * max(e, 0)

    def gel_e_end(tth, e):
        return fcp_e_end(tth, e)

    with nc.Block() as bb:
        @bb.sync
        def _(sy):
            for tth in range(2):
                wg(sy, "fcp", 320 * tth)
                for ec in range(8):
                    sy.dma_start(out=xn2T[:, ec * 1024:ec * 1024 + 1024],
                                 in_=xn2_hbm[tth * 1024:tth * 1024 + 1024, ec * 128:ec * 128 + 128],
                                 transpose=True).then_inc(S["xn2t"], 16)
                wg(sy, "g8v", tth + 1)
                sy.dma_start(out=g8_hbm[:, tth * 1024:tth * 1024 + 1024],
                             in_=g8_sb[0:8, :]).then_inc(S["g8st"], 16)
                for e in range(1, NEXP):
                    slot = tth * 8 + (e - 1)
                    for gi in range(3):
                        og = slot * 3 + gi
                        wg(sy, "opv", og + 1)
                        sy.dma_start(out=out_perm[slot * CAP + gi * 128:slot * CAP + gi * 128 + 128, :],
                                     in_=oprow[og % 2][:]).then_inc(S["ops"], 16)

        @bb.gpsimd
        def _(g):
            r_cap = g.alloc_register()
            g.reg_mov(r_cap, CAP - 1)
            r_tok = g.alloc_register()
            g.reg_mov(r_tok, 1023)
            def gchain(tth, e):
                ei = e - 1
                slot = tth * 8 + ei
                for tc in range(8):
                    wg(g_eng[0], "ix", 8 * tth + tc + 1)
                    g_eng[0].indirect_dma_start(
                        out=idxl_hbm,
                        out_offset=bass.IndirectOffsetOnAxis(
                            idxu_sb[:, tc * 8 + ei:tc * 8 + ei + 1], axis=0),
                        in_=tokid_sb[:, tc:tc + 1], in_offset=None,
                        element_offset=slot * CAP,
                        bounds_check=r_cap, oob_is_err=False
                        ).then_inc(S["sca"], 16)
                wg(g_eng[0], "sca", 16 * (64 * tth + ei * 8 + 8))
                wg(g_eng[0], "xga", 16 * max(24 * tth + 3 * (e - 2), 0))
                g_eng[0].dma_start(out=idxg_sb[ei % 2][:],
                                   in_=idxl_hbm[slot * CAP:slot * CAP + CAP, :]
                                   .rearrange("(gi p) w -> p (gi w)", p=128),
                                   allow_slow_non_contiguous=True
                                   ).then_inc(S["ixl"], 16)
                wg(g_eng[0], "ixl", 16 * (8 * tth + e))
                wg(g_eng[0], "xgp", 64 * tth + 8 * max(ei - 1, 0))
                for gi in range(3):
                    g_eng[0].indirect_dma_start(
                        out=xg_nat[ei % 2][:, gi * 1024:gi * 1024 + 1024],
                        out_offset=None,
                        in_=xn2_hbm,
                        in_offset=bass.IndirectOffsetOnAxis(
                            idxg_sb[ei % 2][:, gi:gi + 1], axis=0),
                        element_offset=tth * 1024 * 1024,
                        bounds_check=r_tok, oob_is_err=False
                        ).then_inc(S["xga"], 16)

            g_eng = [g]
            for tth in range(2):
                for e in range(NEXP):
                    if e + 1 <= 8:
                        gchain(tth, e + 1)
                    for hs in range(8):
                        i1 = (tth * 9 + e) * 8 + hs
                        wg(g, "fcp", fcp_slice_end(i1 - 2))
                        g.dma_start(out=W1_sb[i1 % 2][:], in_=W1H[e, hs],
                                    max_dma_last_dim=8192).then_inc(S["w1"], 16)
                    for m in range(8):
                        i2 = (tth * 9 + e) * 8 + m
                        wg(g, "pse", i2 - 1)
                        g.dma_start(out=W2_sb[i2 % 2][:], in_=W2b[e, m]
                                    ).then_inc(S["w2"], 16)

        @bb.tensor
        def _(te):
            gfcp = [0]

            def fc_tile(i1pct, pslice, lhs_ap, rhs_ap):
                it = gfcp[0]
                wg(te, "gel", max(it - 1, 0))
                for ec in range(8):
                    mm = te.matmul(pslice, lhs_ap(ec), rhs_ap(ec),
                                   start=(ec == 0), stop=(ec == 7))
                mm.then_inc(S["fcp"], 1)
                gfcp[0] += 1
                return it

            for tth in range(2):
                wg(te, "xn2t", 16 * 8 * (tth + 1))
                wg(te, "gel", 320 * tth)
                wg(te, "pse", 72 * tth)
                for tc in range(8):
                    for ec in range(8):
                        mm = te.matmul(ps_fc[1][:, tc * 8:tc * 8 + 8],
                                       xn2T[:, ec * 1024 + tc * 128:ec * 1024 + tc * 128 + 128],
                                       wr_sb[:, ec * 8:ec * 8 + 8],
                                       start=(ec == 0), stop=(ec == 7))
                    mm.then_inc(S["rtm"], 1)
                for tc in range(8):
                    wg(te, "gv", 8 * tth + tc + 1)
                    te.transpose(ps_out[0][0:8, tc * 128:tc * 128 + 128],
                                 gatef_sb[:, tc * 8:tc * 8 + 8],
                                 idf32[:]).then_inc(S["gtt"], 1)
                wg(te, "g8v", tth + 1)
                wg(te, "pscn", tth + 1)
                for tc in range(8):
                    mm = te.transpose(ps_out[0][:, tc * 8:tc * 8 + 8],
                                      posi8[0:8, tc * 128:tc * 128 + 128],
                                      idf32[0:8, 0:8])
                mm.then_inc(S["ptt"], 1)
                wg(te, "gv1", 8 * (tth + 1))
                for e in range(NEXP):
                    ei = e - 1
                    if e >= 1:
                        wg(te, "xga", 16 * (24 * tth + 3 * e))
                        for ec in range(8):
                            xgpc = 64 * tth + 8 * ei + ec
                            wg(te, "xge", xgpc)
                            half = (ec % 2) * 512
                            for gi in range(3):
                                mm = te.transpose(ps_tb[:, half + gi * 128:half + gi * 128 + 128],
                                                  xg_nat[ei % 2][:, gi * 1024 + ec * 128:gi * 1024 + ec * 128 + 128],
                                                  idbf[:])
                            mm.then_inc(S["xgp"], 1)
                    for hs in range(8):
                        i1 = (tth * 9 + e) * 8 + hs
                        wg(te, "w1", 16 * (i1 + 1))
                        if e == 0 and hs == 0:
                            wg(te, "ptv", tth + 1)
                        if e >= 1 and hs == 0:
                            wg(te, "xge", 64 * tth + 8 * ei + 8)
                        for mf in range(4):
                            if e == 0:
                                for half in range(2):
                                    it = gfcp[0]
                                    pf = ps_fc[it % 2]
                                    wg(te, "gel", max(it - 1, 0))
                                    for ec in range(8):
                                        mm = te.matmul(pf[:, 0:512],
                                                       W1_sb[i1 % 2][:, ec * 512 + mf * 128:ec * 512 + mf * 128 + 128],
                                                       xn2T[:, ec * 1024 + half * 512:ec * 1024 + half * 512 + 512],
                                                       start=(ec == 0), stop=(ec == 7))
                                    mm.then_inc(S["fcp"], 1)
                                    gfcp[0] += 1
                            else:
                                it = gfcp[0]
                                pf = ps_fc[it % 2]
                                wg(te, "gel", max(it - 1, 0))
                                for ec in range(8):
                                    mm = te.matmul(pf[:, 0:CAP],
                                                   W1_sb[i1 % 2][:, ec * 512 + mf * 128:ec * 512 + mf * 128 + 128],
                                                   xgT[:, ec * CAP:ec * CAP + CAP],
                                                   start=(ec == 0), stop=(ec == 7))
                                mm.then_inc(S["fcp"], 1)
                                gfcp[0] += 1
                    wg(te, "gel", gel_e_end(tth, e))
                    for m in range(8):
                        i2 = (tth * 9 + e) * 8 + m
                        wg(te, "w2", 16 * (i2 + 1))
                        wg(te, "pse", max(i2 - 1, 0))
                        po = ps_out[i2 % 2]
                        if e == 0:
                            for hb in range(32):
                                for n in range(2):
                                    mm = te.matmul(po[:, n * 512:n * 512 + 512],
                                                   W2_sb[i2 % 2][:, hb * 128:hb * 128 + 128],
                                                   h_sb[:, hb * 1024 + n * 512:hb * 1024 + n * 512 + 512],
                                                   start=(hb == 0), stop=(hb == 31))
                        else:
                            for hb in range(32):
                                mm = te.matmul(po[:, 0:CAP],
                                               W2_sb[i2 % 2][:, hb * 128:hb * 128 + 128],
                                               h_sb[:, hb * 1024:hb * 1024 + CAP],
                                               start=(hb == 0), stop=(hb == 31))
                        mm.then_inc(S["prm"], 1)
                    if e >= 1:
                        for gi in range(3):
                            og = (tth * 8 + ei) * 3 + gi
                            wg(te, "pse", 72 * tth + 8 * (e + 1))
                            wg(te, "opv", og)
                            for m in range(8):
                                mm = te.transpose(ps_tb2[:, m * 128:m * 128 + 128],
                                                  op_fm[:, m * CAP + gi * 128:m * CAP + gi * 128 + 128],
                                                  idbf[:])
                            mm.then_inc(S["ott"], 1)

        @bb.scalar
        def _(sc):
            gfcp = [0]
            for tth in range(2):
                wg(sc, "gv1", 8 * (tth + 1))
                sc.activation(g1_sb[:], d12_sb[:], AF.Sigmoid)
                sc.activation(g2_sb[:], d12_sb[:], AF.Sigmoid,
                              scale=-1.0).then_inc(S["gs"], 1)
                for e in range(NEXP):
                    ei = e - 1
                    if e >= 1:
                        for ec in range(8):
                            xgpc = 64 * tth + 8 * ei + ec
                            wg(sc, "xgp", xgpc + 1)
                            if ec == 0:
                                wg(sc, "fcp", fcp_e_end(tth, e - 1))
                            half = (ec % 2) * 512
                            sc.copy(xgT[:, ec * CAP:ec * CAP + CAP],
                                    ps_tb[:, half:half + CAP]).then_inc(S["xge"], 1)
                    wg(sc, "prm", 8 * (tth * 9 + e))
                    for hs in range(8):
                        for mf in range(4):
                            hb = hs * 4 + mf
                            bcol = bfc_sb[:, e * 32 + hb:e * 32 + hb + 1]
                            if e == 0:
                                for half in range(2):
                                    it = gfcp[0]
                                    wg(sc, "fcp", it + 1)
                                    sc.activation(h_sb[:, hb * 1024 + half * 512:hb * 1024 + half * 512 + 512],
                                                  ps_fc[it % 2][:, 0:512], AF.Gelu_apprx_tanh,
                                                  bias=bcol).then_inc(S["gel"], 1)
                                    gfcp[0] += 1
                            else:
                                it = gfcp[0]
                                wg(sc, "fcp", it + 1)
                                sc.activation(h_sb[:, hb * 1024:hb * 1024 + CAP],
                                              ps_fc[it % 2][:, 0:CAP], AF.Gelu_apprx_tanh,
                                              bias=bcol).then_inc(S["gel"], 1)
                                gfcp[0] += 1
                    if e >= 1:
                        for m in range(8):
                            i2 = (tth * 9 + e) * 8 + m
                            wg(sc, "prm", i2 + 1)
                            if m == 0:
                                wg(sc, "ott", 3 * max(tth * 8 + ei, 0))
                            sc.activation(op_fm[:, m * CAP:m * CAP + CAP],
                                          ps_out[i2 % 2][:, 0:CAP], AF.Identity,
                                          bias=bproj_sb[:, e * 8 + m:e * 8 + m + 1]
                                          ).then_inc(S["pse"], 1)
                        for gi in range(3):
                            og = (tth * 8 + ei) * 3 + gi
                            wg(sc, "ott", og + 1)
                            wg(sc, "ops", 16 * max(og - 1, 0))
                            sc.copy(oprow[og % 2][:], ps_tb2[:]).then_inc(S["opv"], 1)

        @bb.vector
        def _(v):
            v.memset(zer8[:], 0.0)
            for tth in range(2):
                wg(v, "rtm", 8 * (tth + 1))
                for tc in range(8):
                    vmark(v.tensor_copy(lg_sb[:], ps_fc[1][:, tc * 8:tc * 8 + 8]))
                    vwait(v)
                    vmark(v.reduce_max(gst[:, 0:1], lg_sb[:], axis=AX))
                    vwait(v)
                    vmark(v.tensor_scalar(eq1_sb[:, tc * 8:tc * 8 + 8], lg_sb[:],
                                          gst[:, 0:1], 1.0, op0=OP.is_ge, op1=OP.mult))
                    vwait(v)
                    vmark(v.scalar_tensor_tensor(lgm_sb[:], eq1_sb[:, tc * 8:tc * 8 + 8],
                                                 -1.0e9, lg_sb[:], op0=OP.mult, op1=OP.add))
                    vwait(v)
                    vmark(v.reduce_max(gst[:, 1:2], lgm_sb[:], axis=AX))
                    vwait(v)
                    vmark(v.tensor_scalar(eq2_sb[:, tc * 8:tc * 8 + 8], lgm_sb[:],
                                          gst[:, 1:2], 1.0, op0=OP.is_ge, op1=OP.mult))
                    vwait(v)
                    v.tensor_tensor(d12_sb[:, tc:tc + 1], gst[:, 0:1], gst[:, 1:2],
                                    op=OP.subtract).then_inc(S["gv1"], 1)
                wg(v, "gs", tth + 1)
                wg(v, "gtt", 8 * tth)
                for tc in range(8):
                    vmark(v.tensor_scalar(gt_sb[:], eq1_sb[:, tc * 8:tc * 8 + 8],
                                          g1_sb[:, tc:tc + 1], 1.0, op0=OP.mult, op1=OP.mult))
                    vwait(v)
                    v.scalar_tensor_tensor(gatef_sb[:, tc * 8:tc * 8 + 8],
                                           eq2_sb[:, tc * 8:tc * 8 + 8],
                                           g2_sb[:, tc:tc + 1], gt_sb[:],
                                           op0=OP.mult, op1=OP.add).then_inc(S["gv"], 1)
                wg(v, "gtt", 8 * (tth + 1))
                wg(v, "g8st", 16 * tth)
                v.tensor_copy(g8_sb[0:8, :], ps_out[0][0:8, 0:1024]).then_inc(S["g8v"], 1)
                v.wait_ge(S["g8v"], tth + 1)
                vmark(v.tensor_scalar(me8[:], g8_sb[0:8, :], 0.0, 1.0,
                                      op0=OP.is_gt, op1=OP.mult))
                vwait(v)
                vmark(v.tensor_tensor_scan(posi8[:], me8[:], zer8[:], 0.0,
                                           op0=OP.add, op1=OP.add))
                vwait(v)
                v.tensor_tensor(posi8[:], posi8[:], me8[:],
                                op=OP.subtract).then_inc(S["pscn"], 1)
                wg(v, "ptt", tth + 1)
                wg(v, "sca", 16 * 64 * tth)
                v.tensor_copy(posT_sb[:], ps_out[0][:, 0:64]).then_inc(S["ptv"], 1)
                v.wait_ge(S["ptv"], tth + 1)
                for tc in range(8):
                    tg = tth * 8 + tc
                    rank = posT_sb[:, tc * 8:tc * 8 + 8]
                    vmark(v.tensor_scalar(pv3[:], gatef_sb[:, tc * 8:tc * 8 + 8],
                                          0.0, 1.0, op0=OP.is_gt, op1=OP.mult))
                    vwait(v)
                    vmark(v.tensor_scalar(pv2[:], pv3[:], 1.0, -1.0e9,
                                          op0=OP.subtract, op1=OP.mult))
                    vwait(v)
                    vmark(v.tensor_tensor(pv1[:], rank, pv2[:], op=OP.add))
                    vwait(v)
                    vmark(v.tensor_copy(idxu_sb[:, tc * 8:tc * 8 + 8], pv1[:]))
                    vwait(v)
                    vmark(v.tensor_tensor(pv1[:], rank, posbase_sb[:, tth * 8:tth * 8 + 8],
                                          op=OP.add))
                    vwait(v)
                    vmark(v.tensor_scalar(pv2[:], rank, CAP - 0.5, 1.0,
                                          op0=OP.is_gt, op1=OP.mult))
                    vwait(v)
                    vmark(v.tensor_scalar(pv3[:], pv2[:], -1.0, 1.0,
                                          op0=OP.mult, op1=OP.add))
                    vwait(v)
                    vmark(v.tensor_tensor(pv1[:], pv1[:], pv3[:], op=OP.mult))
                    vwait(v)
                    vmark(v.tensor_scalar(pv2[:], pv2[:], float(ZROW), 1.0,
                                          op0=OP.mult, op1=OP.mult))
                    vwait(v)
                    vmark(v.tensor_tensor(pv1[:], pv1[:], pv2[:], op=OP.add))
                    vwait(v)
                    vmark(v.tensor_tensor(pv2[:], eq1_sb[:, tc * 8:tc * 8 + 8], pv1[:],
                                          op=OP.mult))
                    vwait(v)
                    vmark(v.reduce_sum(pc1[:], pv2[:], axis=AX))
                    vwait(v)
                    vmark(v.tensor_scalar(pc1[:], pc1[:], float(ZROW), 1.0,
                                          op0=OP.min, op1=OP.mult))
                    vwait(v)
                    vmark(v.tensor_copy(pos1u[:, tg:tg + 1], pc1[:]))
                    vwait(v)
                    vmark(v.tensor_tensor(pv2[:], eq2_sb[:, tc * 8:tc * 8 + 8], pv1[:],
                                          op=OP.mult))
                    vwait(v)
                    vmark(v.reduce_sum(pc1[:], pv2[:], axis=AX))
                    vwait(v)
                    vmark(v.tensor_scalar(pc1[:], pc1[:], float(ZROW), 1.0,
                                          op0=OP.min, op1=OP.mult))
                    vwait(v)
                    vmark(v.tensor_copy(pos2u[:, tg:tg + 1], pc1[:]))
                    vwait(v)
                    vmark(v.tensor_copy(gcol1[:, tg:tg + 1], g1_sb[:, tc:tc + 1]))
                    vwait(v)
                    v.tensor_copy(gcol2[:, tg:tg + 1], g2_sb[:, tc:tc + 1]
                                  ).then_inc(S["ix"], 1)
                for m in range(8):
                    i2 = (tth * 9 + 0) * 8 + m
                    wg(v, "prm", i2 + 1)
                    aslice = acc[:, m * 2048 + tth * 1024:m * 2048 + tth * 1024 + 1024]
                    v.tensor_scalar(aslice, ps_out[i2 % 2][:, 0:1024],
                                    bproj_sb[:, 0 * 8 + m:0 * 8 + m + 1], 1.0,
                                    op0=OP.add, op1=OP.mult).then_inc(S["pse"], 1)

    stackB.close()
    psB.close()

    stackC = ExitStack()
    x1f = [stackC.enter_context(nc.sbuf_tensor("c_x1f%d" % i, [128, 1024], F32)) for i in range(2)]
    y_sb = [stackC.enter_context(nc.sbuf_tensor("c_y%d" % i, [128, 1024], F32)) for i in range(2)]
    c1_sb = [stackC.enter_context(nc.sbuf_tensor("c_c1_%d" % i, [128, 1024], BF16)) for i in range(2)]
    c2_sb = [stackC.enter_context(nc.sbuf_tensor("c_c2_%d" % i, [128, 1024], BF16)) for i in range(2)]
    tmpc = stackC.enter_context(nc.sbuf_tensor("c_tmpc", [128, 1024], F32))
    psC = ExitStack()
    ps_y = [psC.enter_context(nc.psum_tensor("ps_y%d" % i, [128, 1024], BF16)) for i in range(2)]

    with nc.Block() as bc:
        @bc.sync
        def _(sy):
            for tc in range(16):
                wg(sy, "cy", max(tc - 1, 0))
                sy.dma_start(out=x1f[tc % 2][:],
                             in_=x1_hbm[tc * 128:tc * 128 + 128, :]).then_inc(S["cx1"], 16)
                wg(sy, "cy", tc + 1)
                sy.dma_start(out=y_out[tc * 128:tc * 128 + 128, :],
                             in_=y_sb[tc % 2][:]).then_inc(S["y"], 16)
            sy.wait_ge(S["y"], 16 * 16)

        @bc.gpsimd
        def _(g):
            r_zr = g.alloc_register()
            g.reg_mov(r_zr, ZROW)
            g.wait_ge(S["ops"], 16 * 48)
            g.wait_ge(S["ix"], 16)
            for tc in range(16):
                wg(g, "cy", max(tc - 1, 0))
                for pu, dst in ((pos1u, c1_sb), (pos2u, c2_sb)):
                    g.indirect_dma_start(
                        out=dst[tc % 2][:], out_offset=None,
                        in_=out_perm,
                        in_offset=bass.IndirectOffsetOnAxis(pu[:, tc:tc + 1], axis=0),
                        bounds_check=r_zr, oob_is_err=False
                        ).then_inc(S["cg"], 16)

        @bc.tensor
        def _(te):
            wg(te, "pse", 144)
            for tc in range(16):
                wg(te, "cy", max(tc - 1, 0))
                for m in range(8):
                    mm = te.transpose(ps_y[tc % 2][:, m * 128:m * 128 + 128],
                                      acc[:, m * 2048 + tc * 128:m * 2048 + tc * 128 + 128],
                                      idbf[:])
                mm.then_inc(S["ct"], 1)

        @bc.vector
        def _(v):
            for tc in range(16):
                wg(v, "cx1", 16 * (tc + 1))
                wg(v, "ct", tc + 1)
                wg(v, "cg", 32 * (tc + 1))
                wg(v, "y", 16 * max(tc - 1, 0))
                ys = y_sb[tc % 2][:]
                v.tensor_tensor(ys, x1f[tc % 2][:], ps_y[tc % 2][:, 0:1024],
                                op=OP.add)
                v.tensor_scalar(tmpc[:], c1_sb[tc % 2][:], gcol1[:, tc:tc + 1], 1.0,
                                op0=OP.mult, op1=OP.mult)
                v.tensor_tensor(ys, ys, tmpc[:], op=OP.add)
                v.tensor_scalar(tmpc[:], c2_sb[tc % 2][:], gcol2[:, tc:tc + 1], 1.0,
                                op0=OP.mult, op1=OP.mult)
                v.tensor_tensor(ys, ys, tmpc[:], op=OP.add).then_inc(S["cy"], 1)

    stackC.close()
    psC.close()
    return nc


_CACHED = {}


def _get_nc():
    if "nc" not in _CACHED:
        _CACHED["nc"] = build_program(debug=False)
    return _CACHED["nc"]


def _host_inputs(L0_attn_x, L0_attn_w_in, L0_attn_w_out, L0_ffn_Wfc_s, L0_ffn_bfc_s,
                 L0_ffn_Wproj_s, L0_ffn_bproj_s, L0_ffn_Wr, L0_ffn_Wfc, L0_ffn_bfc,
                 L0_ffn_Wproj, L0_ffn_bproj):
    f32 = np.float32
    x = np.ascontiguousarray(np.asarray(L0_attn_x, f32).reshape(16384, 1024))
    w_inT = np.ascontiguousarray(np.asarray(L0_attn_w_in, f32).T)
    w_outT = np.ascontiguousarray(np.asarray(L0_attn_w_out, f32).T)
    Wfc_all = np.concatenate([np.asarray(L0_ffn_Wfc_s, f32)[None],
                              np.asarray(L0_ffn_Wfc, f32)], 0)
    Wproj_all = np.concatenate([np.asarray(L0_ffn_Wproj_s, f32)[None],
                                np.asarray(L0_ffn_Wproj, f32)], 0)
    bfc_all = np.concatenate([np.asarray(L0_ffn_bfc_s, f32)[None],
                              np.asarray(L0_ffn_bfc, f32)], 0)
    bproj_all = np.concatenate([np.asarray(L0_ffn_bproj_s, f32)[None],
                                np.asarray(L0_ffn_bproj, f32)], 0)
    W1H = np.ascontiguousarray(
        Wfc_all.reshape(NEXP, 8, 128, 8, 512).transpose(0, 3, 2, 1, 4).reshape(NEXP, 8, 128, 4096))
    W2H = np.ascontiguousarray(
        Wproj_all.reshape(NEXP, 32, 128, 8, 128).transpose(0, 3, 2, 1, 4).reshape(NEXP, 8, 128, 4096))
    bfcB = np.ascontiguousarray(bfc_all.reshape(NEXP, 32, 128).transpose(0, 2, 1))
    bprojB = np.ascontiguousarray(bproj_all.reshape(NEXP, 8, 128).transpose(0, 2, 1))
    WrH = np.ascontiguousarray(
        np.asarray(L0_ffn_Wr, f32).reshape(8, 128, 8).transpose(1, 0, 2).reshape(128, 64))
    kjb = np.ascontiguousarray(np.broadcast_to(np.arange(1024, dtype=f32), (128, 1024)))
    qi = np.arange(512, dtype=np.int64)[:, None]
    cc = np.arange(512, dtype=np.int64)[None, :]
    mprev = np.where(qi <= cc, 0.0, NEG).astype(f32)
    mcur = np.where(cc <= qi, 0.0, NEG).astype(f32)
    pp_ = np.arange(128)[:, None]
    cc_ = np.arange(128)[None, :]
    tprev = np.where(pp_ <= cc_, 0.0, NEG).astype(ml_dtypes.bfloat16)
    tcur = np.where(cc_ <= pp_, 0.0, NEG).astype(ml_dtypes.bfloat16)
    maskP0_z = np.zeros((128, 512), ml_dtypes.bfloat16)
    maskP0_n = np.full((128, 512), NEG, ml_dtypes.bfloat16)
    idbf = np.eye(128, dtype=ml_dtypes.bfloat16)
    idf32 = np.eye(128, dtype=f32)
    consts = np.zeros((128, 4), f32)
    consts[:, 0] = 1e-5

    tokid = np.ascontiguousarray(
        (np.arange(8)[None, :] * 128 + np.arange(128)[:, None]).astype(np.uint32))
    posbase = np.ascontiguousarray(np.broadcast_to(
        (np.arange(16, dtype=f32) * 384.0)[None, :], (128, 16)))
    zci = np.zeros((16 * 384, 1), np.uint32)
    z1k = np.zeros((1, 1024), ml_dtypes.bfloat16)
    shared = dict(w_inT=w_inT, w_outT=w_outT, W1H=W1H, W2H=W2H, bfcB=bfcB,
                  bprojB=bprojB, WrH=WrH, kjb=kjb, tprev=tprev, tcur=tcur,
                  idbf=idbf, idf32=idf32, consts=consts,
                  tokid=tokid, posbase=posbase, zci=zci, z1k=z1k)
    zeros_halo = np.zeros((512, 1024), f32)
    in_maps = []
    for c in range(8):
        m = dict(shared)
        m["x_own"] = np.ascontiguousarray(x[c * T:(c + 1) * T])
        m["x_halo"] = zeros_halo if c in (0, 4) else np.ascontiguousarray(x[c * T - 512:c * T])
        m["maskP0"] = maskP0_n if c in (0, 4) else maskP0_z
        in_maps.append(m)
    return in_maps


def kernel(**inputs):
    in_maps = _host_inputs(**inputs)
    nc = _get_nc()
    res = run_bass_kernel_spmd(nc, in_maps, core_ids=list(range(8)))
    y = np.concatenate([np.asarray(res.results[c]["y"]) for c in range(8)], axis=0)
    return y.reshape(2, 8192, 1024).astype(np.float32, copy=False)


# version 15
# speedup vs baseline: 1.1116x; 1.0283x over previous
import numpy as np
import ml_dtypes
from contextlib import ExitStack
import concourse.bass as bass
import concourse.mybir as mybir
from concourse.bass_utils import run_bass_kernel_spmd
from concourse.alu_op_type import AluOpType as OP

F32 = mybir.dt.float32
BF16 = mybir.dt.bfloat16
AF = mybir.ActivationFunctionType
AX = mybir.AxisListType.X

T = 2048
D = 1024
NEXP = 9
H = 16
NITER = 5
CAP = 384
ZROW = 16 * CAP
NEG = -1.0e30
SLOPES = [float(np.exp2(np.float32(-(h + 1) * 8.0 / 16.0))) for h in range(H)]


def qke_cum(j):
    if j < 0:
        return 0
    return 8 + 12 * j


def build_program(debug=False, debug2=False):
    nc = bass.Bass("TRN2", target_bir_lowering=False, debug=False,
                   detect_race_conditions=False)

    x_own = nc.dram_tensor("x_own", [T, D], F32, kind="ExternalInput").ap()
    x_halo = nc.dram_tensor("x_halo", [512, D], F32, kind="ExternalInput").ap()
    w_inT = nc.dram_tensor("w_inT", [D, 3 * D], F32, kind="ExternalInput").ap()
    w_outT = nc.dram_tensor("w_outT", [D, D], F32, kind="ExternalInput").ap()
    W1H = nc.dram_tensor("W1H", [NEXP, 8, 128, 4096], F32, kind="ExternalInput").ap()
    W2H = nc.dram_tensor("W2H", [NEXP, 8, 128, 4096], F32, kind="ExternalInput").ap()
    tokid_in = nc.dram_tensor("tokid", [128, 8], mybir.dt.uint32, kind="ExternalInput").ap()
    posbase_in = nc.dram_tensor("posbase", [128, 16], F32, kind="ExternalInput").ap()
    zci_in = nc.dram_tensor("zci", [16 * CAP, 1], mybir.dt.uint32, kind="ExternalInput").ap()
    z1k_in = nc.dram_tensor("z1k", [1, 1024], BF16, kind="ExternalInput").ap()
    bfcB = nc.dram_tensor("bfcB", [NEXP, 128, 32], F32, kind="ExternalInput").ap()
    bprojB = nc.dram_tensor("bprojB", [NEXP, 128, 8], F32, kind="ExternalInput").ap()
    WrH = nc.dram_tensor("WrH", [128, 64], F32, kind="ExternalInput").ap()
    kjb_in = nc.dram_tensor("kjb", [128, 1024], F32, kind="ExternalInput").ap()
    tprev_in = nc.dram_tensor("tprev", [128, 128], BF16, kind="ExternalInput").ap()
    tcur_in = nc.dram_tensor("tcur", [128, 128], BF16, kind="ExternalInput").ap()
    maskP0_in = nc.dram_tensor("maskP0", [128, 512], BF16, kind="ExternalInput").ap()
    idbf_in = nc.dram_tensor("idbf", [128, 128], BF16, kind="ExternalInput").ap()
    idf_in = nc.dram_tensor("idf32", [128, 128], F32, kind="ExternalInput").ap()
    consts_in = nc.dram_tensor("consts", [128, 4], F32, kind="ExternalInput").ap()

    y_out = nc.dram_tensor("y", [T, D], F32, kind="ExternalOutput").ap()
    skind = "ExternalOutput" if debug else "Internal"
    x1_hbm = nc.dram_tensor("x1_hbm", [T, D], F32, kind=skind).ap()
    xn2_hbm = nc.dram_tensor("xn2_hbm", [T, D], BF16, kind=skind).ap()
    g8_hbm = nc.dram_tensor("g8_hbm", [8, T], F32, kind=skind).ap()
    idxl_hbm = nc.dram_tensor("idxl_hbm", [16 * CAP, 1], mybir.dt.uint32, kind="Internal").ap()
    W1b = nc.dram_tensor("W1b", [NEXP, 8, 128, 4096], BF16, kind="Internal").ap()
    W2b = nc.dram_tensor("W2b", [NEXP, 8, 128, 4096], BF16, kind="Internal").ap()
    out_perm = nc.dram_tensor("out_perm", [16 * CAP + 1, 1024], BF16, kind="Internal").ap()

    SBC = nc.alloc_sbuf_tensor
    idbf = SBC("c_idbf", [128, 128], BF16)
    idf32 = SBC("c_idf32", [128, 128], F32)
    consts = SBC("c_consts", [128, 4], F32)
    wr_sb = SBC("c_wr", [128, 64], BF16)
    bfc_sb = SBC("c_bfc", [128, NEXP * 32], F32)
    bproj_sb = SBC("c_bproj", [128, NEXP * 8], F32)
    tokid_sb = SBC("c_tokid", [128, 8], mybir.dt.uint32)
    posbase_sb = SBC("c_posbase", [128, 16], F32)

    S = {}
    for nm in ["pl", "plg", "x", "sq1", "va1", "std1", "st1", "xn1", "tpp", "tpe",
               "qkm", "qke", "sm1", "sc", "t2", "p", "pn", "tp", "pte", "pv",
               "oe", "op", "x1", "sq2", "va2", "std2", "st2", "xn2",
               "stx1", "stxn2",
               "xn2t", "w1", "w2", "fcp", "gel", "prm", "pse",
               "rtm", "gv1", "gs", "gv", "gtt", "g8v", "g8st",
               "pscn", "ptv", "ptt", "ix", "sca", "ixl", "xga", "xgp", "xge", "pwc", "pww",
               "ott", "opv", "ops", "cg",
               "cx1", "ct", "cy", "y", "vv"]:
        S[nm] = nc.alloc_semaphore("s_" + nm)

    def wg(eng, nm, val):
        if val > 0:
            eng.wait_ge(S[nm], val)

    vvc = [0]

    def vmark(inst):
        inst.then_inc(S["vv"], 1)
        vvc[0] += 1

    def vwait(v):
        if vvc[0] > 0:
            v.wait_ge(S["vv"], vvc[0])

    stackA = ExitStack()

    def SBA(name, shape, dt):
        return stackA.enter_context(nc.sbuf_tensor(name, shape, dt))

    winT = SBA("a_winT", [128, 8 * 3072], BF16)
    woutT = SBA("a_woutT", [128, 8 * 1024], BF16)
    kjb = SBA("a_kjb", [128, 1024], F32)
    tprev = SBA("a_tprev", [128, 128], BF16)
    tcur = SBA("a_tcur", [128, 128], BF16)
    maskP0 = SBA("a_maskP0", [128, 512], BF16)
    x_sb = SBA("a_x", [128, 4096], F32)
    xn1 = SBA("a_xn1", [128, 4096], BF16)
    xn1T = SBA("a_xn1T", [128, 8 * 512], BF16)
    q_sb = SBA("a_q", [128, 8 * 512], BF16)
    k_ring = SBA("a_kring", [128, 8 * 1024], BF16)
    v_ring = SBA("a_vring", [128, 8 * 1024], BF16)
    t_sb = SBA("a_t", [128, 3 * 640], F32)
    p_sb = SBA("a_p", [128, 3 * 640], BF16)
    pT_sb = SBA("a_pT", [128, 2 * 2560], BF16)
    o_sb = SBA("a_o", [128, 8 * 512], BF16)
    x1t = SBA("a_x1t", [128, 2 * 1024], F32)
    xn2t = SBA("a_xn2t", [128, 2 * 1024], BF16)
    sq_scr = SBA("a_sqscr", [128, 1024], F32)
    pwb = [SBA("a_pwb%d" % i, [128, 4096], BF16) for i in range(2)]
    st = SBA("a_stats", [128, 48], F32)

    psA = ExitStack()
    ps_a = [psA.enter_context(nc.psum_tensor("ps_a%d" % i, [128, 1024], F32)) for i in range(2)]
    ps_pt = [psA.enter_context(nc.psum_tensor("ps_pt%d" % i, [128, 1024], BF16)) for i in range(2)]
    ps_o = [psA.enter_context(nc.psum_tensor("ps_o%d" % i, [128, 512], F32)) for i in range(2)]

    with nc.Block() as bp:
        @bp.sync
        def _(sy):
            for dst, src in [(idbf, idbf_in), (idf32, idf_in), (consts, consts_in),
                             (kjb, kjb_in), (tprev, tprev_in), (tcur, tcur_in),
                             (maskP0, maskP0_in)]:
                sy.dma_start(out=dst[:], in_=src).then_inc(S["pl"], 16)
            sy.dma_start(out=bfc_sb[:], in_=bfcB.rearrange("e p c -> p e c")).then_inc(S["pl"], 16)
            sy.dma_start(out=bproj_sb[:], in_=bprojB.rearrange("e p c -> p e c")).then_inc(S["pl"], 16)
            sy.dma_start(out=tokid_sb[:], in_=tokid_in).then_inc(S["pl"], 16)
            sy.dma_start(out=posbase_sb[:], in_=posbase_in).then_inc(S["pl"], 16)
            sy.wait_ge(S["pl"], 11 * 16)

        @bp.gpsimd
        def _(g):
            for ec in range(8):
                g.dma_start(out=winT[:, ec * 3072:(ec + 1) * 3072],
                            in_=w_inT[ec * 128:(ec + 1) * 128, :],
                            max_dma_last_dim=8192).then_inc(S["plg"], 16)
                g.dma_start(out=woutT[:, ec * 1024:(ec + 1) * 1024],
                            in_=w_outT[ec * 128:(ec + 1) * 128, :],
                            max_dma_last_dim=8192).then_inc(S["plg"], 16)
            g.dma_start(out=wr_sb[:], in_=WrH, max_dma_last_dim=8192).then_inc(S["plg"], 16)
            g.dma_start(out=idxl_hbm, in_=zci_in).then_inc(S["plg"], 16)
            g.dma_start(out=out_perm[ZROW:ZROW + 1, :], in_=z1k_in).then_inc(S["plg"], 16)
            g.wait_ge(S["plg"], 19 * 16)

    nc.const_aps.register(F32, 1e-5, consts[:, 0:1])
    PL = 11 * 16
    PLG = 19 * 16

    def ln_scalar_sq(sc, src, base, tc, sem_sq, cnt_sq, sem_va, cnt_va_war):
        wg(sc, sem_va, cnt_va_war)
        c = base + (tc % 2) * 8
        sc.activation(sq_scr[:], src, AF.Square,
                      accum_out=st[:, c + 1:c + 2]).then_inc(S[sem_sq], 1)

    def ln_vector_stats(v, src, base, tc, sem_sq, cnt_sq, sem_va, sem_std, cnt_std,
                        sem_st, sem_xn, cnt_xn_war):
        c = base + (tc % 2) * 8
        wg(v, sem_xn, cnt_xn_war)
        vmark(v.reduce_sum(st[:, c + 0:c + 1], src, axis=AX))
        wg(v, sem_sq, cnt_sq)
        vwait(v)
        vmark(v.tensor_scalar_mul(st[:, c + 2:c + 3], st[:, c + 0:c + 1], 1.0 / 1024.0))
        v.tensor_scalar_mul(st[:, c + 3:c + 4], st[:, c + 1:c + 2], 1.0 / 1024.0)
        vwait(v)
        vmark(v.tensor_scalar(st[:, c + 4:c + 5], st[:, c + 2:c + 3],
                              st[:, c + 2:c + 3], 1.0, op0=OP.mult, op1=OP.mult))
        vwait(v)
        v.tensor_tensor(st[:, c + 3:c + 4], st[:, c + 3:c + 4],
                        st[:, c + 4:c + 5], op=OP.subtract).then_inc(S[sem_va], 1)
        wg(v, sem_std, cnt_std)
        vmark(v.reciprocal(st[:, c + 5:c + 6], st[:, c + 4:c + 5]))
        vwait(v)
        v.tensor_scalar(st[:, c + 6:c + 7], st[:, c + 2:c + 3],
                        st[:, c + 5:c + 6], -1.0, op0=OP.mult, op1=OP.mult
                        ).then_inc(S[sem_st], 1)

    def ln_scalar_sqrt(sc, base, tc, sem_va, cnt_va, sem_std, sem_st, cnt_st_war):
        c = base + (tc % 2) * 8
        wg(sc, sem_va, cnt_va)
        wg(sc, sem_st, cnt_st_war)
        sc.activation(st[:, c + 4:c + 5], st[:, c + 3:c + 4], AF.Sqrt,
                      bias=1e-5).then_inc(S[sem_std], 1)

    def ln_scalar_apply(sc, dst, src, base, tc, sem_st, cnt_st, sem_xn):
        c = base + (tc % 2) * 8
        wg(sc, sem_st, cnt_st)
        sc.activation(dst, src, AF.Identity, bias=st[:, c + 6:c + 7],
                      scale=st[:, c + 5:c + 6]).then_inc(S[sem_xn], 1)

    with nc.Block() as ba:
        @ba.gpsimd
        def _(g):
            slices = [(s_t, d_t, e, hs) for s_t, d_t in ((W2H, W2b),)
                      for e in range(NEXP) for hs in range(8)]
            for i in range(len(slices) + 1):
                if i < len(slices):
                    s_t, d_t, e, hs = slices[i]
                    wg(g, "pww", 16 * max(i - 1, 0))
                    g.dma_start(out=pwb[i % 2][:], in_=s_t[e, hs],
                                max_dma_last_dim=8192).then_inc(S["pwc"], 16)
                if i >= 1:
                    s_t, d_t, e, hs = slices[i - 1]
                    g.wait_ge(S["pwc"], 16 * i)
                    g.dma_start(out=d_t[e, hs], in_=pwb[(i - 1) % 2][:]
                                ).then_inc(S["pww"], 16)
            g.wait_ge(S["pww"], 16 * 72)

        @ba.sync
        def _(sy):
            for j in range(NITER):
                wg(sy, "xn1", 4 * j)
                wg(sy, "st1", 4 * j)
                wg(sy, "sq1", 4 * j)
                wg(sy, "x1", 4 * (j - 1))
                src = x_halo if j == 0 else x_own[(j - 1) * 512:j * 512, :]
                sy.dma_start(out=x_sb[:],
                             in_=src.rearrange("(tc p) f -> p tc f", p=128)
                             ).then_inc(S["x"], 16)
                if j == 0:
                    continue
                b = j - 1
                for tc in range(4):
                    wg(sy, "x1", 4 * b + tc + 1)
                    r0 = b * 512 + tc * 128
                    sy.dma_start(out=x1_hbm[r0:r0 + 128, :],
                                 in_=x1t[:, (tc % 2) * 1024:(tc % 2) * 1024 + 1024]
                                 ).then_inc(S["stx1"], 16)
                    wg(sy, "xn2", 4 * b + tc + 1)
                    sy.dma_start(out=xn2_hbm[r0:r0 + 128, :],
                                 in_=xn2t[:, (tc % 2) * 1024:(tc % 2) * 1024 + 1024]
                                 ).then_inc(S["stxn2"], 16)
            sy.wait_ge(S["stx1"], 16 * 16)
            sy.wait_ge(S["stxn2"], 16 * 16)

        @ba.tensor
        def _(te):
            te.wait_ge(S["pl"], PL)
            te.wait_ge(S["plg"], PLG)
            for j in range(NITER):
                b = j - 1
                slot = j % 2
                wg(te, "xn1", 4 * (j + 1))
                for ec in range(8):
                    wg(te, "tpe", 8 * j + max(ec - 1, 0))
                    if ec <= 1:
                        wg(te, "pte", 64 * max(b, 0))
                    for tc in range(4):
                        mm = te.transpose(ps_pt[ec % 2][:, tc * 128:tc * 128 + 128],
                                          xn1[:, tc * 1024 + ec * 128:tc * 1024 + ec * 128 + 128],
                                          idbf[:])
                    mm.then_inc(S["tpp"], 1)
                wg(te, "tpe", 8 * (j + 1))
                wg(te, "x1", 4 * (j - 1))
                nt = 0
                if j >= 1:
                    for fcp in range(4):
                        wg(te, "qke", qke_cum(j - 1) + max(nt - 1, 0))
                        pa = ps_a[nt % 2]
                        for half in range(2):
                            fc = fcp * 2 + half
                            for ec in range(8):
                                mm = te.matmul(pa[:, half * 512:half * 512 + 512],
                                               winT[:, ec * 3072 + fc * 128:ec * 3072 + fc * 128 + 128],
                                               xn1T[:, ec * 512:ec * 512 + 512],
                                               start=(ec == 0), stop=(ec == 7))
                        mm.then_inc(S["qkm"], 1)
                        nt += 1
                for fcp in range(4):
                    wg(te, "qke", qke_cum(j - 1) + max(nt - 1, 0))
                    pa = ps_a[nt % 2]
                    for half in range(2):
                        fc = fcp * 2 + half
                        for ec in range(8):
                            mm = te.matmul(pa[:, half * 512:half * 512 + 512],
                                           winT[:, ec * 3072 + 1024 + fc * 128:ec * 3072 + 1024 + fc * 128 + 128],
                                           xn1T[:, ec * 512:ec * 512 + 512],
                                           start=(ec == 0), stop=(ec == 7))
                    mm.then_inc(S["qkm"], 1)
                    nt += 1
                for tc in range(4):
                    wg(te, "qke", qke_cum(j - 1) + max(nt - 1, 0))
                    pa = ps_a[nt % 2]
                    for n in range(2):
                        for ec in range(8):
                            mm = te.matmul(pa[:, n * 512:n * 512 + 512],
                                           xn1T[:, ec * 512 + tc * 128:ec * 512 + tc * 128 + 128],
                                           winT[:, ec * 3072 + 2048 + n * 512:ec * 3072 + 2048 + n * 512 + 512],
                                           start=(ec == 0), stop=(ec == 7))
                    mm.then_inc(S["qkm"], 1)
                    nt += 1

                if j == 0:
                    continue
                wg(te, "qke", qke_cum(j))
                prev, cur = (j - 1) % 2, j % 2

                def emit_scores(s):
                    h, qc = s // 4, s % 4
                    hp, hh = h // 2, (h % 2) * 64
                    n1 = 512 - qc * 128
                    wg(te, "sm1", 64 * b + max(s - 1, 0))
                    pa = ps_a[s % 2]
                    q_ap = q_sb[hh:hh + 64, hp * 512 + qc * 128:hp * 512 + qc * 128 + 128]
                    te.matmul(pa[:, 0:n1], q_ap,
                              k_ring[hh:hh + 64, hp * 1024 + prev * 512 + qc * 128:hp * 1024 + prev * 512 + 512],
                              start=True, stop=True)
                    if qc > 0:
                        te.matmul(pa[:, n1:512], q_ap,
                                  k_ring[hh:hh + 64, hp * 1024 + cur * 512:hp * 1024 + cur * 512 + qc * 128],
                                  start=True, stop=True)
                    te.matmul(pa[:, 512:640], q_ap,
                              k_ring[hh:hh + 64, hp * 1024 + cur * 512 + qc * 128:hp * 1024 + cur * 512 + qc * 128 + 128],
                              start=True, stop=True).then_inc(S["sc"], 1)

                def emit_transposes(sg):
                    wg(te, "pn", 64 * b + sg + 1)
                    if sg >= 2:
                        wg(te, "pte", 64 * b + sg - 1)
                    else:
                        wg(te, "tpe", 8 * (j + 1))
                    for kc in range(5):
                        mm = te.transpose(ps_pt[sg % 2][:, kc * 128:kc * 128 + 128],
                                          p_sb[:, (sg % 3) * 640 + kc * 128:(sg % 3) * 640 + kc * 128 + 128],
                                          idbf[:])
                    mm.then_inc(S["tp"], 1)

                def emit_pv(h):
                    pair, hh = h // 2, (h % 2) * 64
                    wg(te, "pte", 64 * b + 4 * h + 4)
                    if h % 2 == 0 and pair >= 2:
                        wg(te, "oe", 8 * b + pair - 1)
                    po = ps_o[pair % 2]
                    for qc in range(4):
                        for kc in range(5):
                            wc = qc + kc
                            sl, tc2 = (prev, wc) if wc < 4 else (cur, wc - 4)
                            mm = te.matmul(po[hh:hh + 64, qc * 128:qc * 128 + 128],
                                           v_ring[:, (sl * 4 + tc2) * 1024 + h * 64:(sl * 4 + tc2) * 1024 + h * 64 + 64],
                                           pT_sb[:, (h % 2) * 2560 + qc * 640 + kc * 128:(h % 2) * 2560 + qc * 640 + kc * 128 + 128],
                                           start=(kc == 0), stop=(kc == 4))
                    mm.then_inc(S["pv"], 1)

                for s in range(68):
                    if s < 64:
                        emit_scores(s)
                    if 2 <= s < 66:
                        emit_transposes(s - 2)
                    if s >= 4 and (s - 4) % 4 == 3:
                        emit_pv((s - 4) // 4)
                wg(te, "oe", 8 * (b + 1))
                wg(te, "sm1", 64 * (b + 1))
                for tc in range(4):
                    wg(te, "x1", 4 * b + max(tc - 1, 0))
                    pa = ps_a[tc % 2]
                    for n in range(2):
                        for ec in range(8):
                            mm = te.matmul(pa[:, n * 512:n * 512 + 512],
                                           o_sb[:, ec * 512 + tc * 128:ec * 512 + tc * 128 + 128],
                                           woutT[:, ec * 1024 + n * 512:ec * 1024 + n * 512 + 512],
                                           start=(ec == 0), stop=(ec == 7))
                    mm.then_inc(S["op"], 1)

        @ba.scalar
        def _(sc):
            sc.wait_ge(S["pl"], PL)
            sc.wait_ge(S["plg"], PLG)
            for j in range(NITER):
                b = j - 1
                slot = j % 2
                wg(sc, "x", 16 * (j + 1))
                xoff = 0
                for tc in range(4):
                    src = x_sb[:, tc * 1024:tc * 1024 + 1024]
                    ln_scalar_sq(sc, src, 0, tc, "sq1", 0, "va1", 4 * j + tc - 1)
                    ln_scalar_sqrt(sc, 0, tc, "va1", 4 * j + tc + 1, "std1", "st1",
                                   4 * j + tc - 1)
                    ln_scalar_apply(sc, xn1[:, tc * 1024:tc * 1024 + 1024], src,
                                    0, tc, "st1", 4 * j + tc + 1, "xn1")
                for ec in range(8):
                    wg(sc, "tpp", 8 * j + ec + 1)
                    sc.copy(xn1T[:, ec * 512:ec * 512 + 512],
                            ps_pt[ec % 2][:, 0:512]).then_inc(S["tpe"], 1)
                wg(sc, "pv", 16 * (j - 1))
                nt = 0
                if j >= 1:
                    for fcp in range(4):
                        wg(sc, "qkm", qke_cum(j - 1) + nt + 1)
                        sc.activation(q_sb[:, fcp * 1024:fcp * 1024 + 1024],
                                      ps_a[nt % 2][:, 0:1024], AF.Identity,
                                      scale=0.125).then_inc(S["qke"], 1)
                        nt += 1
                for fcp in range(4):
                    wg(sc, "qkm", qke_cum(j - 1) + nt + 1)
                    for half in range(2):
                        fc = fcp * 2 + half
                        ins = sc.copy(k_ring[:, fc * 1024 + slot * 512:fc * 1024 + slot * 512 + 512],
                                      ps_a[nt % 2][:, half * 512:half * 512 + 512])
                    ins.then_inc(S["qke"], 1)
                    nt += 1
                for tc in range(4):
                    wg(sc, "qkm", qke_cum(j - 1) + nt + 1)
                    sc.copy(v_ring[:, (slot * 4 + tc) * 1024:(slot * 4 + tc) * 1024 + 1024],
                            ps_a[nt % 2][:, 0:1024]).then_inc(S["qke"], 1)
                    nt += 1

                if j == 0:
                    continue

                def emit_pte(sg):
                    h2 = sg // 4
                    wg(sc, "tp", 64 * b + sg + 1)
                    if sg % 4 == 0:
                        wg(sc, "pv", 16 * b + max(h2 - 1, 0))
                    sc.copy(pT_sb[:, (h2 % 2) * 2560 + (sg % 4) * 640:(h2 % 2) * 2560 + (sg % 4) * 640 + 640],
                            ps_pt[sg % 2][:, 0:640]).then_inc(S["pte"], 1)
                    if sg % 8 == 7:
                        pair2 = sg // 8
                        wg(sc, "pv", 16 * b + 2 * pair2 + 2)
                        sc.copy(o_sb[:, pair2 * 512:pair2 * 512 + 512],
                                ps_o[pair2 % 2][:, 0:512]).then_inc(S["oe"], 1)

                for s in range(66):
                    if s < 64:
                        wg(sc, "t2", 64 * b + s + 1)
                        wg(sc, "tp", 64 * b + s - 2)
                        wg(sc, "pn", 64 * b + s - 2)
                        nmc, smc = 16 + (s % 3), 19 + (s % 3)
                        sc.activation(p_sb[:, (s % 3) * 640:(s % 3) * 640 + 640],
                                      t_sb[:, (s % 3) * 640:(s % 3) * 640 + 640],
                                      AF.Exp, bias=st[:, nmc:nmc + 1],
                                      accum_out=st[:, smc:smc + 1]).then_inc(S["p"], 1)
                    if s >= 2:
                        emit_pte(s - 2)
                for tc in range(4):
                    wg(sc, "x1", 4 * b + tc + 1)
                    src = x1t[:, (tc % 2) * 1024:(tc % 2) * 1024 + 1024]
                    ln_scalar_sq(sc, src, 28, tc, "sq2", 0, "va2", 4 * b + tc - 1)
                    ln_scalar_sqrt(sc, 28, tc, "va2", 4 * b + tc + 1, "std2", "st2",
                                   4 * b + tc - 1)
                    wg(sc, "stxn2", 16 * (4 * b + tc - 1))
                    ln_scalar_apply(sc, xn2t[:, (tc % 2) * 1024:(tc % 2) * 1024 + 1024],
                                    src, 28, tc, "st2", 4 * b + tc + 1, "xn2")

        @ba.vector
        def _(v):
            v.wait_ge(S["pl"], PL)
            v.wait_ge(S["plg"], PLG)
            for j in range(NITER):
                b = j - 1
                wg(v, "x", 16 * (j + 1))
                for tc in range(4):
                    ln_vector_stats(v, x_sb[:, tc * 1024:tc * 1024 + 1024], 0, tc,
                                    "sq1", 4 * j + tc + 1, "va1", "std1",
                                    4 * j + tc + 1, "st1", "xn1", 4 * j + tc - 1)
                if j == 0:
                    continue

                def emit_pn(sp):
                    wg(v, "p", 64 * b + sp + 1)
                    smc, rc = 19 + (sp % 3), 22 + (sp % 3)
                    vmark(v.reciprocal(st[:, rc:rc + 1], st[:, smc:smc + 1]))
                    vwait(v)
                    pp = p_sb[:, (sp % 3) * 640:(sp % 3) * 640 + 640]
                    v.tensor_scalar(pp, pp, st[:, rc:rc + 1], 1.0,
                                    op0=OP.mult, op1=OP.mult).then_inc(S["pn"], 1)

                for s in range(64):
                    h, qc = s // 4, s % 4
                    n1 = 512 - qc * 128
                    wg(v, "sc", 64 * b + s + 1)
                    wg(v, "p", 64 * b + s - 2)
                    t0 = (s % 3) * 640
                    tt = t_sb[:, t0:t0 + 640]
                    v.scalar_tensor_tensor(tt, kjb[:, qc * 128:qc * 128 + 640],
                                           -SLOPES[h], ps_a[s % 2][:, 0:640],
                                           op0=OP.mult, op1=OP.add).then_inc(S["sm1"], 1)
                    v.tensor_tensor(t_sb[:, t0:t0 + 128], t_sb[:, t0:t0 + 128],
                                    tprev[:], op=OP.add)
                    v.tensor_tensor(t_sb[:, t0 + 512:t0 + 640], t_sb[:, t0 + 512:t0 + 640],
                                    tcur[:], op=OP.add)
                    if b == 0:
                        vmark(v.tensor_tensor(t_sb[:, t0:t0 + n1], t_sb[:, t0:t0 + n1],
                                              maskP0[:, qc * 128:512], op=OP.add))
                        vwait(v)
                    nmc = 16 + (s % 3)
                    v.reduce_max(st[:, nmc:nmc + 1], tt, axis=AX,
                                 negate=True).then_inc(S["t2"], 1)
                    if s >= 1:
                        emit_pn(s - 1)
                if True:
                    emit_pn(63)
                for tc in range(4):
                    wg(v, "op", 4 * b + tc + 1)
                    wg(v, "xn2", 4 * b + tc - 1)
                    wg(v, "stx1", 16 * (4 * b + tc - 1))
                    v.tensor_tensor(x1t[:, (tc % 2) * 1024:(tc % 2) * 1024 + 1024],
                                    x_sb[:, tc * 1024:tc * 1024 + 1024],
                                    ps_a[tc % 2][:, 0:1024], op=OP.add).then_inc(S["x1"], 1)
                    ln_vector_stats(v, x1t[:, (tc % 2) * 1024:(tc % 2) * 1024 + 1024],
                                    28, tc, "sq2", 4 * b + tc + 1, "va2", "std2",
                                    4 * b + tc + 1, "st2", "xn2", 4 * b + tc - 1)

    if debug2:
        dbg_names = [("d_x", x_sb, F32, 4096), ("d_xn1", xn1, BF16, 4096),
                     ("d_xn1T", xn1T, BF16, 4096), ("d_q", q_sb, BF16, 4096),
                     ("d_k", k_ring, BF16, 8192), ("d_v", v_ring, BF16, 8192),
                     ("d_t", t_sb, F32, 2048), ("d_p", p_sb, BF16, 2048),
                     ("d_pT", pT_sb, BF16, 4096), ("d_o", o_sb, BF16, 4096),
                     ("d_st", st, F32, 48)]
        dbg_outs = {nm: nc.dram_tensor(nm, [128, w], dt, kind="ExternalOutput").ap()
                    for nm, _, dt, w in dbg_names}
        sdbg = nc.alloc_semaphore("s_dbg")
        with nc.Block() as bdg:
            @bdg.sync
            def _(sy):
                for nm, tile, dt, w in dbg_names:
                    sy.dma_start(out=dbg_outs[nm], in_=tile[:]).then_inc(sdbg, 16)
                sy.wait_ge(sdbg, 16 * len(dbg_names))

    stackA.close()
    psA.close()

    stackB = ExitStack()

    def SBB(name, shape, dt):
        return stackB.enter_context(nc.sbuf_tensor(name, shape, dt))

    stackB2 = ExitStack()
    acc = stackB2.enter_context(nc.sbuf_tensor("b_acc", [128, 8 * 2048], BF16))
    pos1u = stackB2.enter_context(nc.sbuf_tensor("b_pos1u", [128, 16], mybir.dt.uint32))
    pos2u = stackB2.enter_context(nc.sbuf_tensor("b_pos2u", [128, 16], mybir.dt.uint32))
    gcol1 = stackB2.enter_context(nc.sbuf_tensor("b_gcol1", [128, 16], F32))
    gcol2 = stackB2.enter_context(nc.sbuf_tensor("b_gcol2", [128, 16], F32))

    xn2T = SBB("b_xn2T", [128, 8 * 1024], BF16)
    h_sb = SBB("b_h", [128, 32 * 1024], BF16)
    W1_sb = [SBB("b_w1_%d" % i, [128, 4096], BF16) for i in range(2)]
    W2_sb = [SBB("b_w2_%d" % i, [128, 4096], BF16) for i in range(2)]
    lg_sb = SBB("b_lg", [128, 8], F32)
    lgm_sb = SBB("b_lgm", [128, 8], F32)
    eq1_sb = SBB("b_eq1", [128, 64], F32)
    eq2_sb = SBB("b_eq2", [128, 64], F32)
    d12_sb = SBB("b_d12", [128, 8], F32)
    g1_sb = SBB("b_g1", [128, 8], F32)
    g2_sb = SBB("b_g2", [128, 8], F32)
    gt_sb = SBB("b_gt", [128, 8], F32)
    gatef_sb = SBB("b_gatef", [128, 64], F32)
    gst = SBB("b_gst", [128, 4], F32)
    g8_sb = SBB("b_g8", [8, 1024], F32)
    me8 = SBB("b_me8", [8, 1024], F32)
    zer8 = SBB("b_zer8", [8, 1024], F32)
    posi8 = SBB("b_posi8", [8, 1024], F32)
    posT_sb = SBB("b_posT", [128, 64], F32)
    pv1 = SBB("b_pv1", [128, 8], F32)
    pv2 = SBB("b_pv2", [128, 8], F32)
    pv3 = SBB("b_pv3", [128, 8], F32)
    pc1 = SBB("b_pc1", [128, 1], F32)
    idxu_sb = SBB("b_idxu", [128, 64], mybir.dt.uint32)
    idxg_sb = [SBB("b_idxg%d" % i, [128, 3], mybir.dt.uint32) for i in range(2)]
    xg_nat = [SBB("b_xg0", [128, 3 * 1024], BF16)]
    xgT = SBB("b_xgT", [128, 8 * CAP], BF16)
    op_fm = SBB("b_opfm", [128, 8 * CAP], BF16)
    oprow = [SBB("b_oprow%d" % i, [128, 1024], BF16) for i in range(2)]

    psB = ExitStack()
    ps_fc = [psB.enter_context(nc.psum_tensor("ps_fc%d" % i, [128, 512], F32)) for i in range(2)]
    ps_out = [psB.enter_context(nc.psum_tensor("ps_out%d" % i, [128, 1024], F32)) for i in range(2)]
    ps_tb = psB.enter_context(nc.psum_tensor("ps_tb", [128, 1024], BF16))
    ps_tb2 = psB.enter_context(nc.psum_tensor("ps_tb2", [128, 1024], BF16))

    def fcp_slice_end(i1):
        if i1 < 0:
            return 0
        tth, r = divmod(i1, 72)
        e, hs = divmod(r, 8)
        base = 320 * tth
        if e == 0:
            return base + 8 * (hs + 1)
        return base + 64 + 4 * ((e - 1) * 8 + hs + 1)

    def fcp_e_end(tth, e):
        return 320 * tth + (64 if e >= 0 else 0) + 4 * 8 * max(e, 0)

    def gel_e_end(tth, e):
        return fcp_e_end(tth, e)

    with nc.Block() as bb:
        @bb.sync
        def _(sy):
            for tth in range(2):
                wg(sy, "fcp", 320 * tth)
                for ec in range(8):
                    sy.dma_start(out=xn2T[:, ec * 1024:ec * 1024 + 1024],
                                 in_=xn2_hbm[tth * 1024:tth * 1024 + 1024, ec * 128:ec * 128 + 128],
                                 transpose=True).then_inc(S["xn2t"], 16)
                wg(sy, "g8v", tth + 1)
                sy.dma_start(out=g8_hbm[:, tth * 1024:tth * 1024 + 1024],
                             in_=g8_sb[0:8, :]).then_inc(S["g8st"], 16)
                for e in range(NEXP):
                    for m in range(8):
                        i2 = (tth * 9 + e) * 8 + m
                        wg(sy, "pse", i2 - 1)
                        sy.dma_start(out=W2_sb[i2 % 2][:], in_=W2b[e, m]
                                     ).then_inc(S["w2"], 16)
                    if e >= 1:
                        slot = tth * 8 + (e - 1)
                        for gi in range(3):
                            og = slot * 3 + gi
                            wg(sy, "opv", og + 1)
                            sy.dma_start(out=out_perm[slot * CAP + gi * 128:slot * CAP + gi * 128 + 128, :],
                                         in_=oprow[og % 2][:]).then_inc(S["ops"], 16)

        @bb.gpsimd
        def _(g):
            r_cap = g.alloc_register()
            g.reg_mov(r_cap, CAP - 1)
            r_tok = g.alloc_register()
            g.reg_mov(r_tok, 1023)
            for tth in range(2):
                for e in range(NEXP):
                    if e >= 1:
                        ei = e - 1
                        slot = tth * 8 + ei
                        for tc in range(8):
                            wg(g, "ix", 8 * tth + tc + 1)
                            g.indirect_dma_start(
                                out=idxl_hbm,
                                out_offset=bass.IndirectOffsetOnAxis(
                                    idxu_sb[:, tc * 8 + ei:tc * 8 + ei + 1], axis=0),
                                in_=tokid_sb[:, tc:tc + 1], in_offset=None,
                                element_offset=slot * CAP,
                                bounds_check=r_cap, oob_is_err=False
                                ).then_inc(S["sca"], 16)
                        wg(g, "sca", 16 * (64 * tth + ei * 8 + 8))
                        wg(g, "xga", 16 * max(24 * tth + 3 * (e - 2), 0))
                        g.dma_start(out=idxg_sb[ei % 2][:],
                                    in_=idxl_hbm[slot * CAP:slot * CAP + CAP, :]
                                    .rearrange("(gi p) w -> p (gi w)", p=128),
                                    allow_slow_non_contiguous=True
                                    ).then_inc(S["ixl"], 16)
                        wg(g, "ixl", 16 * (8 * tth + e))
                        wg(g, "xgp", 8 * (8 * tth + ei))
                        for gi in range(3):
                            g.indirect_dma_start(
                                out=xg_nat[0][:, gi * 1024:gi * 1024 + 1024],
                                out_offset=None,
                                in_=xn2_hbm,
                                in_offset=bass.IndirectOffsetOnAxis(
                                    idxg_sb[ei % 2][:, gi:gi + 1], axis=0),
                                element_offset=tth * 1024 * 1024,
                                bounds_check=r_tok, oob_is_err=False
                                ).then_inc(S["xga"], 16)
                    for hs in range(8):
                        i1 = (tth * 9 + e) * 8 + hs
                        wg(g, "fcp", fcp_slice_end(i1 - 2))
                        g.dma_start(out=W1_sb[i1 % 2][:], in_=W1H[e, hs],
                                    max_dma_last_dim=8192).then_inc(S["w1"], 16)

        @bb.tensor
        def _(te):
            gfcp = [0]

            def fc_tile(i1pct, pslice, lhs_ap, rhs_ap):
                it = gfcp[0]
                wg(te, "gel", max(it - 1, 0))
                for ec in range(8):
                    mm = te.matmul(pslice, lhs_ap(ec), rhs_ap(ec),
                                   start=(ec == 0), stop=(ec == 7))
                mm.then_inc(S["fcp"], 1)
                gfcp[0] += 1
                return it

            for tth in range(2):
                wg(te, "xn2t", 16 * 8 * (tth + 1))
                wg(te, "gel", 320 * tth)
                wg(te, "pse", 72 * tth)
                for tc in range(8):
                    for ec in range(8):
                        mm = te.matmul(ps_fc[1][:, tc * 8:tc * 8 + 8],
                                       xn2T[:, ec * 1024 + tc * 128:ec * 1024 + tc * 128 + 128],
                                       wr_sb[:, ec * 8:ec * 8 + 8],
                                       start=(ec == 0), stop=(ec == 7))
                    mm.then_inc(S["rtm"], 1)
                for tc in range(8):
                    wg(te, "gv", 8 * tth + tc + 1)
                    te.transpose(ps_out[0][0:8, tc * 128:tc * 128 + 128],
                                 gatef_sb[:, tc * 8:tc * 8 + 8],
                                 idf32[:]).then_inc(S["gtt"], 1)
                wg(te, "g8v", tth + 1)
                wg(te, "pscn", tth + 1)
                for tc in range(8):
                    mm = te.transpose(ps_out[0][:, tc * 8:tc * 8 + 8],
                                      posi8[0:8, tc * 128:tc * 128 + 128],
                                      idf32[0:8, 0:8])
                mm.then_inc(S["ptt"], 1)
                wg(te, "gv1", 8 * (tth + 1))
                for e in range(NEXP):
                    ei = e - 1
                    if e >= 1:
                        wg(te, "xga", 16 * (24 * tth + 3 * e))
                        for ec in range(8):
                            xgpc = 64 * tth + 8 * ei + ec
                            wg(te, "xge", xgpc)
                            half = (ec % 2) * 512
                            for gi in range(3):
                                mm = te.transpose(ps_tb[:, half + gi * 128:half + gi * 128 + 128],
                                                  xg_nat[0][:, gi * 1024 + ec * 128:gi * 1024 + ec * 128 + 128],
                                                  idbf[:])
                            mm.then_inc(S["xgp"], 1)
                    for hs in range(8):
                        i1 = (tth * 9 + e) * 8 + hs
                        wg(te, "w1", 16 * (i1 + 1))
                        if e == 0 and hs == 0:
                            wg(te, "ptv", tth + 1)
                        if e >= 1 and hs == 0:
                            wg(te, "xge", 64 * tth + 8 * ei + 8)
                        for mf in range(4):
                            if e == 0:
                                for half in range(2):
                                    it = gfcp[0]
                                    pf = ps_fc[it % 2]
                                    wg(te, "gel", max(it - 1, 0))
                                    for ec in range(8):
                                        mm = te.matmul(pf[:, 0:512],
                                                       W1_sb[i1 % 2][:, ec * 512 + mf * 128:ec * 512 + mf * 128 + 128],
                                                       xn2T[:, ec * 1024 + half * 512:ec * 1024 + half * 512 + 512],
                                                       start=(ec == 0), stop=(ec == 7))
                                    mm.then_inc(S["fcp"], 1)
                                    gfcp[0] += 1
                            else:
                                it = gfcp[0]
                                pf = ps_fc[it % 2]
                                wg(te, "gel", max(it - 1, 0))
                                for ec in range(8):
                                    mm = te.matmul(pf[:, 0:CAP],
                                                   W1_sb[i1 % 2][:, ec * 512 + mf * 128:ec * 512 + mf * 128 + 128],
                                                   xgT[:, ec * CAP:ec * CAP + CAP],
                                                   start=(ec == 0), stop=(ec == 7))
                                mm.then_inc(S["fcp"], 1)
                                gfcp[0] += 1
                    wg(te, "gel", gel_e_end(tth, e))
                    for m in range(8):
                        i2 = (tth * 9 + e) * 8 + m
                        wg(te, "w2", 16 * (i2 + 1))
                        wg(te, "pse", max(i2 - 1, 0))
                        po = ps_out[i2 % 2]
                        if e == 0:
                            for hb in range(32):
                                for n in range(2):
                                    mm = te.matmul(po[:, n * 512:n * 512 + 512],
                                                   W2_sb[i2 % 2][:, hb * 128:hb * 128 + 128],
                                                   h_sb[:, hb * 1024 + n * 512:hb * 1024 + n * 512 + 512],
                                                   start=(hb == 0), stop=(hb == 31))
                        else:
                            for hb in range(32):
                                mm = te.matmul(po[:, 0:CAP],
                                               W2_sb[i2 % 2][:, hb * 128:hb * 128 + 128],
                                               h_sb[:, hb * 1024:hb * 1024 + CAP],
                                               start=(hb == 0), stop=(hb == 31))
                        mm.then_inc(S["prm"], 1)
                    if e >= 1:
                        for gi in range(3):
                            og = (tth * 8 + ei) * 3 + gi
                            wg(te, "pse", 72 * tth + 8 * (e + 1))
                            wg(te, "opv", og)
                            for m in range(8):
                                mm = te.transpose(ps_tb2[:, m * 128:m * 128 + 128],
                                                  op_fm[:, m * CAP + gi * 128:m * CAP + gi * 128 + 128],
                                                  idbf[:])
                            mm.then_inc(S["ott"], 1)

        @bb.scalar
        def _(sc):
            gfcp = [0]
            for tth in range(2):
                wg(sc, "gv1", 8 * (tth + 1))
                sc.activation(g1_sb[:], d12_sb[:], AF.Sigmoid)
                sc.activation(g2_sb[:], d12_sb[:], AF.Sigmoid,
                              scale=-1.0).then_inc(S["gs"], 1)
                for e in range(NEXP):
                    ei = e - 1
                    if e >= 1:
                        for ec in range(8):
                            xgpc = 64 * tth + 8 * ei + ec
                            wg(sc, "xgp", xgpc + 1)
                            if ec == 0:
                                wg(sc, "fcp", fcp_e_end(tth, e - 1))
                            half = (ec % 2) * 512
                            sc.copy(xgT[:, ec * CAP:ec * CAP + CAP],
                                    ps_tb[:, half:half + CAP]).then_inc(S["xge"], 1)
                    wg(sc, "prm", 8 * (tth * 9 + e))
                    for hs in range(8):
                        for mf in range(4):
                            hb = hs * 4 + mf
                            bcol = bfc_sb[:, e * 32 + hb:e * 32 + hb + 1]
                            if e == 0:
                                for half in range(2):
                                    it = gfcp[0]
                                    wg(sc, "fcp", it + 1)
                                    sc.activation(h_sb[:, hb * 1024 + half * 512:hb * 1024 + half * 512 + 512],
                                                  ps_fc[it % 2][:, 0:512], AF.Gelu_apprx_tanh,
                                                  bias=bcol).then_inc(S["gel"], 1)
                                    gfcp[0] += 1
                            else:
                                it = gfcp[0]
                                wg(sc, "fcp", it + 1)
                                sc.activation(h_sb[:, hb * 1024:hb * 1024 + CAP],
                                              ps_fc[it % 2][:, 0:CAP], AF.Gelu_apprx_tanh,
                                              bias=bcol).then_inc(S["gel"], 1)
                                gfcp[0] += 1
                    if e >= 1:
                        for m in range(8):
                            i2 = (tth * 9 + e) * 8 + m
                            wg(sc, "prm", i2 + 1)
                            if m == 0:
                                wg(sc, "ott", 3 * max(tth * 8 + ei, 0))
                            sc.activation(op_fm[:, m * CAP:m * CAP + CAP],
                                          ps_out[i2 % 2][:, 0:CAP], AF.Identity,
                                          bias=bproj_sb[:, e * 8 + m:e * 8 + m + 1]
                                          ).then_inc(S["pse"], 1)
                        for gi in range(3):
                            og = (tth * 8 + ei) * 3 + gi
                            wg(sc, "ott", og + 1)
                            wg(sc, "ops", 16 * max(og - 1, 0))
                            sc.copy(oprow[og % 2][:], ps_tb2[:]).then_inc(S["opv"], 1)

        @bb.vector
        def _(v):
            v.memset(zer8[:], 0.0)
            for tth in range(2):
                wg(v, "rtm", 8 * (tth + 1))
                for tc in range(8):
                    vmark(v.tensor_copy(lg_sb[:], ps_fc[1][:, tc * 8:tc * 8 + 8]))
                    vwait(v)
                    vmark(v.reduce_max(gst[:, 0:1], lg_sb[:], axis=AX))
                    vwait(v)
                    vmark(v.tensor_scalar(eq1_sb[:, tc * 8:tc * 8 + 8], lg_sb[:],
                                          gst[:, 0:1], 1.0, op0=OP.is_ge, op1=OP.mult))
                    vwait(v)
                    vmark(v.scalar_tensor_tensor(lgm_sb[:], eq1_sb[:, tc * 8:tc * 8 + 8],
                                                 -1.0e9, lg_sb[:], op0=OP.mult, op1=OP.add))
                    vwait(v)
                    vmark(v.reduce_max(gst[:, 1:2], lgm_sb[:], axis=AX))
                    vwait(v)
                    vmark(v.tensor_scalar(eq2_sb[:, tc * 8:tc * 8 + 8], lgm_sb[:],
                                          gst[:, 1:2], 1.0, op0=OP.is_ge, op1=OP.mult))
                    vwait(v)
                    v.tensor_tensor(d12_sb[:, tc:tc + 1], gst[:, 0:1], gst[:, 1:2],
                                    op=OP.subtract).then_inc(S["gv1"], 1)
                wg(v, "gs", tth + 1)
                wg(v, "gtt", 8 * tth)
                for tc in range(8):
                    vmark(v.tensor_scalar(gt_sb[:], eq1_sb[:, tc * 8:tc * 8 + 8],
                                          g1_sb[:, tc:tc + 1], 1.0, op0=OP.mult, op1=OP.mult))
                    vwait(v)
                    v.scalar_tensor_tensor(gatef_sb[:, tc * 8:tc * 8 + 8],
                                           eq2_sb[:, tc * 8:tc * 8 + 8],
                                           g2_sb[:, tc:tc + 1], gt_sb[:],
                                           op0=OP.mult, op1=OP.add).then_inc(S["gv"], 1)
                wg(v, "gtt", 8 * (tth + 1))
                wg(v, "g8st", 16 * tth)
                v.tensor_copy(g8_sb[0:8, :], ps_out[0][0:8, 0:1024]).then_inc(S["g8v"], 1)
                v.wait_ge(S["g8v"], tth + 1)
                vmark(v.tensor_scalar(me8[:], g8_sb[0:8, :], 0.0, 1.0,
                                      op0=OP.is_gt, op1=OP.mult))
                vwait(v)
                vmark(v.tensor_tensor_scan(posi8[:], me8[:], zer8[:], 0.0,
                                           op0=OP.add, op1=OP.add))
                vwait(v)
                v.tensor_tensor(posi8[:], posi8[:], me8[:],
                                op=OP.subtract).then_inc(S["pscn"], 1)
                wg(v, "ptt", tth + 1)
                wg(v, "sca", 16 * 64 * tth)
                v.tensor_copy(posT_sb[:], ps_out[0][:, 0:64]).then_inc(S["ptv"], 1)
                v.wait_ge(S["ptv"], tth + 1)
                for tc in range(8):
                    tg = tth * 8 + tc
                    rank = posT_sb[:, tc * 8:tc * 8 + 8]
                    vmark(v.tensor_scalar(pv3[:], gatef_sb[:, tc * 8:tc * 8 + 8],
                                          0.0, 1.0, op0=OP.is_gt, op1=OP.mult))
                    vwait(v)
                    vmark(v.tensor_scalar(pv2[:], pv3[:], 1.0, -1.0e9,
                                          op0=OP.subtract, op1=OP.mult))
                    vwait(v)
                    vmark(v.tensor_tensor(pv1[:], rank, pv2[:], op=OP.add))
                    vwait(v)
                    vmark(v.tensor_copy(idxu_sb[:, tc * 8:tc * 8 + 8], pv1[:]))
                    vwait(v)
                    vmark(v.tensor_tensor(pv1[:], rank, posbase_sb[:, tth * 8:tth * 8 + 8],
                                          op=OP.add))
                    vwait(v)
                    vmark(v.tensor_scalar(pv2[:], rank, CAP - 0.5, 1.0,
                                          op0=OP.is_gt, op1=OP.mult))
                    vwait(v)
                    vmark(v.tensor_scalar(pv3[:], pv2[:], -1.0, 1.0,
                                          op0=OP.mult, op1=OP.add))
                    vwait(v)
                    vmark(v.tensor_tensor(pv1[:], pv1[:], pv3[:], op=OP.mult))
                    vwait(v)
                    vmark(v.tensor_scalar(pv2[:], pv2[:], float(ZROW), 1.0,
                                          op0=OP.mult, op1=OP.mult))
                    vwait(v)
                    vmark(v.tensor_tensor(pv1[:], pv1[:], pv2[:], op=OP.add))
                    vwait(v)
                    vmark(v.tensor_tensor(pv2[:], eq1_sb[:, tc * 8:tc * 8 + 8], pv1[:],
                                          op=OP.mult))
                    vwait(v)
                    vmark(v.reduce_sum(pc1[:], pv2[:], axis=AX))
                    vwait(v)
                    vmark(v.tensor_scalar(pc1[:], pc1[:], float(ZROW), 1.0,
                                          op0=OP.min, op1=OP.mult))
                    vwait(v)
                    vmark(v.tensor_copy(pos1u[:, tg:tg + 1], pc1[:]))
                    vwait(v)
                    vmark(v.tensor_tensor(pv2[:], eq2_sb[:, tc * 8:tc * 8 + 8], pv1[:],
                                          op=OP.mult))
                    vwait(v)
                    vmark(v.reduce_sum(pc1[:], pv2[:], axis=AX))
                    vwait(v)
                    vmark(v.tensor_scalar(pc1[:], pc1[:], float(ZROW), 1.0,
                                          op0=OP.min, op1=OP.mult))
                    vwait(v)
                    vmark(v.tensor_copy(pos2u[:, tg:tg + 1], pc1[:]))
                    vwait(v)
                    vmark(v.tensor_copy(gcol1[:, tg:tg + 1], g1_sb[:, tc:tc + 1]))
                    vwait(v)
                    v.tensor_copy(gcol2[:, tg:tg + 1], g2_sb[:, tc:tc + 1]
                                  ).then_inc(S["ix"], 1)
                for m in range(8):
                    i2 = (tth * 9 + 0) * 8 + m
                    wg(v, "prm", i2 + 1)
                    aslice = acc[:, m * 2048 + tth * 1024:m * 2048 + tth * 1024 + 1024]
                    v.tensor_scalar(aslice, ps_out[i2 % 2][:, 0:1024],
                                    bproj_sb[:, 0 * 8 + m:0 * 8 + m + 1], 1.0,
                                    op0=OP.add, op1=OP.mult).then_inc(S["pse"], 1)

    stackB.close()
    psB.close()

    stackC = ExitStack()
    x1f = [stackC.enter_context(nc.sbuf_tensor("c_x1f%d" % i, [128, 1024], F32)) for i in range(2)]
    y_sb = [stackC.enter_context(nc.sbuf_tensor("c_y%d" % i, [128, 1024], F32)) for i in range(2)]
    c1_sb = [stackC.enter_context(nc.sbuf_tensor("c_c1_%d" % i, [128, 1024], BF16)) for i in range(2)]
    c2_sb = [stackC.enter_context(nc.sbuf_tensor("c_c2_%d" % i, [128, 1024], BF16)) for i in range(2)]
    tmpc = stackC.enter_context(nc.sbuf_tensor("c_tmpc", [128, 1024], F32))
    psC = ExitStack()
    ps_y = [psC.enter_context(nc.psum_tensor("ps_y%d" % i, [128, 1024], BF16)) for i in range(2)]

    with nc.Block() as bc:
        @bc.sync
        def _(sy):
            for tc in range(16):
                wg(sy, "cy", max(tc - 1, 0))
                sy.dma_start(out=x1f[tc % 2][:],
                             in_=x1_hbm[tc * 128:tc * 128 + 128, :]).then_inc(S["cx1"], 16)
                wg(sy, "cy", tc + 1)
                sy.dma_start(out=y_out[tc * 128:tc * 128 + 128, :],
                             in_=y_sb[tc % 2][:]).then_inc(S["y"], 16)
            sy.wait_ge(S["y"], 16 * 16)

        @bc.gpsimd
        def _(g):
            r_zr = g.alloc_register()
            g.reg_mov(r_zr, ZROW)
            g.wait_ge(S["ops"], 16 * 48)
            g.wait_ge(S["ix"], 16)
            for tc in range(16):
                wg(g, "cy", max(tc - 1, 0))
                for pu, dst in ((pos1u, c1_sb), (pos2u, c2_sb)):
                    g.indirect_dma_start(
                        out=dst[tc % 2][:], out_offset=None,
                        in_=out_perm,
                        in_offset=bass.IndirectOffsetOnAxis(pu[:, tc:tc + 1], axis=0),
                        bounds_check=r_zr, oob_is_err=False
                        ).then_inc(S["cg"], 16)

        @bc.tensor
        def _(te):
            wg(te, "pse", 144)
            for tc in range(16):
                wg(te, "cy", max(tc - 1, 0))
                for m in range(8):
                    mm = te.transpose(ps_y[tc % 2][:, m * 128:m * 128 + 128],
                                      acc[:, m * 2048 + tc * 128:m * 2048 + tc * 128 + 128],
                                      idbf[:])
                mm.then_inc(S["ct"], 1)

        @bc.vector
        def _(v):
            for tc in range(16):
                wg(v, "cx1", 16 * (tc + 1))
                wg(v, "ct", tc + 1)
                wg(v, "cg", 32 * (tc + 1))
                wg(v, "y", 16 * max(tc - 1, 0))
                ys = y_sb[tc % 2][:]
                v.tensor_tensor(ys, x1f[tc % 2][:], ps_y[tc % 2][:, 0:1024],
                                op=OP.add)
                v.tensor_scalar(tmpc[:], c1_sb[tc % 2][:], gcol1[:, tc:tc + 1], 1.0,
                                op0=OP.mult, op1=OP.mult)
                v.tensor_tensor(ys, ys, tmpc[:], op=OP.add)
                v.tensor_scalar(tmpc[:], c2_sb[tc % 2][:], gcol2[:, tc:tc + 1], 1.0,
                                op0=OP.mult, op1=OP.mult)
                v.tensor_tensor(ys, ys, tmpc[:], op=OP.add).then_inc(S["cy"], 1)

    stackC.close()
    psC.close()
    return nc


_CACHED = {}


def _get_nc():
    if "nc" not in _CACHED:
        _CACHED["nc"] = build_program(debug=False)
    return _CACHED["nc"]


def _host_inputs(L0_attn_x, L0_attn_w_in, L0_attn_w_out, L0_ffn_Wfc_s, L0_ffn_bfc_s,
                 L0_ffn_Wproj_s, L0_ffn_bproj_s, L0_ffn_Wr, L0_ffn_Wfc, L0_ffn_bfc,
                 L0_ffn_Wproj, L0_ffn_bproj):
    f32 = np.float32
    x = np.ascontiguousarray(np.asarray(L0_attn_x, f32).reshape(16384, 1024))
    w_inT = np.ascontiguousarray(np.asarray(L0_attn_w_in, f32).T)
    w_outT = np.ascontiguousarray(np.asarray(L0_attn_w_out, f32).T)
    Wfc_all = np.concatenate([np.asarray(L0_ffn_Wfc_s, f32)[None],
                              np.asarray(L0_ffn_Wfc, f32)], 0)
    Wproj_all = np.concatenate([np.asarray(L0_ffn_Wproj_s, f32)[None],
                                np.asarray(L0_ffn_Wproj, f32)], 0)
    bfc_all = np.concatenate([np.asarray(L0_ffn_bfc_s, f32)[None],
                              np.asarray(L0_ffn_bfc, f32)], 0)
    bproj_all = np.concatenate([np.asarray(L0_ffn_bproj_s, f32)[None],
                                np.asarray(L0_ffn_bproj, f32)], 0)
    W1H = np.ascontiguousarray(
        Wfc_all.reshape(NEXP, 8, 128, 8, 512).transpose(0, 3, 2, 1, 4).reshape(NEXP, 8, 128, 4096))
    W2H = np.ascontiguousarray(
        Wproj_all.reshape(NEXP, 32, 128, 8, 128).transpose(0, 3, 2, 1, 4).reshape(NEXP, 8, 128, 4096))
    bfcB = np.ascontiguousarray(bfc_all.reshape(NEXP, 32, 128).transpose(0, 2, 1))
    bprojB = np.ascontiguousarray(bproj_all.reshape(NEXP, 8, 128).transpose(0, 2, 1))
    WrH = np.ascontiguousarray(
        np.asarray(L0_ffn_Wr, f32).reshape(8, 128, 8).transpose(1, 0, 2).reshape(128, 64))
    kjb = np.ascontiguousarray(np.broadcast_to(np.arange(1024, dtype=f32), (128, 1024)))
    qi = np.arange(512, dtype=np.int64)[:, None]
    cc = np.arange(512, dtype=np.int64)[None, :]
    mprev = np.where(qi <= cc, 0.0, NEG).astype(f32)
    mcur = np.where(cc <= qi, 0.0, NEG).astype(f32)
    pp_ = np.arange(128)[:, None]
    cc_ = np.arange(128)[None, :]
    tprev = np.where(pp_ <= cc_, 0.0, NEG).astype(ml_dtypes.bfloat16)
    tcur = np.where(cc_ <= pp_, 0.0, NEG).astype(ml_dtypes.bfloat16)
    maskP0_z = np.zeros((128, 512), ml_dtypes.bfloat16)
    maskP0_n = np.full((128, 512), NEG, ml_dtypes.bfloat16)
    idbf = np.eye(128, dtype=ml_dtypes.bfloat16)
    idf32 = np.eye(128, dtype=f32)
    consts = np.zeros((128, 4), f32)
    consts[:, 0] = 1e-5

    tokid = np.ascontiguousarray(
        (np.arange(8)[None, :] * 128 + np.arange(128)[:, None]).astype(np.uint32))
    posbase = np.ascontiguousarray(np.broadcast_to(
        (np.arange(16, dtype=f32) * 384.0)[None, :], (128, 16)))
    zci = np.zeros((16 * 384, 1), np.uint32)
    z1k = np.zeros((1, 1024), ml_dtypes.bfloat16)
    shared = dict(w_inT=w_inT, w_outT=w_outT, W1H=W1H, W2H=W2H, bfcB=bfcB,
                  bprojB=bprojB, WrH=WrH, kjb=kjb, tprev=tprev, tcur=tcur,
                  idbf=idbf, idf32=idf32, consts=consts,
                  tokid=tokid, posbase=posbase, zci=zci, z1k=z1k)
    zeros_halo = np.zeros((512, 1024), f32)
    in_maps = []
    for c in range(8):
        m = dict(shared)
        m["x_own"] = np.ascontiguousarray(x[c * T:(c + 1) * T])
        m["x_halo"] = zeros_halo if c in (0, 4) else np.ascontiguousarray(x[c * T - 512:c * T])
        m["maskP0"] = maskP0_n if c in (0, 4) else maskP0_z
        in_maps.append(m)
    return in_maps


def kernel(**inputs):
    in_maps = _host_inputs(**inputs)
    nc = _get_nc()
    res = run_bass_kernel_spmd(nc, in_maps, core_ids=list(range(8)))
    y = np.concatenate([np.asarray(res.results[c]["y"]) for c in range(8)], axis=0)
    return y.reshape(2, 8192, 1024).astype(np.float32, copy=False)


# version 16
# speedup vs baseline: 1.1161x; 1.0041x over previous
import numpy as np
import ml_dtypes
from contextlib import ExitStack
import concourse.bass as bass
import concourse.mybir as mybir
from concourse.bass_utils import run_bass_kernel_spmd
from concourse.alu_op_type import AluOpType as OP

F32 = mybir.dt.float32
BF16 = mybir.dt.bfloat16
AF = mybir.ActivationFunctionType
AX = mybir.AxisListType.X

T = 2048
D = 1024
NEXP = 9
H = 16
NITER = 5
CAP = 384
ZROW = 16 * CAP
NEG = -1.0e30
SLOPES = [float(np.exp2(np.float32(-(h + 1) * 8.0 / 16.0))) for h in range(H)]


def qke_cum(j):
    if j < 0:
        return 0
    return 8 + 12 * j


def build_program(debug=False, debug2=False):
    nc = bass.Bass("TRN2", target_bir_lowering=False, debug=False,
                   detect_race_conditions=False)

    x_own = nc.dram_tensor("x_own", [T, D], F32, kind="ExternalInput").ap()
    x_halo = nc.dram_tensor("x_halo", [512, D], F32, kind="ExternalInput").ap()
    w_inT = nc.dram_tensor("w_inT", [D, 3 * D], F32, kind="ExternalInput").ap()
    w_outT = nc.dram_tensor("w_outT", [D, D], F32, kind="ExternalInput").ap()
    W1H = nc.dram_tensor("W1H", [NEXP, 8, 128, 4096], F32, kind="ExternalInput").ap()
    W2H = nc.dram_tensor("W2H", [NEXP, 8, 128, 4096], F32, kind="ExternalInput").ap()
    tokid_in = nc.dram_tensor("tokid", [128, 8], mybir.dt.uint32, kind="ExternalInput").ap()
    posbase_in = nc.dram_tensor("posbase", [128, 16], F32, kind="ExternalInput").ap()
    zci_in = nc.dram_tensor("zci", [16 * CAP, 1], mybir.dt.uint32, kind="ExternalInput").ap()
    z1k_in = nc.dram_tensor("z1k", [1, 1024], BF16, kind="ExternalInput").ap()
    bfcB = nc.dram_tensor("bfcB", [NEXP, 128, 32], F32, kind="ExternalInput").ap()
    bprojB = nc.dram_tensor("bprojB", [NEXP, 128, 8], F32, kind="ExternalInput").ap()
    WrH = nc.dram_tensor("WrH", [128, 64], F32, kind="ExternalInput").ap()
    kjb_in = nc.dram_tensor("kjb", [128, 1024], F32, kind="ExternalInput").ap()
    maskP0_in = nc.dram_tensor("maskP0", [128, 512], BF16, kind="ExternalInput").ap()
    idbf_in = nc.dram_tensor("idbf", [128, 128], BF16, kind="ExternalInput").ap()
    idf_in = nc.dram_tensor("idf32", [128, 128], F32, kind="ExternalInput").ap()
    consts_in = nc.dram_tensor("consts", [128, 4], F32, kind="ExternalInput").ap()

    y_out = nc.dram_tensor("y", [T, D], F32, kind="ExternalOutput").ap()
    skind = "ExternalOutput" if debug else "Internal"
    x1_hbm = nc.dram_tensor("x1_hbm", [T, D], F32, kind=skind).ap()
    xn2_hbm = nc.dram_tensor("xn2_hbm", [T, D], BF16, kind=skind).ap()
    g8_hbm = nc.dram_tensor("g8_hbm", [8, T], F32, kind=skind).ap()
    idxl_hbm = nc.dram_tensor("idxl_hbm", [16 * CAP, 1], mybir.dt.uint32, kind="Internal").ap()
    W1b = nc.dram_tensor("W1b", [NEXP, 8, 128, 4096], BF16, kind="Internal").ap()
    W2b = nc.dram_tensor("W2b", [NEXP, 8, 128, 4096], BF16, kind="Internal").ap()
    out_perm = nc.dram_tensor("out_perm", [16 * CAP + 1, 1024], BF16, kind="Internal").ap()

    SBC = nc.alloc_sbuf_tensor
    idbf = SBC("c_idbf", [128, 128], BF16)
    idf32 = SBC("c_idf32", [128, 128], F32)
    consts = SBC("c_consts", [128, 4], F32)
    wr_sb = SBC("c_wr", [128, 64], BF16)
    bfc_sb = SBC("c_bfc", [128, NEXP * 32], F32)
    bproj_sb = SBC("c_bproj", [128, NEXP * 8], F32)
    tokid_sb = SBC("c_tokid", [128, 8], mybir.dt.uint32)
    posbase_sb = SBC("c_posbase", [128, 16], F32)

    S = {}
    for nm in ["pl", "plg", "x", "sq1", "va1", "std1", "st1", "xn1", "tpp", "tpe",
               "qkm", "qke", "sm1", "sc", "t2", "p", "pn", "tp", "pte", "pv",
               "oe", "op", "x1", "sq2", "va2", "std2", "st2", "xn2",
               "stx1", "stxn2",
               "xn2t", "w1", "w2", "fcp", "gel", "prm", "pse",
               "rtm", "gv1", "gs", "gv", "gtt", "g8v", "g8st",
               "pscn", "ptv", "ptt", "ix", "sca", "ixl", "xga", "xgp", "xge", "pwc", "pww",
               "ott", "opv", "ops", "cg",
               "cx1", "ct", "cy", "y", "vv"]:
        S[nm] = nc.alloc_semaphore("s_" + nm)

    def wg(eng, nm, val):
        if val > 0:
            eng.wait_ge(S[nm], val)

    vvc = [0]

    def vmark(inst):
        inst.then_inc(S["vv"], 1)
        vvc[0] += 1

    def vwait(v):
        if vvc[0] > 0:
            v.wait_ge(S["vv"], vvc[0])

    stackA = ExitStack()

    def SBA(name, shape, dt):
        return stackA.enter_context(nc.sbuf_tensor(name, shape, dt))

    winT = SBA("a_winT", [128, 8 * 3072], BF16)
    woutT = SBA("a_woutT", [128, 8 * 1024], BF16)
    kjb = SBA("a_kjb", [128, 1024], F32)
    maskP0 = SBA("a_maskP0", [128, 512], BF16)
    x_sb = SBA("a_x", [128, 4096], F32)
    xn1 = SBA("a_xn1", [128, 4096], BF16)
    xn1T = SBA("a_xn1T", [128, 8 * 512], BF16)
    q_sb = SBA("a_q", [128, 8 * 512], BF16)
    k_ring = SBA("a_kring", [128, 8 * 1024], BF16)
    v_ring = SBA("a_vring", [128, 8 * 1024], BF16)
    t_sb = SBA("a_t", [128, 3 * 640], F32)
    p_sb = SBA("a_p", [128, 3 * 640], BF16)
    pT_sb = SBA("a_pT", [128, 2 * 2560], BF16)
    o_sb = SBA("a_o", [128, 8 * 512], BF16)
    x1t = SBA("a_x1t", [128, 2 * 1024], F32)
    xn2t = SBA("a_xn2t", [128, 2 * 1024], BF16)
    sq_scr = SBA("a_sqscr", [128, 1024], F32)
    pwb = [SBA("a_pwb%d" % i, [128, 4096], BF16) for i in range(2)]
    st = SBA("a_stats", [128, 48], F32)

    psA = ExitStack()
    ps_a = [psA.enter_context(nc.psum_tensor("ps_a%d" % i, [128, 1024], F32)) for i in range(2)]
    ps_pt = [psA.enter_context(nc.psum_tensor("ps_pt%d" % i, [128, 1024], BF16)) for i in range(2)]
    ps_o = [psA.enter_context(nc.psum_tensor("ps_o%d" % i, [128, 512], F32)) for i in range(2)]

    with nc.Block() as bp:
        @bp.sync
        def _(sy):
            for dst, src in [(idbf, idbf_in), (idf32, idf_in), (consts, consts_in),
                             (kjb, kjb_in), (maskP0, maskP0_in)]:
                sy.dma_start(out=dst[:], in_=src).then_inc(S["pl"], 16)
            sy.dma_start(out=bfc_sb[:], in_=bfcB.rearrange("e p c -> p e c")).then_inc(S["pl"], 16)
            sy.dma_start(out=bproj_sb[:], in_=bprojB.rearrange("e p c -> p e c")).then_inc(S["pl"], 16)
            sy.dma_start(out=tokid_sb[:], in_=tokid_in).then_inc(S["pl"], 16)
            sy.dma_start(out=posbase_sb[:], in_=posbase_in).then_inc(S["pl"], 16)
            sy.wait_ge(S["pl"], 9 * 16)

        @bp.gpsimd
        def _(g):
            for ec in range(8):
                g.dma_start(out=winT[:, ec * 3072:(ec + 1) * 3072],
                            in_=w_inT[ec * 128:(ec + 1) * 128, :],
                            max_dma_last_dim=8192).then_inc(S["plg"], 16)
                g.dma_start(out=woutT[:, ec * 1024:(ec + 1) * 1024],
                            in_=w_outT[ec * 128:(ec + 1) * 128, :],
                            max_dma_last_dim=8192).then_inc(S["plg"], 16)
            g.dma_start(out=wr_sb[:], in_=WrH, max_dma_last_dim=8192).then_inc(S["plg"], 16)
            g.dma_start(out=idxl_hbm, in_=zci_in).then_inc(S["plg"], 16)
            g.dma_start(out=out_perm[ZROW:ZROW + 1, :], in_=z1k_in).then_inc(S["plg"], 16)
            g.wait_ge(S["plg"], 19 * 16)

    nc.const_aps.register(F32, 1e-5, consts[:, 0:1])
    PL = 9 * 16
    PLG = 19 * 16

    def ln_scalar_sq(sc, src, base, tc, sem_sq, cnt_sq, sem_va, cnt_va_war):
        wg(sc, sem_va, cnt_va_war)
        c = base + (tc % 2) * 8
        sc.activation(sq_scr[:], src, AF.Square,
                      accum_out=st[:, c + 1:c + 2]).then_inc(S[sem_sq], 1)

    def ln_vector_stats(v, src, base, tc, sem_sq, cnt_sq, sem_va, sem_std, cnt_std,
                        sem_st, sem_xn, cnt_xn_war):
        c = base + (tc % 2) * 8
        wg(v, sem_xn, cnt_xn_war)
        vmark(v.reduce_sum(st[:, c + 0:c + 1], src, axis=AX))
        wg(v, sem_sq, cnt_sq)
        vwait(v)
        vmark(v.tensor_scalar_mul(st[:, c + 2:c + 3], st[:, c + 0:c + 1], 1.0 / 1024.0))
        v.tensor_scalar_mul(st[:, c + 3:c + 4], st[:, c + 1:c + 2], 1.0 / 1024.0)
        vwait(v)
        vmark(v.tensor_scalar(st[:, c + 4:c + 5], st[:, c + 2:c + 3],
                              st[:, c + 2:c + 3], 1.0, op0=OP.mult, op1=OP.mult))
        vwait(v)
        v.tensor_tensor(st[:, c + 3:c + 4], st[:, c + 3:c + 4],
                        st[:, c + 4:c + 5], op=OP.subtract).then_inc(S[sem_va], 1)
        wg(v, sem_std, cnt_std)
        vmark(v.reciprocal(st[:, c + 5:c + 6], st[:, c + 4:c + 5]))
        vwait(v)
        v.tensor_scalar(st[:, c + 6:c + 7], st[:, c + 2:c + 3],
                        st[:, c + 5:c + 6], -1.0, op0=OP.mult, op1=OP.mult
                        ).then_inc(S[sem_st], 1)

    def ln_scalar_sqrt(sc, base, tc, sem_va, cnt_va, sem_std, sem_st, cnt_st_war):
        c = base + (tc % 2) * 8
        wg(sc, sem_va, cnt_va)
        wg(sc, sem_st, cnt_st_war)
        sc.activation(st[:, c + 4:c + 5], st[:, c + 3:c + 4], AF.Sqrt,
                      bias=1e-5).then_inc(S[sem_std], 1)

    def ln_scalar_apply(sc, dst, src, base, tc, sem_st, cnt_st, sem_xn):
        c = base + (tc % 2) * 8
        wg(sc, sem_st, cnt_st)
        sc.activation(dst, src, AF.Identity, bias=st[:, c + 6:c + 7],
                      scale=st[:, c + 5:c + 6]).then_inc(S[sem_xn], 1)

    with nc.Block() as ba:
        @ba.gpsimd
        def _(g):
            slices = [(s_t, d_t, e, hs) for s_t, d_t in ((W2H, W2b),)
                      for e in range(NEXP) for hs in range(8)]
            for i in range(len(slices) + 1):
                if i < len(slices):
                    s_t, d_t, e, hs = slices[i]
                    wg(g, "pww", 16 * max(i - 1, 0))
                    g.dma_start(out=pwb[i % 2][:], in_=s_t[e, hs],
                                max_dma_last_dim=8192).then_inc(S["pwc"], 16)
                if i >= 1:
                    s_t, d_t, e, hs = slices[i - 1]
                    g.wait_ge(S["pwc"], 16 * i)
                    g.dma_start(out=d_t[e, hs], in_=pwb[(i - 1) % 2][:]
                                ).then_inc(S["pww"], 16)
            g.wait_ge(S["pww"], 16 * 72)

        @ba.sync
        def _(sy):
            for j in range(NITER):
                wg(sy, "xn1", 4 * j)
                wg(sy, "st1", 4 * j)
                wg(sy, "sq1", 4 * j)
                wg(sy, "x1", 4 * (j - 1))
                src = x_halo if j == 0 else x_own[(j - 1) * 512:j * 512, :]
                sy.dma_start(out=x_sb[:],
                             in_=src.rearrange("(tc p) f -> p tc f", p=128)
                             ).then_inc(S["x"], 16)
                if j == 0:
                    continue
                b = j - 1
                for tc in range(4):
                    wg(sy, "x1", 4 * b + tc + 1)
                    r0 = b * 512 + tc * 128
                    sy.dma_start(out=x1_hbm[r0:r0 + 128, :],
                                 in_=x1t[:, (tc % 2) * 1024:(tc % 2) * 1024 + 1024]
                                 ).then_inc(S["stx1"], 16)
                    wg(sy, "xn2", 4 * b + tc + 1)
                    sy.dma_start(out=xn2_hbm[r0:r0 + 128, :],
                                 in_=xn2t[:, (tc % 2) * 1024:(tc % 2) * 1024 + 1024]
                                 ).then_inc(S["stxn2"], 16)
            sy.wait_ge(S["stx1"], 16 * 16)
            sy.wait_ge(S["stxn2"], 16 * 16)

        @ba.tensor
        def _(te):
            te.wait_ge(S["pl"], PL)
            te.wait_ge(S["plg"], PLG)
            for j in range(NITER):
                b = j - 1
                slot = j % 2
                wg(te, "xn1", 4 * (j + 1))
                for ec in range(8):
                    wg(te, "tpe", 8 * j + max(ec - 1, 0))
                    if ec <= 1:
                        wg(te, "pte", 64 * max(b, 0))
                    for tc in range(4):
                        mm = te.transpose(ps_pt[ec % 2][:, tc * 128:tc * 128 + 128],
                                          xn1[:, tc * 1024 + ec * 128:tc * 1024 + ec * 128 + 128],
                                          idbf[:])
                    mm.then_inc(S["tpp"], 1)
                wg(te, "tpe", 8 * (j + 1))
                wg(te, "x1", 4 * (j - 1))
                nt = 0
                if j >= 1:
                    for fcp in range(4):
                        wg(te, "qke", qke_cum(j - 1) + max(nt - 1, 0))
                        pa = ps_a[nt % 2]
                        for half in range(2):
                            fc = fcp * 2 + half
                            for ec in range(8):
                                mm = te.matmul(pa[:, half * 512:half * 512 + 512],
                                               winT[:, ec * 3072 + fc * 128:ec * 3072 + fc * 128 + 128],
                                               xn1T[:, ec * 512:ec * 512 + 512],
                                               start=(ec == 0), stop=(ec == 7))
                        mm.then_inc(S["qkm"], 1)
                        nt += 1
                for fcp in range(4):
                    wg(te, "qke", qke_cum(j - 1) + max(nt - 1, 0))
                    pa = ps_a[nt % 2]
                    for half in range(2):
                        fc = fcp * 2 + half
                        for ec in range(8):
                            mm = te.matmul(pa[:, half * 512:half * 512 + 512],
                                           winT[:, ec * 3072 + 1024 + fc * 128:ec * 3072 + 1024 + fc * 128 + 128],
                                           xn1T[:, ec * 512:ec * 512 + 512],
                                           start=(ec == 0), stop=(ec == 7))
                    mm.then_inc(S["qkm"], 1)
                    nt += 1
                for tc in range(4):
                    wg(te, "qke", qke_cum(j - 1) + max(nt - 1, 0))
                    pa = ps_a[nt % 2]
                    for n in range(2):
                        for ec in range(8):
                            mm = te.matmul(pa[:, n * 512:n * 512 + 512],
                                           xn1T[:, ec * 512 + tc * 128:ec * 512 + tc * 128 + 128],
                                           winT[:, ec * 3072 + 2048 + n * 512:ec * 3072 + 2048 + n * 512 + 512],
                                           start=(ec == 0), stop=(ec == 7))
                    mm.then_inc(S["qkm"], 1)
                    nt += 1

                if j == 0:
                    continue
                wg(te, "qke", qke_cum(j))
                prev, cur = (j - 1) % 2, j % 2

                def emit_scores(s):
                    h, qc = s // 4, s % 4
                    hp, hh = h // 2, (h % 2) * 64
                    n1 = 512 - qc * 128
                    wg(te, "sm1", 64 * b + max(s - 1, 0))
                    pa = ps_a[s % 2]
                    q_ap = q_sb[hh:hh + 64, hp * 512 + qc * 128:hp * 512 + qc * 128 + 128]
                    te.matmul(pa[:, 0:n1], q_ap,
                              k_ring[hh:hh + 64, hp * 1024 + prev * 512 + qc * 128:hp * 1024 + prev * 512 + 512],
                              start=True, stop=True)
                    if qc > 0:
                        te.matmul(pa[:, n1:512], q_ap,
                                  k_ring[hh:hh + 64, hp * 1024 + cur * 512:hp * 1024 + cur * 512 + qc * 128],
                                  start=True, stop=True)
                    te.matmul(pa[:, 512:640], q_ap,
                              k_ring[hh:hh + 64, hp * 1024 + cur * 512 + qc * 128:hp * 1024 + cur * 512 + qc * 128 + 128],
                              start=True, stop=True).then_inc(S["sc"], 1)

                def emit_transposes(sg):
                    wg(te, "pn", 64 * b + sg + 1)
                    if sg >= 2:
                        wg(te, "pte", 64 * b + sg - 1)
                    else:
                        wg(te, "tpe", 8 * (j + 1))
                    for kc in range(5):
                        mm = te.transpose(ps_pt[sg % 2][:, kc * 128:kc * 128 + 128],
                                          p_sb[:, (sg % 3) * 640 + kc * 128:(sg % 3) * 640 + kc * 128 + 128],
                                          idbf[:])
                    mm.then_inc(S["tp"], 1)

                def emit_pv(h):
                    pair, hh = h // 2, (h % 2) * 64
                    wg(te, "pte", 64 * b + 4 * h + 4)
                    if h % 2 == 0 and pair >= 2:
                        wg(te, "oe", 8 * b + pair - 1)
                    po = ps_o[pair % 2]
                    for qc in range(4):
                        for kc in range(5):
                            wc = qc + kc
                            sl, tc2 = (prev, wc) if wc < 4 else (cur, wc - 4)
                            mm = te.matmul(po[hh:hh + 64, qc * 128:qc * 128 + 128],
                                           v_ring[:, (sl * 4 + tc2) * 1024 + h * 64:(sl * 4 + tc2) * 1024 + h * 64 + 64],
                                           pT_sb[:, (h % 2) * 2560 + qc * 640 + kc * 128:(h % 2) * 2560 + qc * 640 + kc * 128 + 128],
                                           start=(kc == 0), stop=(kc == 4))
                    mm.then_inc(S["pv"], 1)

                for s in range(68):
                    if s < 64:
                        emit_scores(s)
                    if 2 <= s < 66:
                        emit_transposes(s - 2)
                    if s >= 4 and (s - 4) % 4 == 3:
                        emit_pv((s - 4) // 4)
                wg(te, "oe", 8 * (b + 1))
                wg(te, "sm1", 64 * (b + 1))
                for tc in range(4):
                    wg(te, "x1", 4 * b + max(tc - 1, 0))
                    pa = ps_a[tc % 2]
                    for n in range(2):
                        for ec in range(8):
                            mm = te.matmul(pa[:, n * 512:n * 512 + 512],
                                           o_sb[:, ec * 512 + tc * 128:ec * 512 + tc * 128 + 128],
                                           woutT[:, ec * 1024 + n * 512:ec * 1024 + n * 512 + 512],
                                           start=(ec == 0), stop=(ec == 7))
                    mm.then_inc(S["op"], 1)

        @ba.scalar
        def _(sc):
            sc.wait_ge(S["pl"], PL)
            sc.wait_ge(S["plg"], PLG)
            for j in range(NITER):
                b = j - 1
                slot = j % 2
                wg(sc, "x", 16 * (j + 1))
                xoff = 0
                for tc in range(4):
                    src = x_sb[:, tc * 1024:tc * 1024 + 1024]
                    ln_scalar_sq(sc, src, 0, tc, "sq1", 0, "va1", 4 * j + tc - 1)
                    ln_scalar_sqrt(sc, 0, tc, "va1", 4 * j + tc + 1, "std1", "st1",
                                   4 * j + tc - 1)
                    ln_scalar_apply(sc, xn1[:, tc * 1024:tc * 1024 + 1024], src,
                                    0, tc, "st1", 4 * j + tc + 1, "xn1")
                for ec in range(8):
                    wg(sc, "tpp", 8 * j + ec + 1)
                    sc.copy(xn1T[:, ec * 512:ec * 512 + 512],
                            ps_pt[ec % 2][:, 0:512]).then_inc(S["tpe"], 1)
                wg(sc, "pv", 16 * (j - 1))
                nt = 0
                if j >= 1:
                    for fcp in range(4):
                        wg(sc, "qkm", qke_cum(j - 1) + nt + 1)
                        sc.activation(q_sb[:, fcp * 1024:fcp * 1024 + 1024],
                                      ps_a[nt % 2][:, 0:1024], AF.Identity,
                                      scale=0.125).then_inc(S["qke"], 1)
                        nt += 1
                for fcp in range(4):
                    wg(sc, "qkm", qke_cum(j - 1) + nt + 1)
                    for half in range(2):
                        fc = fcp * 2 + half
                        ins = sc.copy(k_ring[:, fc * 1024 + slot * 512:fc * 1024 + slot * 512 + 512],
                                      ps_a[nt % 2][:, half * 512:half * 512 + 512])
                    ins.then_inc(S["qke"], 1)
                    nt += 1
                for tc in range(4):
                    wg(sc, "qkm", qke_cum(j - 1) + nt + 1)
                    sc.copy(v_ring[:, (slot * 4 + tc) * 1024:(slot * 4 + tc) * 1024 + 1024],
                            ps_a[nt % 2][:, 0:1024]).then_inc(S["qke"], 1)
                    nt += 1

                if j == 0:
                    continue

                def emit_pte(sg):
                    h2 = sg // 4
                    wg(sc, "tp", 64 * b + sg + 1)
                    if sg % 4 == 0:
                        wg(sc, "pv", 16 * b + max(h2 - 1, 0))
                    sc.copy(pT_sb[:, (h2 % 2) * 2560 + (sg % 4) * 640:(h2 % 2) * 2560 + (sg % 4) * 640 + 640],
                            ps_pt[sg % 2][:, 0:640]).then_inc(S["pte"], 1)
                    if sg % 8 == 7:
                        pair2 = sg // 8
                        wg(sc, "pv", 16 * b + 2 * pair2 + 2)
                        sc.copy(o_sb[:, pair2 * 512:pair2 * 512 + 512],
                                ps_o[pair2 % 2][:, 0:512]).then_inc(S["oe"], 1)

                for s in range(66):
                    if s < 64:
                        wg(sc, "t2", 64 * b + s + 1)
                        wg(sc, "tp", 64 * b + s - 2)
                        wg(sc, "pn", 64 * b + s - 2)
                        nmc, smc = 16 + (s % 3), 19 + (s % 3)
                        sc.activation(p_sb[:, (s % 3) * 640:(s % 3) * 640 + 640],
                                      t_sb[:, (s % 3) * 640:(s % 3) * 640 + 640],
                                      AF.Exp, bias=st[:, nmc:nmc + 1],
                                      accum_out=st[:, smc:smc + 1]).then_inc(S["p"], 1)
                    if s >= 2:
                        emit_pte(s - 2)
                for tc in range(4):
                    wg(sc, "x1", 4 * b + tc + 1)
                    src = x1t[:, (tc % 2) * 1024:(tc % 2) * 1024 + 1024]
                    ln_scalar_sq(sc, src, 28, tc, "sq2", 0, "va2", 4 * b + tc - 1)
                    ln_scalar_sqrt(sc, 28, tc, "va2", 4 * b + tc + 1, "std2", "st2",
                                   4 * b + tc - 1)
                    wg(sc, "stxn2", 16 * (4 * b + tc - 1))
                    ln_scalar_apply(sc, xn2t[:, (tc % 2) * 1024:(tc % 2) * 1024 + 1024],
                                    src, 28, tc, "st2", 4 * b + tc + 1, "xn2")

        @ba.vector
        def _(v):
            v.wait_ge(S["pl"], PL)
            v.wait_ge(S["plg"], PLG)
            for j in range(NITER):
                b = j - 1
                wg(v, "x", 16 * (j + 1))
                for tc in range(4):
                    ln_vector_stats(v, x_sb[:, tc * 1024:tc * 1024 + 1024], 0, tc,
                                    "sq1", 4 * j + tc + 1, "va1", "std1",
                                    4 * j + tc + 1, "st1", "xn1", 4 * j + tc - 1)
                if j == 0:
                    continue

                def emit_pn(sp):
                    wg(v, "p", 64 * b + sp + 1)
                    smc, rc = 19 + (sp % 3), 22 + (sp % 3)
                    vmark(v.reciprocal(st[:, rc:rc + 1], st[:, smc:smc + 1]))
                    vwait(v)
                    pp = p_sb[:, (sp % 3) * 640:(sp % 3) * 640 + 640]
                    v.tensor_scalar(pp, pp, st[:, rc:rc + 1], 1.0,
                                    op0=OP.mult, op1=OP.mult).then_inc(S["pn"], 1)

                for s in range(64):
                    h, qc = s // 4, s % 4
                    n1 = 512 - qc * 128
                    wg(v, "sc", 64 * b + s + 1)
                    wg(v, "p", 64 * b + s - 2)
                    t0 = (s % 3) * 640
                    tt = t_sb[:, t0:t0 + 640]
                    v.scalar_tensor_tensor(tt, kjb[:, 0:640],
                                           -SLOPES[h], ps_a[s % 2][:, 0:640],
                                           op0=OP.mult, op1=OP.add).then_inc(S["sm1"], 1)
                    v.wait_ge(S["sm1"], 64 * b + s + 1)
                    if b == 0:
                        vmark(v.tensor_tensor(t_sb[:, t0:t0 + n1], t_sb[:, t0:t0 + n1],
                                              maskP0[:, qc * 128:512], op=OP.add))
                        vwait(v)
                    nmc = 16 + (s % 3)
                    v.reduce_max(st[:, nmc:nmc + 1], tt, axis=AX,
                                 negate=True).then_inc(S["t2"], 1)
                    if s >= 1:
                        emit_pn(s - 1)
                if True:
                    emit_pn(63)
                for tc in range(4):
                    wg(v, "op", 4 * b + tc + 1)
                    wg(v, "xn2", 4 * b + tc - 1)
                    wg(v, "stx1", 16 * (4 * b + tc - 1))
                    v.tensor_tensor(x1t[:, (tc % 2) * 1024:(tc % 2) * 1024 + 1024],
                                    x_sb[:, tc * 1024:tc * 1024 + 1024],
                                    ps_a[tc % 2][:, 0:1024], op=OP.add).then_inc(S["x1"], 1)
                    ln_vector_stats(v, x1t[:, (tc % 2) * 1024:(tc % 2) * 1024 + 1024],
                                    28, tc, "sq2", 4 * b + tc + 1, "va2", "std2",
                                    4 * b + tc + 1, "st2", "xn2", 4 * b + tc - 1)

    if debug2:
        dbg_names = [("d_x", x_sb, F32, 4096), ("d_xn1", xn1, BF16, 4096),
                     ("d_xn1T", xn1T, BF16, 4096), ("d_q", q_sb, BF16, 4096),
                     ("d_k", k_ring, BF16, 8192), ("d_v", v_ring, BF16, 8192),
                     ("d_t", t_sb, F32, 2048), ("d_p", p_sb, BF16, 2048),
                     ("d_pT", pT_sb, BF16, 4096), ("d_o", o_sb, BF16, 4096),
                     ("d_st", st, F32, 48)]
        dbg_outs = {nm: nc.dram_tensor(nm, [128, w], dt, kind="ExternalOutput").ap()
                    for nm, _, dt, w in dbg_names}
        sdbg = nc.alloc_semaphore("s_dbg")
        with nc.Block() as bdg:
            @bdg.sync
            def _(sy):
                for nm, tile, dt, w in dbg_names:
                    sy.dma_start(out=dbg_outs[nm], in_=tile[:]).then_inc(sdbg, 16)
                sy.wait_ge(sdbg, 16 * len(dbg_names))

    stackA.close()
    psA.close()

    stackB = ExitStack()

    def SBB(name, shape, dt):
        return stackB.enter_context(nc.sbuf_tensor(name, shape, dt))

    stackB2 = ExitStack()
    acc = stackB2.enter_context(nc.sbuf_tensor("b_acc", [128, 8 * 2048], BF16))
    pos1u = stackB2.enter_context(nc.sbuf_tensor("b_pos1u", [128, 16], mybir.dt.uint32))
    pos2u = stackB2.enter_context(nc.sbuf_tensor("b_pos2u", [128, 16], mybir.dt.uint32))
    gcol1 = stackB2.enter_context(nc.sbuf_tensor("b_gcol1", [128, 16], F32))
    gcol2 = stackB2.enter_context(nc.sbuf_tensor("b_gcol2", [128, 16], F32))

    xn2T = SBB("b_xn2T", [128, 8 * 1024], BF16)
    h_sb = SBB("b_h", [128, 32 * 1024], BF16)
    W1_sb = [SBB("b_w1_%d" % i, [128, 4096], BF16) for i in range(2)]
    W2_sb = [SBB("b_w2_%d" % i, [128, 4096], BF16) for i in range(2)]
    lg_sb = SBB("b_lg", [128, 8], F32)
    lgm_sb = SBB("b_lgm", [128, 8], F32)
    eq1_sb = SBB("b_eq1", [128, 64], F32)
    eq2_sb = SBB("b_eq2", [128, 64], F32)
    d12_sb = SBB("b_d12", [128, 8], F32)
    g1_sb = SBB("b_g1", [128, 8], F32)
    g2_sb = SBB("b_g2", [128, 8], F32)
    gt_sb = SBB("b_gt", [128, 8], F32)
    gatef_sb = SBB("b_gatef", [128, 64], F32)
    gst = SBB("b_gst", [128, 4], F32)
    g8_sb = SBB("b_g8", [8, 1024], F32)
    me8 = SBB("b_me8", [8, 1024], F32)
    zer8 = SBB("b_zer8", [8, 1024], F32)
    posi8 = SBB("b_posi8", [8, 1024], F32)
    posT_sb = SBB("b_posT", [128, 64], F32)
    pv1 = SBB("b_pv1", [128, 8], F32)
    pv2 = SBB("b_pv2", [128, 8], F32)
    pv3 = SBB("b_pv3", [128, 8], F32)
    pc1 = SBB("b_pc1", [128, 1], F32)
    idxu_sb = SBB("b_idxu", [128, 64], mybir.dt.uint32)
    idxg_sb = [SBB("b_idxg%d" % i, [128, 3], mybir.dt.uint32) for i in range(2)]
    xg_nat = [SBB("b_xg0", [128, 3 * 1024], BF16)]
    xgT = SBB("b_xgT", [128, 8 * CAP], BF16)
    op_fm = SBB("b_opfm", [128, 8 * CAP], BF16)
    oprow = [SBB("b_oprow%d" % i, [128, 1024], BF16) for i in range(2)]

    psB = ExitStack()
    ps_fc = [psB.enter_context(nc.psum_tensor("ps_fc%d" % i, [128, 512], F32)) for i in range(2)]
    ps_out = [psB.enter_context(nc.psum_tensor("ps_out%d" % i, [128, 1024], F32)) for i in range(2)]
    ps_tb = psB.enter_context(nc.psum_tensor("ps_tb", [128, 1024], BF16))
    ps_tb2 = psB.enter_context(nc.psum_tensor("ps_tb2", [128, 1024], BF16))

    def fcp_slice_end(i1):
        if i1 < 0:
            return 0
        tth, r = divmod(i1, 72)
        e, hs = divmod(r, 8)
        base = 320 * tth
        if e == 0:
            return base + 8 * (hs + 1)
        return base + 64 + 4 * ((e - 1) * 8 + hs + 1)

    def fcp_e_end(tth, e):
        return 320 * tth + (64 if e >= 0 else 0) + 4 * 8 * max(e, 0)

    def gel_e_end(tth, e):
        return fcp_e_end(tth, e)

    with nc.Block() as bb:
        @bb.sync
        def _(sy):
            for tth in range(2):
                wg(sy, "fcp", 320 * tth)
                for ec in range(8):
                    sy.dma_start(out=xn2T[:, ec * 1024:ec * 1024 + 1024],
                                 in_=xn2_hbm[tth * 1024:tth * 1024 + 1024, ec * 128:ec * 128 + 128],
                                 transpose=True).then_inc(S["xn2t"], 16)
                wg(sy, "g8v", tth + 1)
                sy.dma_start(out=g8_hbm[:, tth * 1024:tth * 1024 + 1024],
                             in_=g8_sb[0:8, :]).then_inc(S["g8st"], 16)
                for e in range(NEXP):
                    for m in range(8):
                        i2 = (tth * 9 + e) * 8 + m
                        wg(sy, "pse", i2 - 1)
                        sy.dma_start(out=W2_sb[i2 % 2][:], in_=W2b[e, m]
                                     ).then_inc(S["w2"], 16)
                    if e >= 1:
                        slot = tth * 8 + (e - 1)
                        for gi in range(3):
                            og = slot * 3 + gi
                            wg(sy, "opv", og + 1)
                            sy.dma_start(out=out_perm[slot * CAP + gi * 128:slot * CAP + gi * 128 + 128, :],
                                         in_=oprow[og % 2][:]).then_inc(S["ops"], 16)

        @bb.gpsimd
        def _(g):
            r_cap = g.alloc_register()
            g.reg_mov(r_cap, CAP - 1)
            r_tok = g.alloc_register()
            g.reg_mov(r_tok, 1023)
            for tth in range(2):
                for e in range(NEXP):
                    if e >= 1:
                        ei = e - 1
                        slot = tth * 8 + ei
                        for tc in range(8):
                            wg(g, "ix", 8 * tth + tc + 1)
                            g.indirect_dma_start(
                                out=idxl_hbm,
                                out_offset=bass.IndirectOffsetOnAxis(
                                    idxu_sb[:, tc * 8 + ei:tc * 8 + ei + 1], axis=0),
                                in_=tokid_sb[:, tc:tc + 1], in_offset=None,
                                element_offset=slot * CAP,
                                bounds_check=r_cap, oob_is_err=False
                                ).then_inc(S["sca"], 16)
                        wg(g, "sca", 16 * (64 * tth + ei * 8 + 8))
                        wg(g, "xga", 16 * max(24 * tth + 3 * (e - 2), 0))
                        g.dma_start(out=idxg_sb[ei % 2][:],
                                    in_=idxl_hbm[slot * CAP:slot * CAP + CAP, :]
                                    .rearrange("(gi p) w -> p (gi w)", p=128),
                                    allow_slow_non_contiguous=True
                                    ).then_inc(S["ixl"], 16)
                        wg(g, "ixl", 16 * (8 * tth + e))
                        wg(g, "xgp", 8 * (8 * tth + ei))
                        for gi in range(3):
                            g.indirect_dma_start(
                                out=xg_nat[0][:, gi * 1024:gi * 1024 + 1024],
                                out_offset=None,
                                in_=xn2_hbm,
                                in_offset=bass.IndirectOffsetOnAxis(
                                    idxg_sb[ei % 2][:, gi:gi + 1], axis=0),
                                element_offset=tth * 1024 * 1024,
                                bounds_check=r_tok, oob_is_err=False
                                ).then_inc(S["xga"], 16)
                    for hs in range(8):
                        i1 = (tth * 9 + e) * 8 + hs
                        wg(g, "fcp", fcp_slice_end(i1 - 2))
                        g.dma_start(out=W1_sb[i1 % 2][:], in_=W1H[e, hs],
                                    max_dma_last_dim=8192).then_inc(S["w1"], 16)

        @bb.tensor
        def _(te):
            gfcp = [0]

            def fc_tile(i1pct, pslice, lhs_ap, rhs_ap):
                it = gfcp[0]
                wg(te, "gel", max(it - 1, 0))
                for ec in range(8):
                    mm = te.matmul(pslice, lhs_ap(ec), rhs_ap(ec),
                                   start=(ec == 0), stop=(ec == 7))
                mm.then_inc(S["fcp"], 1)
                gfcp[0] += 1
                return it

            for tth in range(2):
                wg(te, "xn2t", 16 * 8 * (tth + 1))
                wg(te, "gel", 320 * tth)
                wg(te, "pse", 72 * tth)
                for tc in range(8):
                    for ec in range(8):
                        mm = te.matmul(ps_fc[1][:, tc * 8:tc * 8 + 8],
                                       xn2T[:, ec * 1024 + tc * 128:ec * 1024 + tc * 128 + 128],
                                       wr_sb[:, ec * 8:ec * 8 + 8],
                                       start=(ec == 0), stop=(ec == 7))
                    mm.then_inc(S["rtm"], 1)
                for tc in range(8):
                    wg(te, "gv", 8 * tth + tc + 1)
                    te.transpose(ps_out[0][0:8, tc * 128:tc * 128 + 128],
                                 gatef_sb[:, tc * 8:tc * 8 + 8],
                                 idf32[:]).then_inc(S["gtt"], 1)
                wg(te, "g8v", tth + 1)
                wg(te, "pscn", tth + 1)
                for tc in range(8):
                    mm = te.transpose(ps_out[0][:, tc * 8:tc * 8 + 8],
                                      posi8[0:8, tc * 128:tc * 128 + 128],
                                      idf32[0:8, 0:8])
                mm.then_inc(S["ptt"], 1)
                wg(te, "gv1", 8 * (tth + 1))
                for e in range(NEXP):
                    ei = e - 1
                    if e >= 1:
                        wg(te, "xga", 16 * (24 * tth + 3 * e))
                        for ec in range(8):
                            xgpc = 64 * tth + 8 * ei + ec
                            wg(te, "xge", xgpc)
                            half = (ec % 2) * 512
                            for gi in range(3):
                                mm = te.transpose(ps_tb[:, half + gi * 128:half + gi * 128 + 128],
                                                  xg_nat[0][:, gi * 1024 + ec * 128:gi * 1024 + ec * 128 + 128],
                                                  idbf[:])
                            mm.then_inc(S["xgp"], 1)
                    for hs in range(8):
                        i1 = (tth * 9 + e) * 8 + hs
                        wg(te, "w1", 16 * (i1 + 1))
                        if e == 0 and hs == 0:
                            wg(te, "ptv", tth + 1)
                        if e >= 1 and hs == 0:
                            wg(te, "xge", 64 * tth + 8 * ei + 8)
                        for mf in range(4):
                            if e == 0:
                                for half in range(2):
                                    it = gfcp[0]
                                    pf = ps_fc[it % 2]
                                    wg(te, "gel", max(it - 1, 0))
                                    for ec in range(8):
                                        mm = te.matmul(pf[:, 0:512],
                                                       W1_sb[i1 % 2][:, ec * 512 + mf * 128:ec * 512 + mf * 128 + 128],
                                                       xn2T[:, ec * 1024 + half * 512:ec * 1024 + half * 512 + 512],
                                                       start=(ec == 0), stop=(ec == 7))
                                    mm.then_inc(S["fcp"], 1)
                                    gfcp[0] += 1
                            else:
                                it = gfcp[0]
                                pf = ps_fc[it % 2]
                                wg(te, "gel", max(it - 1, 0))
                                for ec in range(8):
                                    mm = te.matmul(pf[:, 0:CAP],
                                                   W1_sb[i1 % 2][:, ec * 512 + mf * 128:ec * 512 + mf * 128 + 128],
                                                   xgT[:, ec * CAP:ec * CAP + CAP],
                                                   start=(ec == 0), stop=(ec == 7))
                                mm.then_inc(S["fcp"], 1)
                                gfcp[0] += 1
                    wg(te, "gel", gel_e_end(tth, e))
                    for m in range(8):
                        i2 = (tth * 9 + e) * 8 + m
                        wg(te, "w2", 16 * (i2 + 1))
                        wg(te, "pse", max(i2 - 1, 0))
                        po = ps_out[i2 % 2]
                        if e == 0:
                            for hb in range(32):
                                for n in range(2):
                                    mm = te.matmul(po[:, n * 512:n * 512 + 512],
                                                   W2_sb[i2 % 2][:, hb * 128:hb * 128 + 128],
                                                   h_sb[:, hb * 1024 + n * 512:hb * 1024 + n * 512 + 512],
                                                   start=(hb == 0), stop=(hb == 31))
                        else:
                            for hb in range(32):
                                mm = te.matmul(po[:, 0:CAP],
                                               W2_sb[i2 % 2][:, hb * 128:hb * 128 + 128],
                                               h_sb[:, hb * 1024:hb * 1024 + CAP],
                                               start=(hb == 0), stop=(hb == 31))
                        mm.then_inc(S["prm"], 1)
                    if e >= 1:
                        for gi in range(3):
                            og = (tth * 8 + ei) * 3 + gi
                            wg(te, "pse", 72 * tth + 8 * (e + 1))
                            wg(te, "opv", og)
                            for m in range(8):
                                mm = te.transpose(ps_tb2[:, m * 128:m * 128 + 128],
                                                  op_fm[:, m * CAP + gi * 128:m * CAP + gi * 128 + 128],
                                                  idbf[:])
                            mm.then_inc(S["ott"], 1)

        @bb.scalar
        def _(sc):
            gfcp = [0]
            for tth in range(2):
                wg(sc, "gv1", 8 * (tth + 1))
                sc.activation(g1_sb[:], d12_sb[:], AF.Sigmoid)
                sc.activation(g2_sb[:], d12_sb[:], AF.Sigmoid,
                              scale=-1.0).then_inc(S["gs"], 1)
                for e in range(NEXP):
                    ei = e - 1
                    if e >= 1:
                        for ec in range(8):
                            xgpc = 64 * tth + 8 * ei + ec
                            wg(sc, "xgp", xgpc + 1)
                            if ec == 0:
                                wg(sc, "fcp", fcp_e_end(tth, e - 1))
                            half = (ec % 2) * 512
                            sc.copy(xgT[:, ec * CAP:ec * CAP + CAP],
                                    ps_tb[:, half:half + CAP]).then_inc(S["xge"], 1)
                    wg(sc, "prm", 8 * (tth * 9 + e))
                    for hs in range(8):
                        for mf in range(4):
                            hb = hs * 4 + mf
                            bcol = bfc_sb[:, e * 32 + hb:e * 32 + hb + 1]
                            if e == 0:
                                for half in range(2):
                                    it = gfcp[0]
                                    wg(sc, "fcp", it + 1)
                                    sc.activation(h_sb[:, hb * 1024 + half * 512:hb * 1024 + half * 512 + 512],
                                                  ps_fc[it % 2][:, 0:512], AF.Gelu_apprx_tanh,
                                                  bias=bcol).then_inc(S["gel"], 1)
                                    gfcp[0] += 1
                            else:
                                it = gfcp[0]
                                wg(sc, "fcp", it + 1)
                                sc.activation(h_sb[:, hb * 1024:hb * 1024 + CAP],
                                              ps_fc[it % 2][:, 0:CAP], AF.Gelu_apprx_tanh,
                                              bias=bcol).then_inc(S["gel"], 1)
                                gfcp[0] += 1
                    if e >= 1:
                        for m in range(8):
                            i2 = (tth * 9 + e) * 8 + m
                            wg(sc, "prm", i2 + 1)
                            if m == 0:
                                wg(sc, "ott", 3 * max(tth * 8 + ei, 0))
                            sc.activation(op_fm[:, m * CAP:m * CAP + CAP],
                                          ps_out[i2 % 2][:, 0:CAP], AF.Identity,
                                          bias=bproj_sb[:, e * 8 + m:e * 8 + m + 1]
                                          ).then_inc(S["pse"], 1)
                        for gi in range(3):
                            og = (tth * 8 + ei) * 3 + gi
                            wg(sc, "ott", og + 1)
                            wg(sc, "ops", 16 * max(og - 1, 0))
                            sc.copy(oprow[og % 2][:], ps_tb2[:]).then_inc(S["opv"], 1)

        @bb.vector
        def _(v):
            v.memset(zer8[:], 0.0)
            for tth in range(2):
                wg(v, "rtm", 8 * (tth + 1))
                for tc in range(8):
                    vmark(v.tensor_copy(lg_sb[:], ps_fc[1][:, tc * 8:tc * 8 + 8]))
                    vwait(v)
                    vmark(v.reduce_max(gst[:, 0:1], lg_sb[:], axis=AX))
                    vwait(v)
                    vmark(v.tensor_scalar(eq1_sb[:, tc * 8:tc * 8 + 8], lg_sb[:],
                                          gst[:, 0:1], 1.0, op0=OP.is_ge, op1=OP.mult))
                    vwait(v)
                    vmark(v.scalar_tensor_tensor(lgm_sb[:], eq1_sb[:, tc * 8:tc * 8 + 8],
                                                 -1.0e9, lg_sb[:], op0=OP.mult, op1=OP.add))
                    vwait(v)
                    vmark(v.reduce_max(gst[:, 1:2], lgm_sb[:], axis=AX))
                    vwait(v)
                    vmark(v.tensor_scalar(eq2_sb[:, tc * 8:tc * 8 + 8], lgm_sb[:],
                                          gst[:, 1:2], 1.0, op0=OP.is_ge, op1=OP.mult))
                    vwait(v)
                    v.tensor_tensor(d12_sb[:, tc:tc + 1], gst[:, 0:1], gst[:, 1:2],
                                    op=OP.subtract).then_inc(S["gv1"], 1)
                wg(v, "gs", tth + 1)
                wg(v, "gtt", 8 * tth)
                for tc in range(8):
                    vmark(v.tensor_scalar(gt_sb[:], eq1_sb[:, tc * 8:tc * 8 + 8],
                                          g1_sb[:, tc:tc + 1], 1.0, op0=OP.mult, op1=OP.mult))
                    vwait(v)
                    v.scalar_tensor_tensor(gatef_sb[:, tc * 8:tc * 8 + 8],
                                           eq2_sb[:, tc * 8:tc * 8 + 8],
                                           g2_sb[:, tc:tc + 1], gt_sb[:],
                                           op0=OP.mult, op1=OP.add).then_inc(S["gv"], 1)
                wg(v, "gtt", 8 * (tth + 1))
                wg(v, "g8st", 16 * tth)
                v.tensor_copy(g8_sb[0:8, :], ps_out[0][0:8, 0:1024]).then_inc(S["g8v"], 1)
                v.wait_ge(S["g8v"], tth + 1)
                vmark(v.tensor_scalar(me8[:], g8_sb[0:8, :], 0.0, 1.0,
                                      op0=OP.is_gt, op1=OP.mult))
                vwait(v)
                vmark(v.tensor_tensor_scan(posi8[:], me8[:], zer8[:], 0.0,
                                           op0=OP.add, op1=OP.add))
                vwait(v)
                v.tensor_tensor(posi8[:], posi8[:], me8[:],
                                op=OP.subtract).then_inc(S["pscn"], 1)
                wg(v, "ptt", tth + 1)
                wg(v, "sca", 16 * 64 * tth)
                v.tensor_copy(posT_sb[:], ps_out[0][:, 0:64]).then_inc(S["ptv"], 1)
                v.wait_ge(S["ptv"], tth + 1)
                for tc in range(8):
                    tg = tth * 8 + tc
                    rank = posT_sb[:, tc * 8:tc * 8 + 8]
                    vmark(v.tensor_scalar(pv3[:], gatef_sb[:, tc * 8:tc * 8 + 8],
                                          0.0, 1.0, op0=OP.is_gt, op1=OP.mult))
                    vwait(v)
                    vmark(v.tensor_scalar(pv2[:], pv3[:], 1.0, -1.0e9,
                                          op0=OP.subtract, op1=OP.mult))
                    vwait(v)
                    vmark(v.tensor_tensor(pv1[:], rank, pv2[:], op=OP.add))
                    vwait(v)
                    vmark(v.tensor_copy(idxu_sb[:, tc * 8:tc * 8 + 8], pv1[:]))
                    vwait(v)
                    vmark(v.tensor_tensor(pv1[:], rank, posbase_sb[:, tth * 8:tth * 8 + 8],
                                          op=OP.add))
                    vwait(v)
                    vmark(v.tensor_scalar(pv2[:], rank, CAP - 0.5, 1.0,
                                          op0=OP.is_gt, op1=OP.mult))
                    vwait(v)
                    vmark(v.tensor_scalar(pv3[:], pv2[:], -1.0, 1.0,
                                          op0=OP.mult, op1=OP.add))
                    vwait(v)
                    vmark(v.tensor_tensor(pv1[:], pv1[:], pv3[:], op=OP.mult))
                    vwait(v)
                    vmark(v.tensor_scalar(pv2[:], pv2[:], float(ZROW), 1.0,
                                          op0=OP.mult, op1=OP.mult))
                    vwait(v)
                    vmark(v.tensor_tensor(pv1[:], pv1[:], pv2[:], op=OP.add))
                    vwait(v)
                    vmark(v.tensor_tensor(pv2[:], eq1_sb[:, tc * 8:tc * 8 + 8], pv1[:],
                                          op=OP.mult))
                    vwait(v)
                    vmark(v.reduce_sum(pc1[:], pv2[:], axis=AX))
                    vwait(v)
                    vmark(v.tensor_scalar(pc1[:], pc1[:], float(ZROW), 1.0,
                                          op0=OP.min, op1=OP.mult))
                    vwait(v)
                    vmark(v.tensor_copy(pos1u[:, tg:tg + 1], pc1[:]))
                    vwait(v)
                    vmark(v.tensor_tensor(pv2[:], eq2_sb[:, tc * 8:tc * 8 + 8], pv1[:],
                                          op=OP.mult))
                    vwait(v)
                    vmark(v.reduce_sum(pc1[:], pv2[:], axis=AX))
                    vwait(v)
                    vmark(v.tensor_scalar(pc1[:], pc1[:], float(ZROW), 1.0,
                                          op0=OP.min, op1=OP.mult))
                    vwait(v)
                    vmark(v.tensor_copy(pos2u[:, tg:tg + 1], pc1[:]))
                    vwait(v)
                    vmark(v.tensor_copy(gcol1[:, tg:tg + 1], g1_sb[:, tc:tc + 1]))
                    vwait(v)
                    v.tensor_copy(gcol2[:, tg:tg + 1], g2_sb[:, tc:tc + 1]
                                  ).then_inc(S["ix"], 1)
                for m in range(8):
                    i2 = (tth * 9 + 0) * 8 + m
                    wg(v, "prm", i2 + 1)
                    aslice = acc[:, m * 2048 + tth * 1024:m * 2048 + tth * 1024 + 1024]
                    v.tensor_scalar(aslice, ps_out[i2 % 2][:, 0:1024],
                                    bproj_sb[:, 0 * 8 + m:0 * 8 + m + 1], 1.0,
                                    op0=OP.add, op1=OP.mult).then_inc(S["pse"], 1)

    stackB.close()
    psB.close()

    stackC = ExitStack()
    x1f = [stackC.enter_context(nc.sbuf_tensor("c_x1f%d" % i, [128, 1024], F32)) for i in range(2)]
    y_sb = [stackC.enter_context(nc.sbuf_tensor("c_y%d" % i, [128, 1024], F32)) for i in range(2)]
    c1_sb = [stackC.enter_context(nc.sbuf_tensor("c_c1_%d" % i, [128, 1024], BF16)) for i in range(2)]
    c2_sb = [stackC.enter_context(nc.sbuf_tensor("c_c2_%d" % i, [128, 1024], BF16)) for i in range(2)]
    tmpc = stackC.enter_context(nc.sbuf_tensor("c_tmpc", [128, 1024], F32))
    psC = ExitStack()
    ps_y = [psC.enter_context(nc.psum_tensor("ps_y%d" % i, [128, 1024], BF16)) for i in range(2)]

    with nc.Block() as bc:
        @bc.sync
        def _(sy):
            for tc in range(16):
                wg(sy, "cy", max(tc - 1, 0))
                sy.dma_start(out=x1f[tc % 2][:],
                             in_=x1_hbm[tc * 128:tc * 128 + 128, :]).then_inc(S["cx1"], 16)
                wg(sy, "cy", tc + 1)
                sy.dma_start(out=y_out[tc * 128:tc * 128 + 128, :],
                             in_=y_sb[tc % 2][:]).then_inc(S["y"], 16)
            sy.wait_ge(S["y"], 16 * 16)

        @bc.gpsimd
        def _(g):
            r_zr = g.alloc_register()
            g.reg_mov(r_zr, ZROW)
            g.wait_ge(S["ops"], 16 * 48)
            g.wait_ge(S["ix"], 16)
            for tc in range(16):
                wg(g, "cy", max(tc - 1, 0))
                for pu, dst in ((pos1u, c1_sb), (pos2u, c2_sb)):
                    g.indirect_dma_start(
                        out=dst[tc % 2][:], out_offset=None,
                        in_=out_perm,
                        in_offset=bass.IndirectOffsetOnAxis(pu[:, tc:tc + 1], axis=0),
                        bounds_check=r_zr, oob_is_err=False
                        ).then_inc(S["cg"], 16)

        @bc.tensor
        def _(te):
            wg(te, "pse", 144)
            for tc in range(16):
                wg(te, "cy", max(tc - 1, 0))
                for m in range(8):
                    mm = te.transpose(ps_y[tc % 2][:, m * 128:m * 128 + 128],
                                      acc[:, m * 2048 + tc * 128:m * 2048 + tc * 128 + 128],
                                      idbf[:])
                mm.then_inc(S["ct"], 1)

        @bc.vector
        def _(v):
            for tc in range(16):
                wg(v, "cx1", 16 * (tc + 1))
                wg(v, "ct", tc + 1)
                wg(v, "cg", 32 * (tc + 1))
                wg(v, "y", 16 * max(tc - 1, 0))
                ys = y_sb[tc % 2][:]
                v.tensor_tensor(ys, x1f[tc % 2][:], ps_y[tc % 2][:, 0:1024],
                                op=OP.add)
                v.tensor_scalar(tmpc[:], c1_sb[tc % 2][:], gcol1[:, tc:tc + 1], 1.0,
                                op0=OP.mult, op1=OP.mult)
                v.tensor_tensor(ys, ys, tmpc[:], op=OP.add)
                v.tensor_scalar(tmpc[:], c2_sb[tc % 2][:], gcol2[:, tc:tc + 1], 1.0,
                                op0=OP.mult, op1=OP.mult)
                v.tensor_tensor(ys, ys, tmpc[:], op=OP.add).then_inc(S["cy"], 1)

    stackC.close()
    psC.close()
    return nc


_CACHED = {}


def _get_nc():
    if "nc" not in _CACHED:
        _CACHED["nc"] = build_program(debug=False)
    return _CACHED["nc"]


def _host_inputs(L0_attn_x, L0_attn_w_in, L0_attn_w_out, L0_ffn_Wfc_s, L0_ffn_bfc_s,
                 L0_ffn_Wproj_s, L0_ffn_bproj_s, L0_ffn_Wr, L0_ffn_Wfc, L0_ffn_bfc,
                 L0_ffn_Wproj, L0_ffn_bproj):
    f32 = np.float32
    x = np.ascontiguousarray(np.asarray(L0_attn_x, f32).reshape(16384, 1024))
    w_inT = np.ascontiguousarray(np.asarray(L0_attn_w_in, f32).T)
    w_outT = np.ascontiguousarray(np.asarray(L0_attn_w_out, f32).T)
    Wfc_all = np.concatenate([np.asarray(L0_ffn_Wfc_s, f32)[None],
                              np.asarray(L0_ffn_Wfc, f32)], 0)
    Wproj_all = np.concatenate([np.asarray(L0_ffn_Wproj_s, f32)[None],
                                np.asarray(L0_ffn_Wproj, f32)], 0)
    bfc_all = np.concatenate([np.asarray(L0_ffn_bfc_s, f32)[None],
                              np.asarray(L0_ffn_bfc, f32)], 0)
    bproj_all = np.concatenate([np.asarray(L0_ffn_bproj_s, f32)[None],
                                np.asarray(L0_ffn_bproj, f32)], 0)
    W1H = np.ascontiguousarray(
        Wfc_all.reshape(NEXP, 8, 128, 8, 512).transpose(0, 3, 2, 1, 4).reshape(NEXP, 8, 128, 4096))
    W2H = np.ascontiguousarray(
        Wproj_all.reshape(NEXP, 32, 128, 8, 128).transpose(0, 3, 2, 1, 4).reshape(NEXP, 8, 128, 4096))
    bfcB = np.ascontiguousarray(bfc_all.reshape(NEXP, 32, 128).transpose(0, 2, 1))
    bprojB = np.ascontiguousarray(bproj_all.reshape(NEXP, 8, 128).transpose(0, 2, 1))
    WrH = np.ascontiguousarray(
        np.asarray(L0_ffn_Wr, f32).reshape(8, 128, 8).transpose(1, 0, 2).reshape(128, 64))
    pp_ = np.arange(128)[:, None]
    cc_ = np.arange(128)[None, :]
    kjb = np.broadcast_to(np.arange(1024, dtype=f32), (128, 1024)).copy()
    kjb[:, 0:128] += np.where(pp_ <= cc_, 0.0, 1e30).astype(f32)
    kjb[:, 512:640] += np.where(cc_ <= pp_, 0.0, 1e30).astype(f32)
    kjb = np.ascontiguousarray(kjb)
    maskP0_z = np.zeros((128, 512), ml_dtypes.bfloat16)
    maskP0_n = np.full((128, 512), NEG, ml_dtypes.bfloat16)
    idbf = np.eye(128, dtype=ml_dtypes.bfloat16)
    idf32 = np.eye(128, dtype=f32)
    consts = np.zeros((128, 4), f32)
    consts[:, 0] = 1e-5

    tokid = np.ascontiguousarray(
        (np.arange(8)[None, :] * 128 + np.arange(128)[:, None]).astype(np.uint32))
    posbase = np.ascontiguousarray(np.broadcast_to(
        (np.arange(16, dtype=f32) * 384.0)[None, :], (128, 16)))
    zci = np.zeros((16 * 384, 1), np.uint32)
    z1k = np.zeros((1, 1024), ml_dtypes.bfloat16)
    shared = dict(w_inT=w_inT, w_outT=w_outT, W1H=W1H, W2H=W2H, bfcB=bfcB,
                  bprojB=bprojB, WrH=WrH, kjb=kjb,
                  idbf=idbf, idf32=idf32, consts=consts,
                  tokid=tokid, posbase=posbase, zci=zci, z1k=z1k)
    zeros_halo = np.zeros((512, 1024), f32)
    in_maps = []
    for c in range(8):
        m = dict(shared)
        m["x_own"] = np.ascontiguousarray(x[c * T:(c + 1) * T])
        m["x_halo"] = zeros_halo if c in (0, 4) else np.ascontiguousarray(x[c * T - 512:c * T])
        m["maskP0"] = maskP0_n if c in (0, 4) else maskP0_z
        in_maps.append(m)
    return in_maps


def kernel(**inputs):
    in_maps = _host_inputs(**inputs)
    nc = _get_nc()
    res = run_bass_kernel_spmd(nc, in_maps, core_ids=list(range(8)))
    y = np.concatenate([np.asarray(res.results[c]["y"]) for c in range(8)], axis=0)
    return y.reshape(2, 8192, 1024).astype(np.float32, copy=False)
